# version 29
# baseline (speedup 1.0000x reference)
import math

import numpy as np
import ml_dtypes

import concourse.bass as bass
import concourse.mybir as mybir
from concourse.bass_utils import run_bass_kernel_spmd

F32 = mybir.dt.float32
BF16 = mybir.dt.bfloat16
AL = mybir.AluOpType
AF = mybir.ActivationFunctionType
AXX = mybir.AxisListType.X

H = 1024
HT = 8
NST = 32
SCALE = 1.0 / math.sqrt(512.0)

ENGS = ("sync", "tensor", "vector", "scalar", "gpsimd")


class Rec:

    def __init__(self):
        self.ops = {e: [] for e in ENGS}
        self.cnt = {}

    def c(self, sem):
        return self.cnt.get(sem, 0)

    def w(self, sem, lag=0):
        return (sem, self.cnt.get(sem, 0) - lag)

    def add(self, eng, emit, waits=(), inc=None, inc_by=1):
        ws = [(s, v) for (s, v) in waits if v > 0]
        self.ops[eng].append((ws, emit, inc, inc_by))
        if inc is not None:
            self.cnt[inc] = self.cnt.get(inc, 0) + inc_by


def build(nst=NST):
    nc = bass.Bass("TRN2", target_bir_lowering=False)
    S = nst * 128
    n_ch = max(1, S // 1024)
    blag = 1 if n_ch >= 2 else 0

    x_d = nc.dram_tensor("x", [S, H], F32, kind="ExternalInput")
    wq_d = nc.dram_tensor("wq", [H, H], F32, kind="ExternalInput")
    wk_d = nc.dram_tensor("wk", [H, H], F32, kind="ExternalInput")
    wv_d = nc.dram_tensor("wv", [H, H], F32, kind="ExternalInput")
    wo_d = nc.dram_tensor("wo", [H, H], F32, kind="ExternalInput")
    bq_d = nc.dram_tensor("bq", [H], F32, kind="ExternalInput")
    bk_d = nc.dram_tensor("bk", [H], F32, kind="ExternalInput")
    bv_d = nc.dram_tensor("bv", [H], F32, kind="ExternalInput")
    bo_d = nc.dram_tensor("bo", [H], F32, kind="ExternalInput")
    rhs_d = nc.dram_tensor("rhs", [H, H], F32, kind="ExternalInput")
    idb_d = nc.dram_tensor("idb", [128, 128], BF16, kind="ExternalInput")
    idf_d = nc.dram_tensor("idf", [128, 128], F32, kind="ExternalInput")
    out_d = nc.dram_tensor("out", [S, H], F32, kind="ExternalOutput")
    qsp_d = nc.dram_tensor("qsp", [nst, 128, H], BF16, kind="Internal")
    scr_d = nc.dram_tensor("scr", [1024], F32, kind="Internal")

    xa = x_d.ap()
    outa = out_d.ap()
    qspa = qsp_d.ap()

    rec = Rec()
    c, w = rec.c, rec.w

    from contextlib import ExitStack
    st = ExitStack()
    with st:
        KT = st.enter_context(nc.sbuf_tensor("KT", [128, HT, S], BF16))
        V = st.enter_context(nc.sbuf_tensor("V", [128, nst, H], BF16))
        WT0 = st.enter_context(nc.sbuf_tensor("WT0", [128, HT, H], BF16))
        WT1 = st.enter_context(nc.sbuf_tensor("WT1", [128, HT, H], BF16))
        WT2 = st.enter_context(nc.sbuf_tensor("WT2", [128, HT, H], BF16))
        BC0 = st.enter_context(nc.sbuf_tensor("BC0", [128, H], F32))
        BC1 = st.enter_context(nc.sbuf_tensor("BC1", [128, H], F32))
        TA = st.enter_context(nc.sbuf_tensor("TA", [128, H], F32))
        TB = st.enter_context(nc.sbuf_tensor("TB", [128, H], F32))
        XST = st.enter_context(nc.sbuf_tensor("XST", [128, H], F32))
        XS2 = nc.alloc_sbuf_tensor_at(
            "XS2", [128, H], F32,
            offset=nc.lookup_mloc(V).addr + (nst - 2) * H * 2)
        XBF = st.enter_context(nc.sbuf_tensor("XBF", [128, H], BF16))
        XTB = st.enter_context(nc.sbuf_tensor("XTB", [128, H], BF16))
        NAT = st.enter_context(nc.sbuf_tensor("NAT", [128, H], BF16))
        QTB0 = st.enter_context(nc.sbuf_tensor("QTB0", [128, H], BF16))
        QTB1 = st.enter_context(nc.sbuf_tensor("QTB1", [128, H], BF16))
        IDB = st.enter_context(nc.sbuf_tensor("IDB", [128, 128], BF16))
        IDF = st.enter_context(nc.sbuf_tensor("IDF", [128, 128], F32))
        ONES = st.enter_context(nc.sbuf_tensor("ONES", [1, 128], F32))
        Qss = st.enter_context(nc.sbuf_tensor("Qss", [128, nst], F32))
        Kss = st.enter_context(nc.sbuf_tensor("Kss", [128, nst], F32))
        NEGC = st.enter_context(nc.sbuf_tensor("NEGC", [128, nst], F32))
        SUM4 = st.enter_context(nc.sbuf_tensor("SUM4", [128, n_ch], F32))
        RSA = st.enter_context(nc.sbuf_tensor("RSA", [128, HT], F32))
        KMB = st.enter_context(nc.sbuf_tensor("KMB", [128, 1], F32))
        MG = st.enter_context(nc.sbuf_tensor("MG", [128, 1], F32))
        SM = st.enter_context(nc.sbuf_tensor("SM", [128, 16], F32))

        QTB = (QTB0, QTB1)

        S0 = st.enter_context(nc.psum_tensor("S0", [128, H], F32))
        S1 = st.enter_context(nc.psum_tensor("S1", [128, H], F32))
        O = st.enter_context(nc.psum_tensor("O", [128, H], F32))
        P = st.enter_context(nc.psum_tensor("P", [128, H], BF16))
        P2 = st.enter_context(nc.psum_tensor("P2", [128, H], BF16))

        sem_names = [
            "dWst0", "dWst1", "dWst2", "dWst3",
            "dBr", "dRh", "dRs", "dXa", "dSp0", "dSp1", "dQt0",
            "dQt1", "dXc", "dOut",
            "aWc", "tWt", "vWe", "tBc",
            "aCast", "tXt", "vXt", "tMM", "vQ", "tQtr", "vQt", "aSqQ",
            "vK", "tKtr", "vKt", "aSqK", "vV",
            "tRh", "vRq", "tRb", "vMp", "aMp",
            "tQk", "aExp", "vRi", "tPtr", "aPt", "vPt", "tPv", "tPvA", "aO2",
            "tO2t", "aO2t", "tAt", "vX1p", "vCen", "aSq1", "vX1",
            "vQl", "tQlt", "aQlt", "tFf", "vFf", "aSq2", "vOut", "gXpb",
        ]
        sems = {n: st.enter_context(nc.semaphore(n)) for n in sem_names}

        def dma(eng, out, in_, waits=(), inc=None):
            rec.add(eng, lambda e, o=out, i=in_: e.dma_start(out=o, in_=i),
                    waits=waits, inc=inc, inc_by=16)

        def tr8(dst_ps, src_sb):
            def f(e, d=dst_ps, s=src_sb):
                for t in range(HT):
                    r = e.transpose(d[:, t * 128:(t + 1) * 128],
                                    s[:, t * 128:(t + 1) * 128], IDB[:, :])
                return r
            return f

        dma("sync", IDB[:, :], idb_d.ap()[:, :], inc="dBr")
        dma("sync", IDF[:, :], idf_d.ap()[:, :], inc="dRh")

        wp_g = [0]

        def weight_prep(w_handle, dst, xst_w=(), dst_w=(), xbf_w=(), p_w=(),
                        xs2_w=(), nbuf=3):
            wap = w_handle.ap()
            for j in range(HT):
                gj = wp_g[0]
                wp_g[0] += 1
                if nbuf == 3:
                    stg = (XST, TB, XS2)[gj % 3]
                    sw = "dWst%d" % (j % 4)
                    lagw = 2
                else:
                    stg = (XST, TB)[j % 2]
                    sw = "dWst%d" % (j % 2)
                    lagw = 1
                cb = (XBF, NAT)[j % 2]
                pp = (P, P2)[j % 2]
                xw = list(xst_w) + (list(xs2_w) if stg is XS2 else [])
                dma(("sync", "gpsimd")[j % 2], stg[:, :],
                    wap[j * 128:(j + 1) * 128, :],
                    waits=[w("aWc", lag=lagw)] + xw, inc=sw)
                rec.add("scalar",
                        lambda e, stg=stg, cb=cb: e.activation(
                            cb[:, :], stg[:, :], AF.Copy),
                        waits=[w(sw), w("tWt", lag=1)] + list(xbf_w),
                        inc="aWc")
                rec.add("tensor", tr8(pp, cb),
                        waits=[w("aWc"), w("vWe", lag=1), ("dBr", 16)]
                        + list(p_w),
                        inc="tWt")
                rec.add("vector",
                        lambda e, j=j, dst=dst, pp=pp: e.tensor_copy(
                            dst[:, :, j * 128:(j + 1) * 128],
                            pp[:, :].rearrange("p (i q) -> p i q", q=128)),
                        waits=[w("tWt")] + list(dst_w), inc="vWe")

        rec.add("vector", lambda e: e.memset(ONES[:, :], 1.0), inc="vWe")
        rec.add("vector", lambda e: e.memset(MG[:, :], 2.0 ** -30), inc="vWe")
        rec.add("vector",
                lambda e: e.tensor_scalar(MG[:, :], MG[:, :], 1.0, 1.0,
                                          AL.add, AL.subtract),
                inc="vWe")
        rec.add("vector",
                lambda e: e.tensor_scalar(MG[:, :], MG[:, :],
                                          float(1.5 * 2.0 ** 82 - 1.5 * 2.0 ** 53),
                                          float(1.5 * 2.0 ** 23),
                                          AL.mult, AL.add),
                inc="vWe")

        def bcast_row(dst, stg, src_dma_in=None, mm_waits=(), evac_waits=(),
                      dma_waits=(), n=H):
            if src_dma_in is not None:
                dma("sync", stg[0:1, 0:n], src_dma_in,
                    waits=[w("dBr")] + list(dma_waits), inc="dBr")

            def mm(e, n=n, stg=stg):
                nh = (n + 511) // 512
                for hh in range(nh):
                    q = min(512, n - hh * 512)
                    r = e.matmul(O[:, hh * 512:hh * 512 + q], ONES[0:1, 0:128],
                                 stg[0:1, hh * 512:hh * 512 + q],
                                 start=True, stop=True)
                return r
            rec.add("tensor", mm, waits=[w("dBr"), w("vWe")] + list(mm_waits),
                    inc="tBc")
            rec.add("vector",
                    lambda e, dst=dst, n=n: e.tensor_copy(dst, O[:, 0:n]),
                    waits=[w("tBc")] + list(evac_waits), inc="vWe")

        dma("sync", XS2[:, :], xa[0:128, :], inc="dXa")
        rec.add("scalar",
                lambda e: e.activation(XTB[:, :], XS2[:, :], AF.Copy),
                waits=[("dXa", 16)], inc="aCast")
        rec.add("tensor", tr8(P, XTB),
                waits=[w("aCast"), w("vWe"), ("dBr", 16)], inc="tXt")
        rec.add("vector", lambda e: e.tensor_copy(XTB[:, :], P[:, :]),
                waits=[w("tXt")], inc="vXt")

        weight_prep(wq_d, WT0, xs2_w=[("aCast", 1)], p_w=[("vXt", 1)])
        bcast_row(BC0[:, :], BC0, bq_d.ap()[:])
        bcast_row(BC1[:, :], BC1, bk_d.ap()[:])
        bcast_row(TA[:, :], TA, bv_d.ap()[:])
        weight_prep(wk_d, WT1, xs2_w=[("aCast", 1)])
        weight_prep(wv_d, WT2, xs2_w=[("aCast", 1)])

        dma("sync", XST[:, :], xa[128:256, :],
            waits=[w("aCast"), ("aWc", c("aWc") - 2)], inc="dXa")

        def do_k_evac(i):
            kb = QTB[(i + 1) % 2]
            rec.add("vector",
                    lambda e, kb=kb: e.scalar_tensor_tensor(
                        kb[:, :], S1[:, :], 1.0, BC1[:, :], AL.mult, AL.add),
                    waits=[w("tMM"), w("tKtr"), w("aSqK"),
                           w("dSp%d" % ((i + 1) % 2))],
                    inc="vK")
            rec.add("scalar",
                    lambda e, i=i, kb=kb: e.activation(
                        S1[:, :], kb[:, :], AF.Square,
                        accum_out=Kss[:, i:i + 1]),
                    waits=[w("vK")], inc="aSqK")

        def mm_pass(ps, wt):
            def f(e, ps=ps, wt=wt):
                for j in range(HT):
                    lt = XTB[:, j * 128:(j + 1) * 128]
                    for h in range(2):
                        r = e.matmul(ps[:, h * 512:(h + 1) * 512], lt,
                                     wt[:, j, h * 512:(h + 1) * 512],
                                     start=(j == 0), stop=(j == HT - 1))
                return r
            return f

        for i in range(nst):
            if i + 1 < nst:
                rec.add("scalar",
                        lambda e: e.activation(XBF[:, :], XST[:, :], AF.Copy),
                        waits=[("dXa", 16 * (i + 2)), w("tXt"), w("tWt")],
                        inc="aCast")
            if i + 2 < nst:
                dma("sync", XST[:, :], xa[(i + 2) * 128:(i + 3) * 128, :],
                    waits=[w("aCast")], inc="dXa")
            rec.add("tensor", mm_pass(S0, WT0),
                    waits=[w("vXt"), w("vQ"), w("aSqQ")], inc="tMM")
            rec.add("vector",
                    lambda e: e.scalar_tensor_tensor(NAT[:, :], S0[:, :], 1.0,
                                                     BC0[:, :], AL.mult,
                                                     AL.add),
                    waits=[w("tMM"), w("tKtr"), w("aSqK")], inc="vQ")
            rec.add("scalar",
                    lambda e, i=i: e.activation(S0[:, :], NAT[:, :], AF.Square,
                                                accum_out=Qss[:, i:i + 1]),
                    waits=[w("vQ")], inc="aSqQ")
            if i + 1 < nst:
                rec.add("tensor", tr8(P, XBF),
                        waits=[w("aCast"), w("vXt"), w("vWe")], inc="tXt")
            rec.add("tensor", mm_pass(S1, WT1),
                    waits=[w("vK"), w("aSqK")], inc="tMM")
            do_k_evac(i)
            rec.add("tensor", tr8(P2, NAT), waits=[w("vQ"), w("vQt")],
                    inc="tQtr")
            rec.add("vector",
                    lambda e, i=i: e.tensor_copy(QTB[i % 2][:, :], P2[:, :]),
                    waits=[w("tQtr"), w("dSp%d" % (i % 2)), w("tKtr"),
                           w("aSqK")],
                    inc="vQt")
            dma("sync", qspa[i], QTB[i % 2][:, :], waits=[w("vQt")],
                inc="dSp%d" % (i % 2))
            rec.add("tensor", mm_pass(O, WT2), waits=[w("vV")], inc="tMM")
            if i + 1 < nst:
                rec.add("vector", lambda e: e.tensor_copy(XTB[:, :], P[:, :]),
                        waits=[w("tXt"), ("tMM", c("tMM"))], inc="vXt")
            rec.add("vector",
                    lambda e, i=i: e.scalar_tensor_tensor(V[:, i, :], O[:, :],
                                                          1.0, TA[:, :],
                                                          AL.mult, AL.add),
                    waits=[w("tMM"), w("aWc")], inc="vV")
            rec.add("tensor", tr8(P2, QTB[(i + 1) % 2]),
                    waits=[w("vK"), w("vQt")], inc="tKtr")
            rec.add("vector",
                    lambda e, i=i: e.tensor_copy(
                        KT[:, :, i * 128:(i + 1) * 128],
                        P2[:, :].rearrange("p (t q) -> p t q", q=128)),
                    waits=[w("tKtr")], inc="vKt")

        nA_mm = c("tMM")
        weight_prep(wo_d, WT0,
                    xst_w=[w("aCast")], dst_w=[("tMM", nA_mm)],
                    xbf_w=[w("tXt"), w("tQtr"), w("aSqQ"), w("aSqK")],
                    p_w=[w("vKt"), w("vQt")], nbuf=2)
        bcast_row(BC0[:, :], BC0, bo_d.ap()[:],
                  mm_waits=[w("aSqQ"), w("vV")], evac_waits=[w("vQ")],
                  dma_waits=[w("vQ")])

        rhs_r = rhs_d.ap().rearrange("(i p) o -> p i o", p=128)
        rq_done = {}

        def rhs_load_rt(j):
            stg = (XST, TB)[j % 2]
            pp = (S0, S1)[j % 2]
            dma("sync",
                stg[:, :].rearrange("p (i o) -> p i o", o=128),
                rhs_r[:, :, j * 128:(j + 1) * 128],
                waits=[w("dRh"), w("aWc"), w("aCast"), w("tWt"),
                       w("tRh", lag=1)],
                inc="dRh")

            def rt8(e, stg=stg, pp=pp):
                for t in range(HT):
                    r = e.transpose(pp[:, t * 128:(t + 1) * 128],
                                    stg[:, t * 128:(t + 1) * 128], IDF[:, :])
                return r
            waits = [w("dRh"), w("aSqQ"), w("aSqK"), ("tMM", nA_mm), w("vWe")]
            if j >= 2:
                waits.append(("vRq", rq_done.get(j - 2, 0)))
            rec.add("tensor", rt8, waits=waits, inc="tRh")

        def rhs_quant(j):
            pp = (S0, S1)[j % 2]
            qb = (QTB0, QTB1)[j % 2]
            rec.add("vector",
                    lambda e, pp=pp: e.tensor_reduce(SM[:, 0:1], pp[:, :], AXX,
                                                     AL.max,
                                                     apply_absolute_value=True),
                    waits=[("tRh", c("tRh"))], inc="vRq")
            rec.add("vector",
                    lambda e: e.tensor_scalar(SM[:, 0:1], SM[:, 0:1], 1e-12,
                                              None, AL.max),
                    inc="vRq")
            rec.add("vector",
                    lambda e, j=j: e.tensor_scalar(RSA[:, j:j + 1], SM[:, 0:1],
                                                   1.0 / 127.0, None, AL.mult),
                    inc="vRq")
            rec.add("vector", lambda e: e.reciprocal(SM[:, 1:2], SM[:, 0:1]),
                    inc="vRq")
            rec.add("vector",
                    lambda e: e.tensor_scalar(SM[:, 1:2], SM[:, 1:2], 127.0,
                                              None, AL.mult),
                    inc="vRq")
            rec.add("vector",
                    lambda e, pp=pp: e.tensor_scalar(TA[:, :], pp[:, :],
                                                     SM[:, 1:2], None,
                                                     AL.mult),
                    waits=[w("vV")], inc="vRq")
            rec.add("vector",
                    lambda e, qb=qb: e.tensor_scalar(qb[:, :], TA[:, :],
                                                     MG[:, 0:1], MG[:, 0:1],
                                                     AL.add, AL.subtract),
                    waits=[w("dSp0"), w("dSp1"), w("tRb", lag=1)], inc="vRq")
            rq_done[j] = c("vRq")

        def rhs_back(j):
            qb = (QTB0, QTB1)[j % 2]
            rec.add("tensor", tr8(P, qb),
                    waits=[("vRq", rq_done[j]), w("vMp"), w("vKt"), w("vQt"),
                           w("tWt")],
                    inc="tRb")
            rec.add("vector",
                    lambda e, j=j: e.tensor_copy(
                        WT1[:, :, j * 128:(j + 1) * 128],
                        P[:, :].rearrange("p (t q) -> p t q", q=128)),
                    waits=[w("tRb"), ("tMM", nA_mm)], inc="vMp")

        for j in range(HT):
            rhs_load_rt(j)
            if j >= 1:
                rhs_back(j - 1)
            rhs_quant(j)
        rhs_back(HT - 1)

        rec.add("sync",
                lambda e: e.dma_start(
                    out=bass.AP(scr_d, 0, [[HT, 128], [1, HT]]),
                    in_=RSA[:, :]),
                waits=[w("vRq")], inc="dRs", inc_by=16)
        for j in range(HT):
            rec.add("sync",
                    lambda e, j=j: e.dma_start(
                        out=BC1[0:1, j * 128:(j + 1) * 128],
                        in_=bass.AP(scr_d, j, [[HT, 128]]),
                        allow_slow_non_contiguous=True),
                    waits=[w("dRs"), w("vRq"), w("aMp"), w("vK")], inc="dRs",
                    inc_by=16)
        bcast_row(BC1[:, :], BC1, None,
                  mm_waits=[w("vRq"), w("dRs")], evac_waits=[w("vK")])

        rec.add("vector",
                lambda e: e.tensor_reduce(SM[:, 2:3], Kss[:, :], AXX, AL.max),
                waits=[w("aSqK")], inc="vMp")
        rec.add("sync",
                lambda e: e.dma_start(out=bass.AP(scr_d, 0, [[1, 128]]),
                                      in_=SM[:, 2:3]),
                waits=[w("vMp"), w("dRs")], inc="dRs", inc_by=16)
        rec.add("sync",
                lambda e: e.dma_start(out=TB[0:1, 0:128],
                                      in_=bass.AP(scr_d, 0, [[1, 128]])),
                waits=[w("dRs"), w("vWe")], inc="dRs", inc_by=16)
        rec.add("vector",
                lambda e: e.tensor_reduce(TB[0:1, 200:201], TB[0:1, 0:128],
                                          AXX, AL.max),
                waits=[w("dRs")], inc="vMp")

        def km_mm(e):
            return e.matmul(S0[:, 0:1], ONES[0:1, 0:128], TB[0:1, 200:201],
                            start=True, stop=True)
        rec.add("tensor", km_mm, waits=[w("vMp"), w("vWe")], inc="tWt")
        rec.add("vector", lambda e: e.tensor_copy(KMB[:, :], S0[:, 0:1]),
                waits=[w("tWt")], inc="vMp")
        rec.add("vector",
                lambda e: e.tensor_scalar(NEGC[:, :], Qss[:, :], KMB[:, 0:1],
                                          1e-30, AL.mult, AL.max),
                waits=[w("aSqQ")], inc="vMp")
        rec.add("scalar",
                lambda e: e.activation(NEGC[:, :], NEGC[:, :], AF.Ln,
                                       scale=1.0 / 512.0),
                waits=[w("vMp")], inc="aMp")
        rec.add("scalar",
                lambda e: e.activation(NEGC[:, :], NEGC[:, :], AF.Exp,
                                       scale=0.5),
                inc="aMp")
        rec.add("vector",
                lambda e: e.tensor_scalar(NEGC[:, :], NEGC[:, :], -1.0, -0.01,
                                          AL.mult, AL.add),
                waits=[w("aMp")], inc="vMp")

        fin = [w("dWst0"), w("dWst1"), w("dWst2"), w("dWst3"), w("dBr"),
               w("dRh"), w("dRs")]

        qt_ld = {}
        exp_done = {}
        pt_done = {}
        perm = {4: [1, 0, 3, 2], 2: [1, 0], 1: [0]}[n_ch]

        def emit_loads(t):
            dma("sync", XST[:, :], xa[t * 128:(t + 1) * 128, :],
                waits=[w("gXpb"), w("aWc"), w("vRq")], inc="dXc")
            sp = "dQt%d" % (t % 2)
            dma("sync", QTB[t % 2][:, :], qspa[t],
                waits=[w("tQlt"), w("dSp0"), w("dSp1"), w("vRq"), w("tRb")],
                inc=sp)
            qt_ld[t] = (sp, c(sp))

        def emit_xpb(t):
            rec.add("gpsimd",
                    lambda e: e.tensor_tensor(TA[:, :], XST[:, :], BC0[:, :],
                                              AL.add),
                    waits=[("dXc", 16 * (t + 1)), w("vFf"),
                           w("vV"), w("vRq"), ("tBc", c("tBc"))] + fin,
                    inc="gXpb")

        qk_done = {}

        def emit_qk_mm(t, ch, half=None, extra=()):
            ps = (S0, S1)[ch % 2]
            hs = (0, 1) if half is None else (half,)

            def qk(e, ch=ch, ps=ps, t=t, hs=hs):
                for h in hs:
                    for j in range(HT):
                        lt = QTB[t % 2][:, j * 128:(j + 1) * 128]
                        r = e.matmul(ps[:, h * 512:(h + 1) * 512], lt,
                                     KT[:, j, ch * 1024 + h * 512:
                                        ch * 1024 + (h + 1) * 512],
                                     start=(j == 0), stop=(j == HT - 1))
                return r
            rec.add("tensor", qk, waits=[qt_ld[t], w("vRq")] + list(extra),
                    inc="tQk")
            qk_done[(t, ch)] = c("tQk")

        def emit_exp(t, ch):
            rec.add("scalar",
                    lambda e, ch=ch, t=t: e.activation(
                        WT2[:, ch, :], (S0, S1)[ch % 2][:, :], AF.Exp,
                        bias=NEGC[:, t:t + 1], scale=SCALE,
                        accum_out=SUM4[:, ch:ch + 1]),
                    waits=[("tQk", qk_done[(t, ch)]), ("tMM", nA_mm),
                           w("tPv"), w("aMp"), w("vMp"), w("vRi")],
                    inc="aExp")
            exp_done[(t, ch)] = c("aExp")

        def emit_J2(t):
            rec.add("scalar",
                    lambda e: e.activation(XTB[:, :], TB[:, :], AF.Square,
                                           accum_out=SM[:, 13:14]),
                    waits=[w("vFf"), ("tXt", c("tXt"))], inc="aSq2")
            rec.add("vector",
                    lambda e: e.tensor_scalar(SM[:, 14:15], SM[:, 13:14],
                                              1.0 / H, 1e-5, AL.mult, AL.add),
                    waits=[w("aSq2")], inc="vOut")
            rec.add("vector",
                    lambda e: e.tensor_scalar(SM[:, 13:14], SM[:, 12:13],
                                              SM[:, 12:13], None, AL.mult),
                    inc="vOut")
            rec.add("vector",
                    lambda e: e.tensor_tensor(SM[:, 14:15], SM[:, 14:15],
                                              SM[:, 13:14], AL.subtract),
                    inc="vOut")
            rec.add("scalar",
                    lambda e: e.activation(SM[:, 14:15], SM[:, 14:15], AF.Ln),
                    waits=[w("vOut")], inc="aSq2")
            rec.add("scalar",
                    lambda e: e.activation(SM[:, 15:16], SM[:, 14:15], AF.Exp,
                                           scale=-0.5),
                    inc="aSq2")
            rec.add("vector",
                    lambda e: e.tensor_scalar(TB[:, :], TB[:, :], SM[:, 12:13],
                                              SM[:, 15:16], AL.add, AL.mult),
                    waits=[w("aSq2")], inc="vOut")
            dma("sync", outa[t * 128:(t + 1) * 128, :], TB[:, :],
                waits=[w("vOut")], inc="dOut")

        def mk_pt8(g, pg):
            def pt8(e, g=g, pg=pg):
                for tt in range(HT):
                    r = e.transpose(pg[:, tt * 128:(tt + 1) * 128],
                                    WT2[:, g, tt * 128:(tt + 1) * 128],
                                    IDB[:, :])
                return r
            return pt8

        def mk_pv(g):
            def pvg(e, g=g):
                for kt_ in range(8 * g, 8 * g + 8):
                    lt = WT2[:, kt_ // 8, (kt_ % 8) * 128:
                             (kt_ % 8 + 1) * 128]
                    for h in range(2):
                        r = e.matmul(O[:, h * 512:(h + 1) * 512], lt,
                                     V[:, kt_, h * 512:(h + 1) * 512],
                                     start=(kt_ == 0),
                                     stop=(kt_ == nst - 1))
                return r
            return pvg

        def mk_pv_half(g, h):
            def pvh(e, g=g, h=h):
                for kt_ in range(8 * g, 8 * g + 8):
                    lt = WT2[:, kt_ // 8, (kt_ % 8) * 128:
                             (kt_ % 8 + 1) * 128]
                    r = e.matmul(O[:, h * 512:(h + 1) * 512], lt,
                                 V[:, kt_, h * 512:(h + 1) * 512],
                                 start=(kt_ == 0),
                                 stop=(kt_ == nst - 1))
                return r
            return pvh

        def emit_pt(t, g):
            pg = (P, P2)[g % 2]
            waits = [("aExp", exp_done[(t, g)]), w("aPt", lag=blag)]
            if g == 0:
                waits += [w("aO2t"), w("tRb"), w("aQlt")]
            if g == 1:
                waits += [w("aQlt")]
            rec.add("tensor", mk_pt8(g, pg), waits=waits, inc="tPtr")
            rec.add("scalar",
                    lambda e, g=g, pg=pg: e.activation(
                        WT2[:, g, :], pg[:, :], AF.Copy),
                    waits=[w("tPtr")], inc="aPt")
            pt_done[(t, g)] = ("aPt", c("aPt"))

        def emit_pv(t, g):
            waits = [pt_done[(t, g)]]
            if g == 0:
                waits += [w("aO2"), w("vFf", lag=3)]
            if g == n_ch - 1:
                rec.add("tensor", mk_pv_half(g, 0), waits=waits,
                        inc="tPvA")
                rec.add("tensor", mk_pv_half(g, 1), inc="tPv")
            else:
                rec.add("tensor", mk_pv(g), waits=waits)

        emit_loads(0)
        emit_xpb(0)
        emit_loads(1)
        for ch in perm:
            emit_qk_mm(0, ch, extra=([w("vMp")] if ch % 2 == 0 else []))
            emit_exp(0, ch)

        for qt in range(nst):
            par = qt % 2
            rec.add("vector",
                    lambda e: e.tensor_reduce(SM[:, 3:4], SUM4[:, :], AXX,
                                              AL.add),
                    waits=[w("aExp")], inc="vRi")
            rec.add("vector", lambda e: e.reciprocal(SM[:, 4:5], SM[:, 3:4]),
                    inc="vRi")

            if qt >= 1:
                emit_xpb(qt)
            if qt >= 1 and qt + 1 < nst:
                emit_loads(qt + 1)

            if (qt, 0) not in pt_done:
                emit_pt(qt, 0)
            for g in range(1, min(n_ch, 3)):
                emit_pt(qt, g)
            for g in range(n_ch):
                emit_pv(qt, g)
                if g == 0 and n_ch >= 4:
                    emit_pt(qt, 3)

            if qt >= 1:
                emit_J2(qt - 1)

            def tr4(lo):
                pq = (P, P2)[lo]

                def f(e, lo=lo, pq=pq):
                    for t in range(4 * lo, 4 * lo + 4):
                        r = e.transpose(pq[:, t * 128:(t + 1) * 128],
                                        XBF[:, t * 128:(t + 1) * 128],
                                        IDB[:, :])
                    return r
                return f

            def atmm_half(lo):
                def f(e, lo=lo):
                    for j in range(4 * lo, 4 * lo + 4):
                        lt = WT2[:, 4, j * 128:(j + 1) * 128]
                        for h in range(2):
                            r = e.matmul(S0[:, h * 512:(h + 1) * 512], lt,
                                         WT0[:, j, h * 512:(h + 1) * 512],
                                         start=(j == 0), stop=(j == HT - 1))
                    return r
                return f

            rec.add("scalar",
                    lambda e: e.activation(XBF[:, 0:512], O[:, 0:512],
                                           AF.Copy),
                    waits=[w("tPvA"), w("tO2t")], inc="aO2")
            rec.add("scalar",
                    lambda e: e.activation(XBF[:, 512:1024], O[:, 512:1024],
                                           AF.Copy),
                    waits=[w("tPv")], inc="aO2")
            rec.add("tensor", tr4(0), waits=[w("aO2", lag=1)], inc="tO2t")
            if qt + 1 < nst and n_ch >= 2:
                emit_qk_mm(qt + 1, 1, half=0,
                           extra=[("aExp", exp_done[(qt, n_ch - 1)])])
            rec.add("tensor", tr4(1), waits=[w("aO2")], inc="tO2t")
            rec.add("scalar",
                    lambda e: e.activation(WT2[:, 4, 0:512], P[:, 0:512],
                                           AF.Copy),
                    waits=[w("tO2t", lag=1), ("tMM", nA_mm), w("tAt")],
                    inc="aO2t")
            rec.add("scalar",
                    lambda e: e.activation(WT2[:, 4, 512:1024],
                                           P2[:, 512:1024], AF.Copy),
                    waits=[w("tO2t")], inc="aO2t")
            rec.add("tensor", atmm_half(0),
                    waits=[w("aO2t", lag=1), w("aExp"), w("vX1p"), w("vWe")],
                    inc=None)
            rec.add("tensor", atmm_half(1), waits=[w("aO2t")], inc="tAt")
            if qt + 1 < nst and n_ch >= 2:
                emit_qk_mm(qt + 1, 1, half=1)
                emit_exp(qt + 1, 1)

            rec.add("vector",
                    lambda e: e.scalar_tensor_tensor(TA[:, :], S0[:, :],
                                                     SM[:, 4:5], TA[:, :],
                                                     AL.mult, AL.add),
                    waits=[w("tAt"), w("gXpb")], inc="vX1p")
            rec.add("vector",
                    lambda e: e.tensor_reduce(SM[:, 5:6], TA[:, :], AXX,
                                              AL.add, negate=True),
                    waits=[w("vX1p")], inc="vCen")
            rec.add("vector",
                    lambda e: e.tensor_scalar(SM[:, 5:6], SM[:, 5:6], 1.0 / H,
                                              None, AL.mult),
                    inc="vCen")
            rec.add("scalar",
                    lambda e: e.activation(XTB[:, :], TA[:, :], AF.Square,
                                           accum_out=SM[:, 6:7]),
                    waits=[w("vX1p"), ("tXt", c("tXt"))], inc="aSq1")
            rec.add("vector",
                    lambda e: e.tensor_reduce(SM[:, 9:10], TA[:, :], AXX,
                                              AL.max),
                    inc="vX1")
            rec.add("vector",
                    lambda e: e.tensor_reduce(SM[:, 10:11], TA[:, :], AXX,
                                              AL.min),
                    inc="vX1")
            rec.add("vector",
                    lambda e: e.tensor_scalar(SM[:, 7:8], SM[:, 6:7], 1.0 / H,
                                              1e-5, AL.mult, AL.add),
                    waits=[w("aSq1")], inc="vX1")
            rec.add("vector",
                    lambda e: e.tensor_scalar(SM[:, 6:7], SM[:, 5:6],
                                              SM[:, 5:6], None, AL.mult),
                    inc="vX1")
            rec.add("vector",
                    lambda e: e.tensor_tensor(SM[:, 7:8], SM[:, 7:8],
                                              SM[:, 6:7], AL.subtract),
                    inc="vX1")
            rec.add("scalar",
                    lambda e: e.activation(SM[:, 7:8], SM[:, 7:8], AF.Ln),
                    waits=[w("vX1")], inc="aSq1")
            rec.add("scalar",
                    lambda e: e.activation(SM[:, 8:9], SM[:, 7:8], AF.Exp,
                                           scale=-0.5),
                    inc="aSq1")

            if qt + 1 < nst:
                t = qt + 1
                if n_ch >= 4:
                    emit_qk_mm(t, 0, extra=[w("vX1p")])
                    emit_exp(t, 0)
                    emit_qk_mm(t, 3, extra=[("aExp", exp_done[(t, 1)])])
                    emit_exp(t, 3)
                elif n_ch == 1:
                    emit_qk_mm(t, 0, extra=[w("vX1p")])
                    emit_exp(t, 0)

            rec.add("vector",
                    lambda e: e.tensor_scalar(SM[:, 9:10], SM[:, 9:10],
                                              SM[:, 5:6], None, AL.add),
                    inc="vX1")
            rec.add("vector",
                    lambda e: e.tensor_scalar(SM[:, 10:11], SM[:, 10:11], -1.0,
                                              SM[:, 5:6], AL.mult,
                                              AL.subtract),
                    inc="vX1")
            rec.add("vector",
                    lambda e: e.tensor_tensor(SM[:, 9:10], SM[:, 9:10],
                                              SM[:, 10:11], AL.max),
                    inc="vX1")
            rec.add("vector",
                    lambda e: e.tensor_scalar(SM[:, 9:10], SM[:, 9:10], 1e-12,
                                              None, AL.max),
                    inc="vX1")
            rec.add("vector", lambda e: e.reciprocal(SM[:, 10:11], SM[:, 9:10]),
                    inc="vX1")
            rec.add("vector",
                    lambda e: e.tensor_scalar(SM[:, 10:11], SM[:, 10:11],
                                              127.0, None, AL.mult),
                    inc="vX1")
            rec.add("vector",
                    lambda e: e.tensor_scalar(TB[:, :], TA[:, :], SM[:, 5:6],
                                              SM[:, 10:11], AL.add, AL.mult),
                    waits=[w("aSq2", lag=2), ("dOut", 16 * qt)] + fin,
                    inc="vX1")
            rec.add("vector",
                    lambda e, par=par: e.tensor_scalar(
                        QTB[par][:, :], TB[:, :], MG[:, 0:1], MG[:, 0:1],
                        AL.add, AL.subtract),
                    inc="vQl")

            def ffn_half(lo):
                def f(e, lo=lo):
                    for j in range(4 * lo, 4 * lo + 4):
                        lt = NAT[:, j * 128:(j + 1) * 128]
                        for h in range(2):
                            r = e.matmul(O[:, h * 512:(h + 1) * 512], lt,
                                         WT1[:, j, h * 512:(h + 1) * 512],
                                         start=(j == 0), stop=(j == HT - 1))
                    return r
                return f

            def qlt4(lo):
                pq = (P2, P)[lo]

                def f(e, lo=lo, pq=pq, qb=QTB[par]):
                    for t in range(4 * lo, 4 * lo + 4):
                        r = e.transpose(pq[:, t * 128:(t + 1) * 128],
                                        qb[:, t * 128:(t + 1) * 128],
                                        IDB[:, :])
                    return r
                return f

            if qt + 1 < nst and n_ch >= 2:
                t = qt + 1
                ch = 2 if n_ch >= 4 else 0
                ext = ([("aExp", exp_done[(t, 0)])] if n_ch >= 4
                       else [w("vX1p")])
                emit_qk_mm(t, ch, extra=ext)
                emit_exp(t, ch)
            rec.add("tensor", qlt4(0),
                    waits=[w("vQl"), w("aSq1")], inc="tQlt")
            rec.add("vector",
                    lambda e: e.tensor_copy(NAT[:, 0:512], P2[:, 0:512]),
                    waits=[w("tQlt"), w("aSqK"), ("tKtr", c("tKtr"))],
                    inc="aQlt")
            rec.add("tensor", qlt4(1), waits=[w("vQl")], inc="tQlt")
            rec.add("vector",
                    lambda e: e.tensor_copy(NAT[:, 512:1024], P[:, 512:1024]),
                    waits=[w("tQlt")], inc="aQlt")
            rec.add("vector",
                    lambda e: e.tensor_scalar(SM[:, 11:12], SM[:, 9:10],
                                              1.0 / 127.0, None, AL.mult),
                    inc="vX1")
            rec.add("vector",
                    lambda e: e.tensor_scalar(SM[:, 11:12], SM[:, 11:12],
                                              SM[:, 8:9], None, AL.mult),
                    waits=[w("aSq1")], inc="vX1")
            rec.add("tensor", ffn_half(0),
                    waits=[w("aQlt", lag=1), w("aO2"), w("vMp")],
                    inc=None)
            if qt + 1 < nst:
                emit_pt(qt + 1, 0)
            rec.add("tensor", ffn_half(1), waits=[w("aQlt")], inc="tFf")

            rec.add("vector",
                    lambda e: e.scalar_tensor_tensor(TA[:, :], O[:, :],
                                                     SM[:, 11:12], BC1[:, :],
                                                     AL.mult, AL.mult),
                    waits=[w("tFf")], inc="vFf")
            rec.add("vector",
                    lambda e: e.scalar_tensor_tensor(TB[:, :], TB[:, :],
                                                     SM[:, 11:12], TA[:, :],
                                                     AL.mult, AL.add),
                    inc="vFf")
            rec.add("vector",
                    lambda e: e.tensor_reduce(SM[:, 12:13], TB[:, :], AXX,
                                              AL.add, negate=True),
                    inc="vFf")
            rec.add("vector",
                    lambda e: e.tensor_scalar(SM[:, 12:13], SM[:, 12:13],
                                              1.0 / H, None, AL.mult),
                    inc="vFf")

        emit_J2(nst - 1)

        rec.add("sync", lambda e: e.nop(), waits=[("dOut", 16 * nst)])

        with nc.Block() as block:
            def emit_stream(eng_name):
                last = {}
                cnt = {}
                self_chain = eng_name in ("vector", "scalar", "gpsimd")
                prev = [None]

                def run(e):
                    for ws, emit, inc, inc_by in rec.ops[eng_name]:
                        if self_chain and prev[0] is not None:
                            s, v = prev[0]
                            if last.get(s, -1) < v:
                                e.wait_ge(sems[s], v)
                                last[s] = v
                        for s, v in ws:
                            if last.get(s, -1) < v:
                                e.wait_ge(sems[s], v)
                                last[s] = v
                        instr = emit(e)
                        if inc is not None:
                            instr.then_inc(sems[inc], inc_by)
                            cnt[inc] = cnt.get(inc, 0) + inc_by
                            prev[0] = (inc, cnt[inc])
                return run

            block.sync(emit_stream("sync"))
            block.tensor(emit_stream("tensor"))
            block.vector(emit_stream("vector"))
            block.scalar(emit_stream("scalar"))
            block.gpsimd(emit_stream("gpsimd"))

    return nc


_CACHE = {}


def kernel(L0_attn_x, L0_attn_Wq, L0_attn_bq, L0_attn_Wk, L0_attn_bk,
           L0_attn_Wv, L0_attn_bv, L0_attn_Wo, L0_attn_bo, L0_ffn_rhs):
    x = np.ascontiguousarray(np.asarray(L0_attn_x, dtype=np.float32))
    B, S, _ = x.shape
    nst = S // 128
    if nst not in _CACHE:
        _CACHE[nst] = build(nst)
    nc = _CACHE[nst]

    com = {
        "wq": np.ascontiguousarray(L0_attn_Wq, dtype=np.float32),
        "wk": np.ascontiguousarray(L0_attn_Wk, dtype=np.float32),
        "wv": np.ascontiguousarray(L0_attn_Wv, dtype=np.float32),
        "wo": np.ascontiguousarray(L0_attn_Wo, dtype=np.float32),
        "bq": np.ascontiguousarray(L0_attn_bq, dtype=np.float32),
        "bk": np.ascontiguousarray(L0_attn_bk, dtype=np.float32),
        "bv": np.ascontiguousarray(L0_attn_bv, dtype=np.float32),
        "bo": np.ascontiguousarray(L0_attn_bo, dtype=np.float32),
        "rhs": np.ascontiguousarray(L0_ffn_rhs, dtype=np.float32),
        "idb": np.eye(128, dtype=ml_dtypes.bfloat16),
        "idf": np.eye(128, dtype=np.float32),
    }
    in_maps = [dict(com, x=np.ascontiguousarray(x[b])) for b in range(B)]
    res = run_bass_kernel_spmd(nc, in_maps, core_ids=list(range(B)))
    return np.stack([res.results[b]["out"] for b in range(B)], axis=0)


# version 30
# speedup vs baseline: 1.0664x; 1.0664x over previous
import math

import numpy as np
import ml_dtypes

import concourse.bass as bass
import concourse.mybir as mybir
from concourse.bass_utils import run_bass_kernel_spmd

F32 = mybir.dt.float32
BF16 = mybir.dt.bfloat16
AL = mybir.AluOpType
AF = mybir.ActivationFunctionType
AXX = mybir.AxisListType.X

H = 1024
HT = 8
NST = 32
SCALE = 1.0 / math.sqrt(512.0)

ENGS = ("sync", "tensor", "vector", "scalar", "gpsimd")


class Rec:

    def __init__(self):
        self.ops = {e: [] for e in ENGS}
        self.cnt = {}

    def c(self, sem):
        return self.cnt.get(sem, 0)

    def w(self, sem, lag=0):
        return (sem, self.cnt.get(sem, 0) - lag)

    def add(self, eng, emit, waits=(), inc=None, inc_by=1):
        ws = [(s, v) for (s, v) in waits if v > 0]
        self.ops[eng].append((ws, emit, inc, inc_by))
        if inc is not None:
            self.cnt[inc] = self.cnt.get(inc, 0) + inc_by


def build(nst=NST):
    nc = bass.Bass("TRN2", target_bir_lowering=False)
    S = nst * 128
    n_ch = max(1, S // 1024)
    blag = 1 if n_ch >= 2 else 0

    x_d = nc.dram_tensor("x", [S, H], F32, kind="ExternalInput")
    wq_d = nc.dram_tensor("wq", [H, H], F32, kind="ExternalInput")
    wk_d = nc.dram_tensor("wk", [H, H], F32, kind="ExternalInput")
    wv_d = nc.dram_tensor("wv", [H, H], F32, kind="ExternalInput")
    wo_d = nc.dram_tensor("wo", [H, H], F32, kind="ExternalInput")
    bq_d = nc.dram_tensor("bq", [H], F32, kind="ExternalInput")
    bk_d = nc.dram_tensor("bk", [H], F32, kind="ExternalInput")
    bv_d = nc.dram_tensor("bv", [H], F32, kind="ExternalInput")
    bo_d = nc.dram_tensor("bo", [H], F32, kind="ExternalInput")
    rhs_d = nc.dram_tensor("rhs", [H, H], F32, kind="ExternalInput")
    idb_d = nc.dram_tensor("idb", [128, 128], BF16, kind="ExternalInput")
    idf_d = nc.dram_tensor("idf", [128, 128], F32, kind="ExternalInput")
    out_d = nc.dram_tensor("out", [S, H], F32, kind="ExternalOutput")
    qsp_d = nc.dram_tensor("qsp", [nst, 128, H], BF16, kind="Internal")
    scr_d = nc.dram_tensor("scr", [1024], F32, kind="Internal")

    xa = x_d.ap()
    outa = out_d.ap()
    qspa = qsp_d.ap()

    rec = Rec()
    c, w = rec.c, rec.w

    from contextlib import ExitStack
    st = ExitStack()
    with st:
        KT = st.enter_context(nc.sbuf_tensor("KT", [128, HT, S], BF16))
        V = st.enter_context(nc.sbuf_tensor("V", [128, nst, H], BF16))
        WT0 = st.enter_context(nc.sbuf_tensor("WT0", [128, HT, H], BF16))
        WT1 = st.enter_context(nc.sbuf_tensor("WT1", [128, HT, H], BF16))
        WT2 = st.enter_context(nc.sbuf_tensor("WT2", [128, HT, H], BF16))
        BC0 = st.enter_context(nc.sbuf_tensor("BC0", [128, H], F32))
        BC1 = st.enter_context(nc.sbuf_tensor("BC1", [128, H], F32))
        TA = st.enter_context(nc.sbuf_tensor("TA", [128, H], F32))
        TB = st.enter_context(nc.sbuf_tensor("TB", [128, H], F32))
        XST = st.enter_context(nc.sbuf_tensor("XST", [128, H], F32))
        XS2 = nc.alloc_sbuf_tensor_at(
            "XS2", [128, H], F32,
            offset=nc.lookup_mloc(V).addr + (nst - 2) * H * 2)
        XBF = st.enter_context(nc.sbuf_tensor("XBF", [128, H], BF16))
        XTB = st.enter_context(nc.sbuf_tensor("XTB", [128, H], BF16))
        NAT = st.enter_context(nc.sbuf_tensor("NAT", [128, H], BF16))
        QTB0 = st.enter_context(nc.sbuf_tensor("QTB0", [128, H], BF16))
        QTB1 = st.enter_context(nc.sbuf_tensor("QTB1", [128, H], BF16))
        IDB = st.enter_context(nc.sbuf_tensor("IDB", [128, 128], BF16))
        IDF = st.enter_context(nc.sbuf_tensor("IDF", [128, 128], F32))
        ONES = st.enter_context(nc.sbuf_tensor("ONES", [1, 128], F32))
        Qss = st.enter_context(nc.sbuf_tensor("Qss", [128, nst], F32))
        Kss = st.enter_context(nc.sbuf_tensor("Kss", [128, nst], F32))
        NEGC = st.enter_context(nc.sbuf_tensor("NEGC", [128, nst], F32))
        SUM4 = st.enter_context(nc.sbuf_tensor("SUM4", [128, n_ch], F32))
        RSA = st.enter_context(nc.sbuf_tensor("RSA", [128, HT], F32))
        KMB = st.enter_context(nc.sbuf_tensor("KMB", [128, 1], F32))
        MG = st.enter_context(nc.sbuf_tensor("MG", [128, 1], F32))
        SM = st.enter_context(nc.sbuf_tensor("SM", [128, 16], F32))

        QTB = (QTB0, QTB1)

        S0 = st.enter_context(nc.psum_tensor("S0", [128, H], F32))
        S1 = st.enter_context(nc.psum_tensor("S1", [128, H], F32))
        O = st.enter_context(nc.psum_tensor("O", [128, H], F32))
        P = st.enter_context(nc.psum_tensor("P", [128, H], BF16))
        P2 = st.enter_context(nc.psum_tensor("P2", [128, H], BF16))

        sem_names = [
            "dWst0", "dWst1", "dWst2", "dWst3",
            "dBr", "dRh", "dRs", "dXa", "dSp0", "dSp1", "dQt0",
            "dQt1", "dXc", "dOut",
            "aWc", "tWt", "vWe", "tBc",
            "aCast", "tXt", "vXt", "tMM", "vQ", "tQtr", "vQt", "aSqQ",
            "vK", "tKtr", "vKt", "aSqK", "vV",
            "tRh", "vRq", "tRb", "vMp", "aMp",
            "tQk", "aExp", "vRi", "tPtr", "aPt", "vPt", "tPv", "tPvA", "aO2",
            "tO2t", "aO2t", "tAt", "vX1p", "vCen", "aSq1", "vX1",
            "vQl", "tQlt", "aQlt", "tFf", "vFf", "aSq2", "vOut", "gXpb",
        ]
        sems = {n: st.enter_context(nc.semaphore(n)) for n in sem_names}

        def dma(eng, out, in_, waits=(), inc=None):
            rec.add(eng, lambda e, o=out, i=in_: e.dma_start(out=o, in_=i),
                    waits=waits, inc=inc, inc_by=16)

        def tr8(dst_ps, src_sb):
            def f(e, d=dst_ps, s=src_sb):
                for t in range(HT):
                    r = e.transpose(d[:, t * 128:(t + 1) * 128],
                                    s[:, t * 128:(t + 1) * 128], IDB[:, :])
                return r
            return f

        dma("sync", IDB[:, :], idb_d.ap()[:, :], inc="dBr")
        dma("sync", IDF[:, :], idf_d.ap()[:, :], inc="dRh")

        wp_g = [0]

        def weight_prep(w_handle, dst, xst_w=(), dst_w=(), xbf_w=(), p_w=(),
                        xs2_w=(), nbuf=3):
            wap = w_handle.ap()
            for j in range(HT):
                gj = wp_g[0]
                wp_g[0] += 1
                if nbuf == 3:
                    stg = (XST, TB, XS2)[gj % 3]
                    sw = "dWst%d" % (j % 4)
                    lagw = 2
                else:
                    stg = (XST, TB)[j % 2]
                    sw = "dWst%d" % (j % 2)
                    lagw = 1
                cb = (XBF, NAT)[j % 2]
                pp = (P, P2)[j % 2]
                xw = list(xst_w) + (list(xs2_w) if stg is XS2 else [])
                dma(("sync", "gpsimd")[j % 2], stg[:, :],
                    wap[j * 128:(j + 1) * 128, :],
                    waits=[w("aWc", lag=lagw)] + xw, inc=sw)
                rec.add("scalar",
                        lambda e, stg=stg, cb=cb: e.activation(
                            cb[:, :], stg[:, :], AF.Copy),
                        waits=[w(sw), w("tWt", lag=1)] + list(xbf_w),
                        inc="aWc")
                rec.add("tensor", tr8(pp, cb),
                        waits=[w("aWc"), w("vWe", lag=1), ("dBr", 16)]
                        + list(p_w),
                        inc="tWt")
                rec.add("vector",
                        lambda e, j=j, dst=dst, pp=pp: e.tensor_copy(
                            dst[:, :, j * 128:(j + 1) * 128],
                            pp[:, :].rearrange("p (i q) -> p i q", q=128)),
                        waits=[w("tWt")] + list(dst_w), inc="vWe")

        rec.add("vector", lambda e: e.memset(ONES[:, :], 1.0), inc="vWe")
        rec.add("vector", lambda e: e.memset(MG[:, :], 2.0 ** -30), inc="vWe")
        rec.add("vector",
                lambda e: e.tensor_scalar(MG[:, :], MG[:, :], 1.0, 1.0,
                                          AL.add, AL.subtract),
                inc="vWe")
        rec.add("vector",
                lambda e: e.tensor_scalar(MG[:, :], MG[:, :],
                                          float(1.5 * 2.0 ** 82 - 1.5 * 2.0 ** 53),
                                          float(1.5 * 2.0 ** 23),
                                          AL.mult, AL.add),
                inc="vWe")

        def bcast_row(dst, stg, src_dma_in=None, mm_waits=(), evac_waits=(),
                      dma_waits=(), n=H):
            if src_dma_in is not None:
                dma("sync", stg[0:1, 0:n], src_dma_in,
                    waits=[w("dBr")] + list(dma_waits), inc="dBr")

            def mm(e, n=n, stg=stg):
                nh = (n + 511) // 512
                for hh in range(nh):
                    q = min(512, n - hh * 512)
                    r = e.matmul(O[:, hh * 512:hh * 512 + q], ONES[0:1, 0:128],
                                 stg[0:1, hh * 512:hh * 512 + q],
                                 start=True, stop=True)
                return r
            rec.add("tensor", mm, waits=[w("dBr"), w("vWe")] + list(mm_waits),
                    inc="tBc")
            rec.add("vector",
                    lambda e, dst=dst, n=n: e.tensor_copy(dst, O[:, 0:n]),
                    waits=[w("tBc")] + list(evac_waits), inc="vWe")

        dma("sync", XS2[:, :], xa[0:128, :], inc="dXa")
        rec.add("scalar",
                lambda e: e.activation(XTB[:, :], XS2[:, :], AF.Copy),
                waits=[("dXa", 16)], inc="aCast")
        rec.add("tensor", tr8(P, XTB),
                waits=[w("aCast"), w("vWe"), ("dBr", 16)], inc="tXt")
        rec.add("vector", lambda e: e.tensor_copy(XTB[:, :], P[:, :]),
                waits=[w("tXt")], inc="vXt")

        weight_prep(wq_d, WT0, xs2_w=[("aCast", 1)], p_w=[("vXt", 1)])
        for bh, dstb in ((bq_d, BC0), (bk_d, BC1), (bv_d, TA)):
            dma("scalar", dstb[:, :],
                bass.AP(bh, 0, [[0, 128], [1, H]]),
                waits=[w("dBr")], inc="dBr")
        weight_prep(wk_d, WT1, xs2_w=[("aCast", 1)])
        weight_prep(wv_d, WT2, xs2_w=[("aCast", 1)])

        dma("sync", XST[:, :], xa[128:256, :],
            waits=[w("aCast"), ("aWc", c("aWc") - 2)], inc="dXa")

        def do_k_evac(i):
            kb = QTB[(i + 1) % 2]
            rec.add("vector",
                    lambda e, kb=kb: e.scalar_tensor_tensor(
                        kb[:, :], S1[:, :], 1.0, BC1[:, :], AL.mult, AL.add),
                    waits=[w("tMM"), w("tKtr"), w("aSqK"), ("dBr", 48),
                           w("dSp%d" % ((i + 1) % 2))],
                    inc="vK")
            rec.add("scalar",
                    lambda e, i=i, kb=kb: e.activation(
                        S1[:, :], kb[:, :], AF.Square,
                        accum_out=Kss[:, i:i + 1]),
                    waits=[w("vK")], inc="aSqK")

        def mm_pass(ps, wt):
            def f(e, ps=ps, wt=wt):
                for j in range(HT):
                    lt = XTB[:, j * 128:(j + 1) * 128]
                    for h in range(2):
                        r = e.matmul(ps[:, h * 512:(h + 1) * 512], lt,
                                     wt[:, j, h * 512:(h + 1) * 512],
                                     start=(j == 0), stop=(j == HT - 1))
                return r
            return f

        for i in range(nst):
            if i + 1 < nst:
                rec.add("scalar",
                        lambda e: e.activation(XBF[:, :], XST[:, :], AF.Copy),
                        waits=[("dXa", 16 * (i + 2)), w("tXt"), w("tWt")],
                        inc="aCast")
            if i + 2 < nst:
                dma("sync", XST[:, :], xa[(i + 2) * 128:(i + 3) * 128, :],
                    waits=[w("aCast")], inc="dXa")
            rec.add("tensor", mm_pass(S0, WT0),
                    waits=[w("vXt"), w("vQ"), w("aSqQ")], inc="tMM")
            rec.add("vector",
                    lambda e: e.scalar_tensor_tensor(NAT[:, :], S0[:, :], 1.0,
                                                     BC0[:, :], AL.mult,
                                                     AL.add),
                    waits=[w("tMM"), w("tKtr"), w("aSqK"), ("dBr", 32)],
                    inc="vQ")
            rec.add("scalar",
                    lambda e, i=i: e.activation(S0[:, :], NAT[:, :], AF.Square,
                                                accum_out=Qss[:, i:i + 1]),
                    waits=[w("vQ")], inc="aSqQ")
            if i + 1 < nst:
                rec.add("tensor", tr8(P, XBF),
                        waits=[w("aCast"), w("vXt"), w("vWe")], inc="tXt")
            rec.add("tensor", mm_pass(S1, WT1),
                    waits=[w("vK"), w("aSqK")], inc="tMM")
            do_k_evac(i)
            rec.add("tensor", tr8(P2, NAT), waits=[w("vQ"), w("vQt")],
                    inc="tQtr")
            rec.add("vector",
                    lambda e, i=i: e.tensor_copy(QTB[i % 2][:, :], P2[:, :]),
                    waits=[w("tQtr"), w("dSp%d" % (i % 2)), w("tKtr"),
                           w("aSqK")],
                    inc="vQt")
            dma("sync", qspa[i], QTB[i % 2][:, :], waits=[w("vQt")],
                inc="dSp%d" % (i % 2))
            rec.add("tensor", mm_pass(O, WT2), waits=[w("vV")], inc="tMM")
            if i + 1 < nst:
                rec.add("vector", lambda e: e.tensor_copy(XTB[:, :], P[:, :]),
                        waits=[w("tXt"), ("tMM", c("tMM"))], inc="vXt")
            rec.add("vector",
                    lambda e, i=i: e.scalar_tensor_tensor(V[:, i, :], O[:, :],
                                                          1.0, TA[:, :],
                                                          AL.mult, AL.add),
                    waits=[w("tMM"), w("aWc"), ("dBr", 64)], inc="vV")
            rec.add("tensor", tr8(P2, QTB[(i + 1) % 2]),
                    waits=[w("vK"), w("vQt")], inc="tKtr")
            rec.add("vector",
                    lambda e, i=i: e.tensor_copy(
                        KT[:, :, i * 128:(i + 1) * 128],
                        P2[:, :].rearrange("p (t q) -> p t q", q=128)),
                    waits=[w("tKtr")], inc="vKt")

        nA_mm = c("tMM")
        weight_prep(wo_d, WT0,
                    xst_w=[w("aCast")], dst_w=[("tMM", nA_mm)],
                    xbf_w=[w("tXt"), w("tQtr"), w("aSqQ"), w("aSqK")],
                    p_w=[w("vKt"), w("vQt")], nbuf=2)
        dma("scalar", BC0[:, :], bass.AP(bo_d, 0, [[0, 128], [1, H]]),
            waits=[w("vQ")], inc="dBr")

        rhs_r = rhs_d.ap().rearrange("(i p) o -> p i o", p=128)
        rq_done = {}

        def rhs_load_rt(j):
            stg = (XST, TB)[j % 2]
            pp = (S0, S1)[j % 2]
            dma("sync",
                stg[:, :].rearrange("p (i o) -> p i o", o=128),
                rhs_r[:, :, j * 128:(j + 1) * 128],
                waits=[w("dRh"), w("aWc"), w("aCast"), w("tWt"),
                       w("tRh", lag=1)],
                inc="dRh")

            def rt8(e, stg=stg, pp=pp):
                for t in range(HT):
                    r = e.transpose(pp[:, t * 128:(t + 1) * 128],
                                    stg[:, t * 128:(t + 1) * 128], IDF[:, :])
                return r
            waits = [w("dRh"), w("aSqQ"), w("aSqK"), ("tMM", nA_mm), w("vWe")]
            if j >= 2:
                waits.append(("vRq", rq_done.get(j - 2, 0)))
            rec.add("tensor", rt8, waits=waits, inc="tRh")

        def rhs_quant(j):
            pp = (S0, S1)[j % 2]
            qb = (QTB0, QTB1)[j % 2]
            rec.add("vector",
                    lambda e, pp=pp: e.tensor_reduce(SM[:, 0:1], pp[:, :], AXX,
                                                     AL.max,
                                                     apply_absolute_value=True),
                    waits=[("tRh", c("tRh"))], inc="vRq")
            rec.add("vector",
                    lambda e: e.tensor_scalar(SM[:, 0:1], SM[:, 0:1], 1e-12,
                                              None, AL.max),
                    inc="vRq")
            rec.add("vector",
                    lambda e, j=j: e.tensor_scalar(RSA[:, j:j + 1], SM[:, 0:1],
                                                   1.0 / 127.0, None, AL.mult),
                    inc="vRq")
            rec.add("vector", lambda e: e.reciprocal(SM[:, 1:2], SM[:, 0:1]),
                    inc="vRq")
            rec.add("vector",
                    lambda e: e.tensor_scalar(SM[:, 1:2], SM[:, 1:2], 127.0,
                                              None, AL.mult),
                    inc="vRq")
            rec.add("vector",
                    lambda e, pp=pp: e.tensor_scalar(TA[:, :], pp[:, :],
                                                     SM[:, 1:2], None,
                                                     AL.mult),
                    waits=[w("vV")], inc="vRq")
            rec.add("vector",
                    lambda e, qb=qb: e.tensor_scalar(qb[:, :], TA[:, :],
                                                     MG[:, 0:1], MG[:, 0:1],
                                                     AL.add, AL.subtract),
                    waits=[w("dSp0"), w("dSp1"), w("tRb", lag=1)], inc="vRq")
            rq_done[j] = c("vRq")

        def rhs_back(j):
            qb = (QTB0, QTB1)[j % 2]
            rec.add("tensor", tr8(P, qb),
                    waits=[("vRq", rq_done[j]), w("vMp"), w("vKt"), w("vQt"),
                           w("tWt")],
                    inc="tRb")
            rec.add("vector",
                    lambda e, j=j: e.tensor_copy(
                        WT1[:, :, j * 128:(j + 1) * 128],
                        P[:, :].rearrange("p (t q) -> p t q", q=128)),
                    waits=[w("tRb"), ("tMM", nA_mm)], inc="vMp")

        for j in range(HT):
            rhs_load_rt(j)
            if j >= 1:
                rhs_back(j - 1)
            rhs_quant(j)
        rhs_back(HT - 1)

        rec.add("sync",
                lambda e: e.dma_start(
                    out=bass.AP(scr_d, 0, [[1, 128], [128, HT]]),
                    in_=RSA[:, :],
                    allow_slow_non_contiguous=True),
                waits=[w("vRq")], inc="dRs", inc_by=16)
        dma("scalar", BC1[:, :], bass.AP(scr_d, 0, [[0, 128], [1, H]]),
            waits=[w("dRs"), w("vK")], inc="dRs")

        rec.add("vector",
                lambda e: e.tensor_reduce(SM[:, 2:3], Kss[:, :], AXX, AL.max),
                waits=[w("aSqK")], inc="vMp")
        rec.add("sync",
                lambda e: e.dma_start(out=bass.AP(scr_d, 0, [[1, 128]]),
                                      in_=SM[:, 2:3]),
                waits=[w("vMp"), w("dRs")], inc="dRs", inc_by=16)
        rec.add("sync",
                lambda e: e.dma_start(out=TB[0:1, 0:128],
                                      in_=bass.AP(scr_d, 0, [[1, 128]])),
                waits=[w("dRs"), w("vWe")], inc="dRs", inc_by=16)
        rec.add("vector",
                lambda e: e.tensor_reduce(TB[0:1, 200:201], TB[0:1, 0:128],
                                          AXX, AL.max),
                waits=[w("dRs")], inc="vMp")
        rec.add("sync",
                lambda e: e.dma_start(out=bass.AP(scr_d, 0, [[1, 1]]),
                                      in_=TB[0:1, 200:201]),
                waits=[w("vMp")], inc="dRs", inc_by=16)
        dma("sync", KMB[:, :], bass.AP(scr_d, 0, [[0, 128], [1, 1]]),
            waits=[w("dRs")], inc="dRs")
        rec.add("vector",
                lambda e: e.tensor_scalar(NEGC[:, :], Qss[:, :], KMB[:, 0:1],
                                          1e-30, AL.mult, AL.max),
                waits=[w("aSqQ"), w("dRs")], inc="vMp")
        rec.add("scalar",
                lambda e: e.activation(NEGC[:, :], NEGC[:, :], AF.Ln,
                                       scale=1.0 / 512.0),
                waits=[w("vMp")], inc="aMp")
        rec.add("scalar",
                lambda e: e.activation(NEGC[:, :], NEGC[:, :], AF.Exp,
                                       scale=0.5),
                inc="aMp")
        rec.add("vector",
                lambda e: e.tensor_scalar(NEGC[:, :], NEGC[:, :], -1.0, -0.01,
                                          AL.mult, AL.add),
                waits=[w("aMp")], inc="vMp")

        fin = [w("dWst0"), w("dWst1"), w("dWst2"), w("dWst3"), w("dBr"),
               w("dRh"), w("dRs")]

        qt_ld = {}
        exp_done = {}
        pt_done = {}
        perm = {4: [1, 0, 3, 2], 2: [1, 0], 1: [0]}[n_ch]

        def emit_loads(t):
            dma("sync", XST[:, :], xa[t * 128:(t + 1) * 128, :],
                waits=[w("gXpb"), w("aWc"), w("vRq")], inc="dXc")
            sp = "dQt%d" % (t % 2)
            dma("sync", QTB[t % 2][:, :], qspa[t],
                waits=[w("tQlt"), w("dSp0"), w("dSp1"), w("vRq"), w("tRb")],
                inc=sp)
            qt_ld[t] = (sp, c(sp))

        def emit_xpb(t):
            rec.add("gpsimd",
                    lambda e: e.tensor_tensor(TA[:, :], XST[:, :], BC0[:, :],
                                              AL.add),
                    waits=[("dXc", 16 * (t + 1)), w("vFf"),
                           w("vV"), w("vRq"), ("tBc", c("tBc"))] + fin,
                    inc="gXpb")

        qk_done = {}

        def emit_qk_mm(t, ch, half=None, extra=()):
            ps = (S0, S1)[ch % 2]
            hs = (0, 1) if half is None else (half,)

            def qk(e, ch=ch, ps=ps, t=t, hs=hs):
                for h in hs:
                    for j in range(HT):
                        lt = QTB[t % 2][:, j * 128:(j + 1) * 128]
                        r = e.matmul(ps[:, h * 512:(h + 1) * 512], lt,
                                     KT[:, j, ch * 1024 + h * 512:
                                        ch * 1024 + (h + 1) * 512],
                                     start=(j == 0), stop=(j == HT - 1))
                return r
            rec.add("tensor", qk, waits=[qt_ld[t], w("vRq")] + list(extra),
                    inc="tQk")
            qk_done[(t, ch)] = c("tQk")

        def emit_exp(t, ch):
            rec.add("scalar",
                    lambda e, ch=ch, t=t: e.activation(
                        WT2[:, ch, :], (S0, S1)[ch % 2][:, :], AF.Exp,
                        bias=NEGC[:, t:t + 1], scale=SCALE,
                        accum_out=SUM4[:, ch:ch + 1]),
                    waits=[("tQk", qk_done[(t, ch)]), ("tMM", nA_mm),
                           w("tPv"), w("aMp"), w("vMp"), w("vRi")],
                    inc="aExp")
            exp_done[(t, ch)] = c("aExp")

        def emit_J2(t):
            rec.add("scalar",
                    lambda e: e.activation(XTB[:, :], TB[:, :], AF.Square,
                                           accum_out=SM[:, 13:14]),
                    waits=[w("vFf"), ("tXt", c("tXt"))], inc="aSq2")
            rec.add("vector",
                    lambda e: e.tensor_scalar(SM[:, 14:15], SM[:, 13:14],
                                              1.0 / H, 1e-5, AL.mult, AL.add),
                    waits=[w("aSq2")], inc="vOut")
            rec.add("vector",
                    lambda e: e.tensor_scalar(SM[:, 13:14], SM[:, 12:13],
                                              SM[:, 12:13], None, AL.mult),
                    inc="vOut")
            rec.add("vector",
                    lambda e: e.tensor_tensor(SM[:, 14:15], SM[:, 14:15],
                                              SM[:, 13:14], AL.subtract),
                    inc="vOut")
            rec.add("scalar",
                    lambda e: e.activation(SM[:, 14:15], SM[:, 14:15], AF.Ln),
                    waits=[w("vOut")], inc="aSq2")
            rec.add("scalar",
                    lambda e: e.activation(SM[:, 15:16], SM[:, 14:15], AF.Exp,
                                           scale=-0.5),
                    inc="aSq2")
            rec.add("vector",
                    lambda e: e.tensor_scalar(TB[:, :], TB[:, :], SM[:, 12:13],
                                              SM[:, 15:16], AL.add, AL.mult),
                    waits=[w("aSq2")], inc="vOut")
            dma("sync", outa[t * 128:(t + 1) * 128, :], TB[:, :],
                waits=[w("vOut")], inc="dOut")

        def mk_pt8(g, pg):
            def pt8(e, g=g, pg=pg):
                for tt in range(HT):
                    r = e.transpose(pg[:, tt * 128:(tt + 1) * 128],
                                    WT2[:, g, tt * 128:(tt + 1) * 128],
                                    IDB[:, :])
                return r
            return pt8

        def mk_pv(g):
            def pvg(e, g=g):
                for kt_ in range(8 * g, 8 * g + 8):
                    lt = WT2[:, kt_ // 8, (kt_ % 8) * 128:
                             (kt_ % 8 + 1) * 128]
                    for h in range(2):
                        r = e.matmul(O[:, h * 512:(h + 1) * 512], lt,
                                     V[:, kt_, h * 512:(h + 1) * 512],
                                     start=(kt_ == 0),
                                     stop=(kt_ == nst - 1))
                return r
            return pvg

        def mk_pv_half(g, h):
            def pvh(e, g=g, h=h):
                for kt_ in range(8 * g, 8 * g + 8):
                    lt = WT2[:, kt_ // 8, (kt_ % 8) * 128:
                             (kt_ % 8 + 1) * 128]
                    r = e.matmul(O[:, h * 512:(h + 1) * 512], lt,
                                 V[:, kt_, h * 512:(h + 1) * 512],
                                 start=(kt_ == 0),
                                 stop=(kt_ == nst - 1))
                return r
            return pvh

        def emit_pt(t, g):
            pg = (P, P2)[g % 2]
            waits = [("aExp", exp_done[(t, g)]), w("aPt", lag=blag)]
            if g == 0:
                waits += [w("aO2t"), w("tRb"), w("aQlt")]
            if g == 1:
                waits += [w("aQlt")]
            rec.add("tensor", mk_pt8(g, pg), waits=waits, inc="tPtr")
            rec.add("scalar",
                    lambda e, g=g, pg=pg: e.activation(
                        WT2[:, g, :], pg[:, :], AF.Copy),
                    waits=[w("tPtr")], inc="aPt")
            pt_done[(t, g)] = ("aPt", c("aPt"))

        def emit_pv(t, g):
            waits = [pt_done[(t, g)]]
            if g == 0:
                waits += [w("aO2"), w("vFf", lag=3)]
            if g == n_ch - 1:
                rec.add("tensor", mk_pv_half(g, 0), waits=waits,
                        inc="tPvA")
                rec.add("tensor", mk_pv_half(g, 1), inc="tPv")
            else:
                rec.add("tensor", mk_pv(g), waits=waits)

        emit_loads(0)
        emit_xpb(0)
        emit_loads(1)
        for ch in perm:
            emit_qk_mm(0, ch)
            emit_exp(0, ch)

        for qt in range(nst):
            par = qt % 2
            rec.add("vector",
                    lambda e: e.tensor_reduce(SM[:, 3:4], SUM4[:, :], AXX,
                                              AL.add),
                    waits=[w("aExp")], inc="vRi")
            rec.add("vector", lambda e: e.reciprocal(SM[:, 4:5], SM[:, 3:4]),
                    inc="vRi")

            if qt >= 1:
                emit_xpb(qt)
            if qt >= 1 and qt + 1 < nst:
                emit_loads(qt + 1)

            if (qt, 0) not in pt_done:
                emit_pt(qt, 0)
            for g in range(1, min(n_ch, 3)):
                emit_pt(qt, g)
            for g in range(n_ch):
                emit_pv(qt, g)
                if g == 0 and n_ch >= 4:
                    emit_pt(qt, 3)

            if qt >= 1:
                emit_J2(qt - 1)

            def tr4(lo):
                pq = (P, P2)[lo]

                def f(e, lo=lo, pq=pq):
                    for t in range(4 * lo, 4 * lo + 4):
                        r = e.transpose(pq[:, t * 128:(t + 1) * 128],
                                        XBF[:, t * 128:(t + 1) * 128],
                                        IDB[:, :])
                    return r
                return f

            def atmm_half(lo):
                def f(e, lo=lo):
                    for j in range(4 * lo, 4 * lo + 4):
                        lt = WT2[:, 4, j * 128:(j + 1) * 128]
                        for h in range(2):
                            r = e.matmul(S0[:, h * 512:(h + 1) * 512], lt,
                                         WT0[:, j, h * 512:(h + 1) * 512],
                                         start=(j == 0), stop=(j == HT - 1))
                    return r
                return f

            rec.add("scalar",
                    lambda e: e.activation(XBF[:, 0:512], O[:, 0:512],
                                           AF.Copy),
                    waits=[w("tPvA"), w("tO2t")], inc="aO2")
            rec.add("scalar",
                    lambda e: e.activation(XBF[:, 512:1024], O[:, 512:1024],
                                           AF.Copy),
                    waits=[w("tPv")], inc="aO2")
            rec.add("tensor", tr4(0), waits=[w("aO2", lag=1)], inc="tO2t")
            if qt + 1 < nst and n_ch >= 2:
                emit_qk_mm(qt + 1, 1, half=0,
                           extra=[("aExp", exp_done[(qt, n_ch - 1)])])
            rec.add("tensor", tr4(1), waits=[w("aO2")], inc="tO2t")
            rec.add("scalar",
                    lambda e: e.activation(WT2[:, 4, 0:512], P[:, 0:512],
                                           AF.Copy),
                    waits=[w("tO2t", lag=1), ("tMM", nA_mm), w("tAt")],
                    inc="aO2t")
            rec.add("scalar",
                    lambda e: e.activation(WT2[:, 4, 512:1024],
                                           P2[:, 512:1024], AF.Copy),
                    waits=[w("tO2t")], inc="aO2t")
            rec.add("tensor", atmm_half(0),
                    waits=[w("aO2t", lag=1), w("aExp"), w("vX1p"), w("vWe")],
                    inc=None)
            rec.add("tensor", atmm_half(1), waits=[w("aO2t")], inc="tAt")
            if qt + 1 < nst and n_ch >= 2:
                emit_qk_mm(qt + 1, 1, half=1)
                emit_exp(qt + 1, 1)

            rec.add("vector",
                    lambda e: e.scalar_tensor_tensor(TA[:, :], S0[:, :],
                                                     SM[:, 4:5], TA[:, :],
                                                     AL.mult, AL.add),
                    waits=[w("tAt"), w("gXpb")], inc="vX1p")
            rec.add("vector",
                    lambda e: e.tensor_reduce(SM[:, 5:6], TA[:, :], AXX,
                                              AL.add, negate=True),
                    waits=[w("vX1p")], inc="vCen")
            rec.add("vector",
                    lambda e: e.tensor_scalar(SM[:, 5:6], SM[:, 5:6], 1.0 / H,
                                              None, AL.mult),
                    inc="vCen")
            rec.add("scalar",
                    lambda e: e.activation(XTB[:, :], TA[:, :], AF.Square,
                                           accum_out=SM[:, 6:7]),
                    waits=[w("vX1p"), ("tXt", c("tXt"))], inc="aSq1")
            rec.add("vector",
                    lambda e: e.tensor_reduce(SM[:, 9:10], TA[:, :], AXX,
                                              AL.max),
                    inc="vX1")
            rec.add("vector",
                    lambda e: e.tensor_reduce(SM[:, 10:11], TA[:, :], AXX,
                                              AL.min),
                    inc="vX1")
            rec.add("vector",
                    lambda e: e.tensor_scalar(SM[:, 7:8], SM[:, 6:7], 1.0 / H,
                                              1e-5, AL.mult, AL.add),
                    waits=[w("aSq1")], inc="vX1")
            rec.add("vector",
                    lambda e: e.tensor_scalar(SM[:, 6:7], SM[:, 5:6],
                                              SM[:, 5:6], None, AL.mult),
                    inc="vX1")
            rec.add("vector",
                    lambda e: e.tensor_tensor(SM[:, 7:8], SM[:, 7:8],
                                              SM[:, 6:7], AL.subtract),
                    inc="vX1")
            rec.add("scalar",
                    lambda e: e.activation(SM[:, 7:8], SM[:, 7:8], AF.Ln),
                    waits=[w("vX1")], inc="aSq1")
            rec.add("scalar",
                    lambda e: e.activation(SM[:, 8:9], SM[:, 7:8], AF.Exp,
                                           scale=-0.5),
                    inc="aSq1")

            if qt + 1 < nst:
                t = qt + 1
                if n_ch >= 4:
                    emit_qk_mm(t, 0, extra=[w("vX1p")])
                    emit_exp(t, 0)
                    emit_qk_mm(t, 3, extra=[("aExp", exp_done[(t, 1)])])
                    emit_exp(t, 3)
                elif n_ch == 1:
                    emit_qk_mm(t, 0, extra=[w("vX1p")])
                    emit_exp(t, 0)

            rec.add("vector",
                    lambda e: e.tensor_scalar(SM[:, 9:10], SM[:, 9:10],
                                              SM[:, 5:6], None, AL.add),
                    inc="vX1")
            rec.add("vector",
                    lambda e: e.tensor_scalar(SM[:, 10:11], SM[:, 10:11], -1.0,
                                              SM[:, 5:6], AL.mult,
                                              AL.subtract),
                    inc="vX1")
            rec.add("vector",
                    lambda e: e.tensor_tensor(SM[:, 9:10], SM[:, 9:10],
                                              SM[:, 10:11], AL.max),
                    inc="vX1")
            rec.add("vector",
                    lambda e: e.tensor_scalar(SM[:, 9:10], SM[:, 9:10], 1e-12,
                                              None, AL.max),
                    inc="vX1")
            rec.add("vector", lambda e: e.reciprocal(SM[:, 10:11], SM[:, 9:10]),
                    inc="vX1")
            rec.add("vector",
                    lambda e: e.tensor_scalar(SM[:, 10:11], SM[:, 10:11],
                                              127.0, None, AL.mult),
                    inc="vX1")
            rec.add("vector",
                    lambda e: e.tensor_scalar(TB[:, :], TA[:, :], SM[:, 5:6],
                                              SM[:, 10:11], AL.add, AL.mult),
                    waits=[w("aSq2", lag=2), ("dOut", 16 * qt)] + fin,
                    inc="vX1")
            rec.add("vector",
                    lambda e, par=par: e.tensor_scalar(
                        QTB[par][:, :], TB[:, :], MG[:, 0:1], MG[:, 0:1],
                        AL.add, AL.subtract),
                    inc="vQl")

            def ffn_half(lo):
                def f(e, lo=lo):
                    for j in range(4 * lo, 4 * lo + 4):
                        lt = NAT[:, j * 128:(j + 1) * 128]
                        for h in range(2):
                            r = e.matmul(O[:, h * 512:(h + 1) * 512], lt,
                                         WT1[:, j, h * 512:(h + 1) * 512],
                                         start=(j == 0), stop=(j == HT - 1))
                    return r
                return f

            def qlt4(lo):
                pq = (P2, P)[lo]

                def f(e, lo=lo, pq=pq, qb=QTB[par]):
                    for t in range(4 * lo, 4 * lo + 4):
                        r = e.transpose(pq[:, t * 128:(t + 1) * 128],
                                        qb[:, t * 128:(t + 1) * 128],
                                        IDB[:, :])
                    return r
                return f

            if qt + 1 < nst and n_ch >= 2:
                t = qt + 1
                ch = 2 if n_ch >= 4 else 0
                ext = ([("aExp", exp_done[(t, 0)])] if n_ch >= 4
                       else [w("vX1p")])
                emit_qk_mm(t, ch, extra=ext)
                emit_exp(t, ch)
            rec.add("tensor", qlt4(0),
                    waits=[w("vQl"), w("aSq1")], inc="tQlt")
            rec.add("vector",
                    lambda e: e.tensor_copy(NAT[:, 0:512], P2[:, 0:512]),
                    waits=[w("tQlt"), w("aSqK"), ("tKtr", c("tKtr"))],
                    inc="aQlt")
            rec.add("tensor", qlt4(1), waits=[w("vQl")], inc="tQlt")
            rec.add("vector",
                    lambda e: e.tensor_copy(NAT[:, 512:1024], P[:, 512:1024]),
                    waits=[w("tQlt")], inc="aQlt")
            rec.add("vector",
                    lambda e: e.tensor_scalar(SM[:, 11:12], SM[:, 9:10],
                                              1.0 / 127.0, None, AL.mult),
                    inc="vX1")
            rec.add("vector",
                    lambda e: e.tensor_scalar(SM[:, 11:12], SM[:, 11:12],
                                              SM[:, 8:9], None, AL.mult),
                    waits=[w("aSq1")], inc="vX1")
            rec.add("tensor", ffn_half(0),
                    waits=[w("aQlt", lag=1), w("aO2"), w("vMp")],
                    inc=None)
            if qt + 1 < nst:
                emit_pt(qt + 1, 0)
            rec.add("tensor", ffn_half(1), waits=[w("aQlt")], inc="tFf")

            rec.add("vector",
                    lambda e: e.scalar_tensor_tensor(TA[:, :], O[:, :],
                                                     SM[:, 11:12], BC1[:, :],
                                                     AL.mult, AL.mult),
                    waits=[w("tFf"), w("dRs")], inc="vFf")
            rec.add("vector",
                    lambda e: e.scalar_tensor_tensor(TB[:, :], TB[:, :],
                                                     SM[:, 11:12], TA[:, :],
                                                     AL.mult, AL.add),
                    inc="vFf")
            rec.add("vector",
                    lambda e: e.tensor_reduce(SM[:, 12:13], TB[:, :], AXX,
                                              AL.add, negate=True),
                    inc="vFf")
            rec.add("vector",
                    lambda e: e.tensor_scalar(SM[:, 12:13], SM[:, 12:13],
                                              1.0 / H, None, AL.mult),
                    inc="vFf")

        emit_J2(nst - 1)

        rec.add("sync", lambda e: e.nop(), waits=[("dOut", 16 * nst)])

        with nc.Block() as block:
            def emit_stream(eng_name):
                last = {}
                cnt = {}
                self_chain = eng_name in ("vector", "scalar", "gpsimd")
                prev = [None]

                def run(e):
                    for ws, emit, inc, inc_by in rec.ops[eng_name]:
                        if self_chain and prev[0] is not None:
                            s, v = prev[0]
                            if last.get(s, -1) < v:
                                e.wait_ge(sems[s], v)
                                last[s] = v
                        for s, v in ws:
                            if last.get(s, -1) < v:
                                e.wait_ge(sems[s], v)
                                last[s] = v
                        instr = emit(e)
                        if inc is not None:
                            instr.then_inc(sems[inc], inc_by)
                            cnt[inc] = cnt.get(inc, 0) + inc_by
                            prev[0] = (inc, cnt[inc])
                return run

            block.sync(emit_stream("sync"))
            block.tensor(emit_stream("tensor"))
            block.vector(emit_stream("vector"))
            block.scalar(emit_stream("scalar"))
            block.gpsimd(emit_stream("gpsimd"))

    return nc


_CACHE = {}


def kernel(L0_attn_x, L0_attn_Wq, L0_attn_bq, L0_attn_Wk, L0_attn_bk,
           L0_attn_Wv, L0_attn_bv, L0_attn_Wo, L0_attn_bo, L0_ffn_rhs):
    x = np.ascontiguousarray(np.asarray(L0_attn_x, dtype=np.float32))
    B, S, _ = x.shape
    nst = S // 128
    if nst not in _CACHE:
        _CACHE[nst] = build(nst)
    nc = _CACHE[nst]

    com = {
        "wq": np.ascontiguousarray(L0_attn_Wq, dtype=np.float32),
        "wk": np.ascontiguousarray(L0_attn_Wk, dtype=np.float32),
        "wv": np.ascontiguousarray(L0_attn_Wv, dtype=np.float32),
        "wo": np.ascontiguousarray(L0_attn_Wo, dtype=np.float32),
        "bq": np.ascontiguousarray(L0_attn_bq, dtype=np.float32),
        "bk": np.ascontiguousarray(L0_attn_bk, dtype=np.float32),
        "bv": np.ascontiguousarray(L0_attn_bv, dtype=np.float32),
        "bo": np.ascontiguousarray(L0_attn_bo, dtype=np.float32),
        "rhs": np.ascontiguousarray(L0_ffn_rhs, dtype=np.float32),
        "idb": np.eye(128, dtype=ml_dtypes.bfloat16),
        "idf": np.eye(128, dtype=np.float32),
    }
    in_maps = [dict(com, x=np.ascontiguousarray(x[b])) for b in range(B)]
    res = run_bass_kernel_spmd(nc, in_maps, core_ids=list(range(B)))
    return np.stack([res.results[b]["out"] for b in range(B)], axis=0)


# version 31
# speedup vs baseline: 1.0741x; 1.0072x over previous
import math

import numpy as np
import ml_dtypes

import concourse.bass as bass
import concourse.mybir as mybir
from concourse.bass_utils import run_bass_kernel_spmd

F32 = mybir.dt.float32
BF16 = mybir.dt.bfloat16
AL = mybir.AluOpType
AF = mybir.ActivationFunctionType
AXX = mybir.AxisListType.X

H = 1024
HT = 8
NST = 32
SCALE = 1.0 / math.sqrt(512.0)

ENGS = ("sync", "tensor", "vector", "scalar", "gpsimd")


class Rec:

    def __init__(self):
        self.ops = {e: [] for e in ENGS}
        self.cnt = {}

    def c(self, sem):
        return self.cnt.get(sem, 0)

    def w(self, sem, lag=0):
        return (sem, self.cnt.get(sem, 0) - lag)

    def add(self, eng, emit, waits=(), inc=None, inc_by=1):
        ws = [(s, v) for (s, v) in waits if v > 0]
        self.ops[eng].append((ws, emit, inc, inc_by))
        if inc is not None:
            self.cnt[inc] = self.cnt.get(inc, 0) + inc_by


def build(nst=NST):
    nc = bass.Bass("TRN2", target_bir_lowering=False)
    S = nst * 128
    n_ch = max(1, S // 1024)
    blag = 1 if n_ch >= 2 else 0

    x_d = nc.dram_tensor("x", [S, H], F32, kind="ExternalInput")
    wq_d = nc.dram_tensor("wq", [H, H], F32, kind="ExternalInput")
    wk_d = nc.dram_tensor("wk", [H, H], F32, kind="ExternalInput")
    wv_d = nc.dram_tensor("wv", [H, H], F32, kind="ExternalInput")
    wo_d = nc.dram_tensor("wo", [H, H], F32, kind="ExternalInput")
    bq_d = nc.dram_tensor("bq", [H], F32, kind="ExternalInput")
    bk_d = nc.dram_tensor("bk", [H], F32, kind="ExternalInput")
    bv_d = nc.dram_tensor("bv", [H], F32, kind="ExternalInput")
    bo_d = nc.dram_tensor("bo", [H], F32, kind="ExternalInput")
    rhs_d = nc.dram_tensor("rhs", [H, H], F32, kind="ExternalInput")
    idb_d = nc.dram_tensor("idb", [128, 128], BF16, kind="ExternalInput")
    idf_d = nc.dram_tensor("idf", [128, 128], F32, kind="ExternalInput")
    out_d = nc.dram_tensor("out", [S, H], F32, kind="ExternalOutput")
    qsp_d = nc.dram_tensor("qsp", [nst, 128, H], BF16, kind="Internal")
    scr_d = nc.dram_tensor("scr", [1024], F32, kind="Internal")

    xa = x_d.ap()
    outa = out_d.ap()
    qspa = qsp_d.ap()

    rec = Rec()
    c, w = rec.c, rec.w

    from contextlib import ExitStack
    st = ExitStack()
    with st:
        KT = st.enter_context(nc.sbuf_tensor("KT", [128, HT, S], BF16))
        V = st.enter_context(nc.sbuf_tensor("V", [128, nst, H], BF16))
        WT0 = st.enter_context(nc.sbuf_tensor("WT0", [128, HT, H], BF16))
        WT1 = st.enter_context(nc.sbuf_tensor("WT1", [128, HT, H], BF16))
        WT2 = st.enter_context(nc.sbuf_tensor("WT2", [128, HT, H], BF16))
        BC0 = st.enter_context(nc.sbuf_tensor("BC0", [128, H], F32))
        BC1 = st.enter_context(nc.sbuf_tensor("BC1", [128, H], F32))
        TA = st.enter_context(nc.sbuf_tensor("TA", [128, H], F32))
        TB = st.enter_context(nc.sbuf_tensor("TB", [128, H], F32))
        XST = st.enter_context(nc.sbuf_tensor("XST", [128, H], F32))
        XS2 = nc.alloc_sbuf_tensor_at(
            "XS2", [128, H], F32,
            offset=nc.lookup_mloc(V).addr + (nst - 2) * H * 2)
        XBF = st.enter_context(nc.sbuf_tensor("XBF", [128, H], BF16))
        XTB = st.enter_context(nc.sbuf_tensor("XTB", [128, H], BF16))
        NAT = st.enter_context(nc.sbuf_tensor("NAT", [128, H], BF16))
        QTB0 = st.enter_context(nc.sbuf_tensor("QTB0", [128, H], BF16))
        QTB1 = st.enter_context(nc.sbuf_tensor("QTB1", [128, H], BF16))
        IDB = st.enter_context(nc.sbuf_tensor("IDB", [128, 128], BF16))
        IDF = st.enter_context(nc.sbuf_tensor("IDF", [128, 128], F32))
        ONES = st.enter_context(nc.sbuf_tensor("ONES", [1, 128], F32))
        Qss = st.enter_context(nc.sbuf_tensor("Qss", [128, nst], F32))
        Kss = st.enter_context(nc.sbuf_tensor("Kss", [128, nst], F32))
        NEGC = st.enter_context(nc.sbuf_tensor("NEGC", [128, nst], F32))
        SUM4 = st.enter_context(nc.sbuf_tensor("SUM4", [128, n_ch], F32))
        RSA = st.enter_context(nc.sbuf_tensor("RSA", [128, HT], F32))
        KMB = st.enter_context(nc.sbuf_tensor("KMB", [128, 1], F32))
        MG = st.enter_context(nc.sbuf_tensor("MG", [128, 1], F32))
        MGN = st.enter_context(nc.sbuf_tensor("MGN", [128, 1], F32))
        SM = st.enter_context(nc.sbuf_tensor("SM", [128, 16], F32))

        QTB = (QTB0, QTB1)

        S0 = st.enter_context(nc.psum_tensor("S0", [128, H], F32))
        S1 = st.enter_context(nc.psum_tensor("S1", [128, H], F32))
        O = st.enter_context(nc.psum_tensor("O", [128, H], F32))
        P = st.enter_context(nc.psum_tensor("P", [128, H], BF16))
        P2 = st.enter_context(nc.psum_tensor("P2", [128, H], BF16))

        sem_names = [
            "dWst0", "dWst1", "dWst2", "dWst3",
            "dBr", "dRh", "dRs", "dXa", "dSp0", "dSp1", "dQt0",
            "dQt1", "dXc", "dOut",
            "aWc", "tWt", "vWe", "tBc",
            "aCast", "tXt", "vXt", "tMM", "vQ", "tQtr", "vQt", "aSqQ",
            "vK", "tKtr", "vKt", "aSqK", "vV",
            "tRh", "vRq", "tRb", "vMp", "aMp", "dRh1", "aRn",
            "tQk", "aExp", "vRi", "tPtr", "aPt", "vPt", "tPv", "tPvA", "aO2",
            "tO2t", "aO2t", "tAt", "vX1p", "vCen", "aSq1", "vX1",
            "vQl", "tQlt", "aQlt", "tFf", "vFf", "aSq2", "vOut", "gXpb",
        ]
        sems = {n: st.enter_context(nc.semaphore(n)) for n in sem_names}

        def dma(eng, out, in_, waits=(), inc=None):
            rec.add(eng, lambda e, o=out, i=in_: e.dma_start(out=o, in_=i),
                    waits=waits, inc=inc, inc_by=16)

        def tr8(dst_ps, src_sb):
            def f(e, d=dst_ps, s=src_sb):
                for t in range(HT):
                    r = e.transpose(d[:, t * 128:(t + 1) * 128],
                                    s[:, t * 128:(t + 1) * 128], IDB[:, :])
                return r
            return f

        dma("sync", IDB[:, :], idb_d.ap()[:, :], inc="dBr")
        dma("sync", IDF[:, :], idf_d.ap()[:, :], inc="dRh")

        wp_g = [0]

        def weight_prep(w_handle, dst, xst_w=(), dst_w=(), xbf_w=(), p_w=(),
                        xs2_w=(), nbuf=3):
            wap = w_handle.ap()
            for j in range(HT):
                gj = wp_g[0]
                wp_g[0] += 1
                if nbuf == 3:
                    stg = (XST, TB, XS2)[gj % 3]
                    sw = "dWst%d" % (j % 4)
                    lagw = 2
                else:
                    stg = (XST, TB)[j % 2]
                    sw = "dWst%d" % (j % 2)
                    lagw = 1
                cb = (XBF, NAT)[j % 2]
                pp = (P, P2)[j % 2]
                xw = list(xst_w) + (list(xs2_w) if stg is XS2 else [])
                dma(("sync", "gpsimd")[j % 2], stg[:, :],
                    wap[j * 128:(j + 1) * 128, :],
                    waits=[w("aWc", lag=lagw)] + xw, inc=sw)
                rec.add("scalar",
                        lambda e, stg=stg, cb=cb: e.activation(
                            cb[:, :], stg[:, :], AF.Copy),
                        waits=[w(sw), w("tWt", lag=1)] + list(xbf_w),
                        inc="aWc")
                rec.add("tensor", tr8(pp, cb),
                        waits=[w("aWc"), w("vWe", lag=1), ("dBr", 16)]
                        + list(p_w),
                        inc="tWt")
                rec.add("vector",
                        lambda e, j=j, dst=dst, pp=pp: e.tensor_copy(
                            dst[:, :, j * 128:(j + 1) * 128],
                            pp[:, :].rearrange("p (i q) -> p i q", q=128)),
                        waits=[w("tWt")] + list(dst_w), inc="vWe")

        rec.add("vector", lambda e: e.memset(ONES[:, :], 1.0), inc="vWe")
        rec.add("vector", lambda e: e.memset(MG[:, :], 2.0 ** -30), inc="vWe")
        rec.add("vector",
                lambda e: e.tensor_scalar(MG[:, :], MG[:, :], 1.0, 1.0,
                                          AL.add, AL.subtract),
                inc="vWe")
        rec.add("vector",
                lambda e: e.tensor_scalar(MG[:, :], MG[:, :],
                                          float(1.5 * 2.0 ** 82 - 1.5 * 2.0 ** 53),
                                          float(1.5 * 2.0 ** 23),
                                          AL.mult, AL.add),
                inc="vWe")
        rec.add("vector",
                lambda e: e.tensor_scalar(MGN[:, :], MG[:, :], -1.0, None,
                                          AL.mult),
                inc="vWe")

        def bcast_row(dst, stg, src_dma_in=None, mm_waits=(), evac_waits=(),
                      dma_waits=(), n=H):
            if src_dma_in is not None:
                dma("sync", stg[0:1, 0:n], src_dma_in,
                    waits=[w("dBr")] + list(dma_waits), inc="dBr")

            def mm(e, n=n, stg=stg):
                nh = (n + 511) // 512
                for hh in range(nh):
                    q = min(512, n - hh * 512)
                    r = e.matmul(O[:, hh * 512:hh * 512 + q], ONES[0:1, 0:128],
                                 stg[0:1, hh * 512:hh * 512 + q],
                                 start=True, stop=True)
                return r
            rec.add("tensor", mm, waits=[w("dBr"), w("vWe")] + list(mm_waits),
                    inc="tBc")
            rec.add("vector",
                    lambda e, dst=dst, n=n: e.tensor_copy(dst, O[:, 0:n]),
                    waits=[w("tBc")] + list(evac_waits), inc="vWe")

        dma("sync", XS2[:, :], xa[0:128, :], inc="dXa")
        rec.add("scalar",
                lambda e: e.activation(XTB[:, :], XS2[:, :], AF.Copy),
                waits=[("dXa", 16)], inc="aCast")
        rec.add("tensor", tr8(P, XTB),
                waits=[w("aCast"), w("vWe"), ("dBr", 16)], inc="tXt")
        rec.add("vector", lambda e: e.tensor_copy(XTB[:, :], P[:, :]),
                waits=[w("tXt")], inc="vXt")

        weight_prep(wq_d, WT0, xs2_w=[("aCast", 1)], p_w=[("vXt", 1)])
        for bh, dstb in ((bq_d, BC0), (bk_d, BC1), (bv_d, TA)):
            dma("scalar", dstb[:, :],
                bass.AP(bh, 0, [[0, 128], [1, H]]),
                waits=[w("dBr")], inc="dBr")
        weight_prep(wk_d, WT1, xs2_w=[("aCast", 1)])
        weight_prep(wv_d, WT2, xs2_w=[("aCast", 1)])

        dma("sync", XST[:, :], xa[128:256, :],
            waits=[w("aCast"), ("aWc", c("aWc") - 2)], inc="dXa")

        def do_k_evac(i):
            kb = QTB[(i + 1) % 2]
            rec.add("vector",
                    lambda e, kb=kb: e.scalar_tensor_tensor(
                        kb[:, :], S1[:, :], 1.0, BC1[:, :], AL.mult, AL.add),
                    waits=[w("tMM"), w("tKtr"), w("aSqK"), ("dBr", 48),
                           w("dSp%d" % ((i + 1) % 2))],
                    inc="vK")
            rec.add("scalar",
                    lambda e, i=i, kb=kb: e.activation(
                        S1[:, :], kb[:, :], AF.Square,
                        accum_out=Kss[:, i:i + 1]),
                    waits=[w("vK")], inc="aSqK")

        def mm_pass(ps, wt):
            def f(e, ps=ps, wt=wt):
                for j in range(HT):
                    lt = XTB[:, j * 128:(j + 1) * 128]
                    for h in range(2):
                        r = e.matmul(ps[:, h * 512:(h + 1) * 512], lt,
                                     wt[:, j, h * 512:(h + 1) * 512],
                                     start=(j == 0), stop=(j == HT - 1))
                return r
            return f

        for i in range(nst):
            if i + 1 < nst:
                rec.add("scalar",
                        lambda e: e.activation(XBF[:, :], XST[:, :], AF.Copy),
                        waits=[("dXa", 16 * (i + 2)), w("tXt"), w("tWt")],
                        inc="aCast")
            if i + 2 < nst:
                dma("sync", XST[:, :], xa[(i + 2) * 128:(i + 3) * 128, :],
                    waits=[w("aCast")], inc="dXa")
            rec.add("tensor", mm_pass(S0, WT0),
                    waits=[w("vXt"), w("vQ"), w("aSqQ")], inc="tMM")
            rec.add("vector",
                    lambda e: e.scalar_tensor_tensor(NAT[:, :], S0[:, :], 1.0,
                                                     BC0[:, :], AL.mult,
                                                     AL.add),
                    waits=[w("tMM"), w("tKtr"), w("aSqK"), ("dBr", 32)],
                    inc="vQ")
            rec.add("scalar",
                    lambda e, i=i: e.activation(S0[:, :], NAT[:, :], AF.Square,
                                                accum_out=Qss[:, i:i + 1]),
                    waits=[w("vQ")], inc="aSqQ")
            if i + 1 < nst:
                rec.add("tensor", tr8(P, XBF),
                        waits=[w("aCast"), w("vXt"), w("vWe")], inc="tXt")
            rec.add("tensor", mm_pass(S1, WT1),
                    waits=[w("vK"), w("aSqK")], inc="tMM")
            do_k_evac(i)
            rec.add("tensor", tr8(P2, NAT), waits=[w("vQ"), w("vQt")],
                    inc="tQtr")
            rec.add("vector",
                    lambda e, i=i: e.tensor_copy(QTB[i % 2][:, :], P2[:, :]),
                    waits=[w("tQtr"), w("dSp%d" % (i % 2)), w("tKtr"),
                           w("aSqK")],
                    inc="vQt")
            dma("sync", qspa[i], QTB[i % 2][:, :], waits=[w("vQt")],
                inc="dSp%d" % (i % 2))
            rec.add("tensor", mm_pass(O, WT2), waits=[w("vV")], inc="tMM")
            if i + 1 < nst:
                rec.add("vector", lambda e: e.tensor_copy(XTB[:, :], P[:, :]),
                        waits=[w("tXt"), ("tMM", c("tMM"))], inc="vXt")
            rec.add("vector",
                    lambda e, i=i: e.scalar_tensor_tensor(V[:, i, :], O[:, :],
                                                          1.0, TA[:, :],
                                                          AL.mult, AL.add),
                    waits=[w("tMM"), w("aWc"), ("dBr", 64)], inc="vV")
            rec.add("tensor", tr8(P2, QTB[(i + 1) % 2]),
                    waits=[w("vK"), w("vQt")], inc="tKtr")
            rec.add("vector",
                    lambda e, i=i: e.tensor_copy(
                        KT[:, :, i * 128:(i + 1) * 128],
                        P2[:, :].rearrange("p (t q) -> p t q", q=128)),
                    waits=[w("tKtr")], inc="vKt")

        nA_mm = c("tMM")
        weight_prep(wo_d, WT0,
                    xst_w=[w("aCast")], dst_w=[("tMM", nA_mm)],
                    xbf_w=[w("tXt"), w("tQtr"), w("aSqQ"), w("aSqK")],
                    p_w=[w("vKt"), w("vQt")], nbuf=2)
        dma("scalar", BC0[:, :], bass.AP(bo_d, 0, [[0, 128], [1, H]]),
            waits=[w("vQ")], inc="dBr")

        rhs_r = rhs_d.ap().rearrange("(i p) o -> p i o", p=128)
        rq_done = {}
        sread = {}

        def rhs_load_rt(j):
            stg = (XST, TB)[j % 2]
            pp = (S0, S1)[j % 2]
            sw = ("dRh", "dRh1")[j % 2]
            dma(("sync", "gpsimd")[j % 2],
                stg[:, :].rearrange("p (i o) -> p i o", o=128),
                rhs_r[:, :, j * 128:(j + 1) * 128],
                waits=[w(sw), w("aWc"), w("aCast"), w("tWt"),
                       w("tRh", lag=1)],
                inc=sw)

            def rt8(e, stg=stg, pp=pp):
                for t in range(HT):
                    r = e.transpose(pp[:, t * 128:(t + 1) * 128],
                                    stg[:, t * 128:(t + 1) * 128], IDF[:, :])
                return r
            waits = [w(sw), ("dRh", 16), w("aSqQ"), w("aSqK"),
                     ("tMM", nA_mm), w("vWe")]
            if j >= 2:
                waits.append(sread[j - 2])
            rec.add("tensor", rt8, waits=waits, inc="tRh")

        def rhs_quant(j):
            pp = (S0, S1)[j % 2]
            qb = (XBF, NAT)[j % 2]
            rec.add("vector",
                    lambda e, pp=pp: e.tensor_reduce(SM[:, 0:1], pp[:, :], AXX,
                                                     AL.max,
                                                     apply_absolute_value=True),
                    waits=[("tRh", c("tRh"))], inc="vRq")
            rec.add("vector",
                    lambda e: e.tensor_scalar(SM[:, 0:1], SM[:, 0:1], 1e-12,
                                              None, AL.max),
                    inc="vRq")
            rec.add("vector",
                    lambda e, j=j: e.tensor_scalar(RSA[:, j:j + 1], SM[:, 0:1],
                                                   1.0 / 127.0, None, AL.mult),
                    inc="vRq")
            rec.add("vector", lambda e: e.reciprocal(SM[:, 1:2], SM[:, 0:1]),
                    inc="vRq")
            rec.add("vector",
                    lambda e: e.tensor_scalar(SM[:, 1:2], SM[:, 1:2], 127.0,
                                              None, AL.mult),
                    inc="vRq")
            rec.add("scalar",
                    lambda e, pp=pp: e.activation(TA[:, :], pp[:, :],
                                                  AF.Identity,
                                                  scale=SM[:, 1:2],
                                                  bias=MG[:, 0:1]),
                    waits=[("vRq", c("vRq")), w("vV")], inc="aRn")
            sread[j] = ("aRn", c("aRn"))
            rec.add("scalar",
                    lambda e, qb=qb: e.activation(qb[:, :], TA[:, :],
                                                  AF.Identity,
                                                  bias=MGN[:, 0:1]),
                    waits=[w("aWc"), w("tWt"), w("tRb", lag=1)], inc="aRn")
            rq_done[j] = ("aRn", c("aRn"))

        def rhs_back(j):
            qb = (XBF, NAT)[j % 2]
            rec.add("tensor", tr8(P, qb),
                    waits=[rq_done[j], w("vMp"), w("vWe"), w("vKt"),
                           w("vQt"), w("tWt")],
                    inc="tRb")
            rec.add("vector",
                    lambda e, j=j: e.tensor_copy(
                        WT1[:, :, j * 128:(j + 1) * 128],
                        P[:, :].rearrange("p (t q) -> p t q", q=128)),
                    waits=[w("tRb"), ("tMM", nA_mm)], inc="vMp")

        for j in range(HT):
            rhs_load_rt(j)
            if j >= 1:
                rhs_back(j - 1)
            rhs_quant(j)
        rhs_back(HT - 1)

        rec.add("sync",
                lambda e: e.dma_start(
                    out=bass.AP(scr_d, 0, [[1, 128], [128, HT]]),
                    in_=RSA[:, :],
                    allow_slow_non_contiguous=True),
                waits=[w("vRq")], inc="dRs", inc_by=16)
        dma("scalar", BC1[:, :], bass.AP(scr_d, 0, [[0, 128], [1, H]]),
            waits=[w("dRs"), w("vK")], inc="dRs")

        rec.add("vector",
                lambda e: e.tensor_reduce(SM[:, 2:3], Kss[:, :], AXX, AL.max),
                waits=[w("aSqK")], inc="vMp")
        rec.add("sync",
                lambda e: e.dma_start(out=bass.AP(scr_d, 0, [[1, 128]]),
                                      in_=SM[:, 2:3]),
                waits=[w("vMp"), w("dRs")], inc="dRs", inc_by=16)
        rec.add("sync",
                lambda e: e.dma_start(out=TB[0:1, 0:128],
                                      in_=bass.AP(scr_d, 0, [[1, 128]])),
                waits=[w("dRs"), w("vWe")], inc="dRs", inc_by=16)
        rec.add("vector",
                lambda e: e.tensor_reduce(TB[0:1, 200:201], TB[0:1, 0:128],
                                          AXX, AL.max),
                waits=[w("dRs")], inc="vMp")
        rec.add("sync",
                lambda e: e.dma_start(out=bass.AP(scr_d, 0, [[1, 1]]),
                                      in_=TB[0:1, 200:201]),
                waits=[w("vMp")], inc="dRs", inc_by=16)
        dma("sync", KMB[:, :], bass.AP(scr_d, 0, [[0, 128], [1, 1]]),
            waits=[w("dRs")], inc="dRs")
        rec.add("vector",
                lambda e: e.tensor_scalar(NEGC[:, :], Qss[:, :], KMB[:, 0:1],
                                          1e-30, AL.mult, AL.max),
                waits=[w("aSqQ"), w("dRs")], inc="vMp")
        rec.add("scalar",
                lambda e: e.activation(NEGC[:, :], NEGC[:, :], AF.Ln,
                                       scale=1.0 / 512.0),
                waits=[w("vMp")], inc="aMp")
        rec.add("scalar",
                lambda e: e.activation(NEGC[:, :], NEGC[:, :], AF.Exp,
                                       scale=0.5),
                inc="aMp")
        rec.add("vector",
                lambda e: e.tensor_scalar(NEGC[:, :], NEGC[:, :], -1.0, -0.01,
                                          AL.mult, AL.add),
                waits=[w("aMp")], inc="vMp")

        fin = [w("dWst0"), w("dWst1"), w("dWst2"), w("dWst3"), w("dBr"),
               w("dRh"), w("dRh1"), w("dRs")]

        qt_ld = {}
        exp_done = {}
        pt_done = {}
        perm = {4: [1, 0, 3, 2], 2: [1, 0], 1: [0]}[n_ch]

        def emit_loads(t):
            dma("sync", XST[:, :], xa[t * 128:(t + 1) * 128, :],
                waits=[w("gXpb"), w("aWc"), w("vRq")], inc="dXc")
            sp = "dQt%d" % (t % 2)
            dma("sync", QTB[t % 2][:, :], qspa[t],
                waits=[w("tQlt"), w("dSp0"), w("dSp1"), w("tKtr")],
                inc=sp)
            qt_ld[t] = (sp, c(sp))

        def emit_xpb(t):
            rec.add("gpsimd",
                    lambda e: e.tensor_tensor(TA[:, :], XST[:, :], BC0[:, :],
                                              AL.add),
                    waits=[("dXc", 16 * (t + 1)), w("vFf"),
                           w("vV"), w("aRn")] + fin,
                    inc="gXpb")

        qk_done = {}

        def emit_qk_mm(t, ch, half=None, extra=()):
            ps = (S0, S1)[ch % 2]
            hs = (0, 1) if half is None else (half,)

            def qk(e, ch=ch, ps=ps, t=t, hs=hs):
                for h in hs:
                    for j in range(HT):
                        lt = QTB[t % 2][:, j * 128:(j + 1) * 128]
                        r = e.matmul(ps[:, h * 512:(h + 1) * 512], lt,
                                     KT[:, j, ch * 1024 + h * 512:
                                        ch * 1024 + (h + 1) * 512],
                                     start=(j == 0), stop=(j == HT - 1))
                return r
            rec.add("tensor", qk, waits=[qt_ld[t], w("aRn")] + list(extra),
                    inc="tQk")
            qk_done[(t, ch)] = c("tQk")

        def emit_exp(t, ch):
            rec.add("scalar",
                    lambda e, ch=ch, t=t: e.activation(
                        WT2[:, ch, :], (S0, S1)[ch % 2][:, :], AF.Exp,
                        bias=NEGC[:, t:t + 1], scale=SCALE,
                        accum_out=SUM4[:, ch:ch + 1]),
                    waits=[("tQk", qk_done[(t, ch)]), ("tMM", nA_mm),
                           w("tPv"), w("aMp"), w("vMp"), w("vRi")],
                    inc="aExp")
            exp_done[(t, ch)] = c("aExp")

        def emit_J2(t):
            rec.add("scalar",
                    lambda e: e.activation(XTB[:, :], TB[:, :], AF.Square,
                                           accum_out=SM[:, 13:14]),
                    waits=[w("vFf"), ("tXt", c("tXt"))], inc="aSq2")
            rec.add("vector",
                    lambda e: e.tensor_scalar(SM[:, 14:15], SM[:, 13:14],
                                              1.0 / H, 1e-5, AL.mult, AL.add),
                    waits=[w("aSq2")], inc="vOut")
            rec.add("vector",
                    lambda e: e.tensor_scalar(SM[:, 13:14], SM[:, 12:13],
                                              SM[:, 12:13], None, AL.mult),
                    inc="vOut")
            rec.add("vector",
                    lambda e: e.tensor_tensor(SM[:, 14:15], SM[:, 14:15],
                                              SM[:, 13:14], AL.subtract),
                    inc="vOut")
            rec.add("scalar",
                    lambda e: e.activation(SM[:, 14:15], SM[:, 14:15], AF.Ln),
                    waits=[w("vOut")], inc="aSq2")
            rec.add("scalar",
                    lambda e: e.activation(SM[:, 15:16], SM[:, 14:15], AF.Exp,
                                           scale=-0.5),
                    inc="aSq2")
            rec.add("vector",
                    lambda e: e.tensor_scalar(TB[:, :], TB[:, :], SM[:, 12:13],
                                              SM[:, 15:16], AL.add, AL.mult),
                    waits=[w("aSq2")], inc="vOut")
            dma("sync", outa[t * 128:(t + 1) * 128, :], TB[:, :],
                waits=[w("vOut")], inc="dOut")

        def mk_pt8(g, pg):
            def pt8(e, g=g, pg=pg):
                for tt in range(HT):
                    r = e.transpose(pg[:, tt * 128:(tt + 1) * 128],
                                    WT2[:, g, tt * 128:(tt + 1) * 128],
                                    IDB[:, :])
                return r
            return pt8

        def mk_pv(g):
            def pvg(e, g=g):
                for kt_ in range(8 * g, 8 * g + 8):
                    lt = WT2[:, kt_ // 8, (kt_ % 8) * 128:
                             (kt_ % 8 + 1) * 128]
                    for h in range(2):
                        r = e.matmul(O[:, h * 512:(h + 1) * 512], lt,
                                     V[:, kt_, h * 512:(h + 1) * 512],
                                     start=(kt_ == 0),
                                     stop=(kt_ == nst - 1))
                return r
            return pvg

        def mk_pv_half(g, h):
            def pvh(e, g=g, h=h):
                for kt_ in range(8 * g, 8 * g + 8):
                    lt = WT2[:, kt_ // 8, (kt_ % 8) * 128:
                             (kt_ % 8 + 1) * 128]
                    r = e.matmul(O[:, h * 512:(h + 1) * 512], lt,
                                 V[:, kt_, h * 512:(h + 1) * 512],
                                 start=(kt_ == 0),
                                 stop=(kt_ == nst - 1))
                return r
            return pvh

        def emit_pt(t, g):
            pg = (P, P2)[g % 2]
            waits = [("aExp", exp_done[(t, g)]), w("aPt", lag=blag)]
            if g == 0:
                waits += [w("aO2t"), w("tRb"), w("aQlt")]
            if g == 1:
                waits += [w("aQlt")]
            rec.add("tensor", mk_pt8(g, pg), waits=waits, inc="tPtr")
            rec.add("scalar",
                    lambda e, g=g, pg=pg: e.activation(
                        WT2[:, g, :], pg[:, :], AF.Copy),
                    waits=[w("tPtr")], inc="aPt")
            pt_done[(t, g)] = ("aPt", c("aPt"))

        def emit_pv(t, g):
            waits = [pt_done[(t, g)]]
            if g == 0:
                waits += [w("aO2"), w("vFf", lag=3)]
            if g == n_ch - 1:
                rec.add("tensor", mk_pv_half(g, 0), waits=waits,
                        inc="tPvA")
                rec.add("tensor", mk_pv_half(g, 1), inc="tPv")
            else:
                rec.add("tensor", mk_pv(g), waits=waits)

        emit_loads(0)
        emit_xpb(0)
        emit_loads(1)
        for ch in perm:
            emit_qk_mm(0, ch)
            emit_exp(0, ch)

        for qt in range(nst):
            par = qt % 2
            rec.add("vector",
                    lambda e: e.tensor_reduce(SM[:, 3:4], SUM4[:, :], AXX,
                                              AL.add),
                    waits=[w("aExp")], inc="vRi")
            rec.add("vector", lambda e: e.reciprocal(SM[:, 4:5], SM[:, 3:4]),
                    inc="vRi")

            if qt >= 1:
                emit_xpb(qt)
            if qt >= 1 and qt + 1 < nst:
                emit_loads(qt + 1)

            if (qt, 0) not in pt_done:
                emit_pt(qt, 0)
            for g in range(1, min(n_ch, 3)):
                emit_pt(qt, g)
            for g in range(n_ch):
                emit_pv(qt, g)
                if g == 0 and n_ch >= 4:
                    emit_pt(qt, 3)

            if qt >= 1:
                emit_J2(qt - 1)

            def tr4(lo):
                pq = (P, P2)[lo]

                def f(e, lo=lo, pq=pq):
                    for t in range(4 * lo, 4 * lo + 4):
                        r = e.transpose(pq[:, t * 128:(t + 1) * 128],
                                        XBF[:, t * 128:(t + 1) * 128],
                                        IDB[:, :])
                    return r
                return f

            def atmm_half(lo):
                def f(e, lo=lo):
                    for j in range(4 * lo, 4 * lo + 4):
                        lt = WT2[:, 4, j * 128:(j + 1) * 128]
                        for h in range(2):
                            r = e.matmul(S0[:, h * 512:(h + 1) * 512], lt,
                                         WT0[:, j, h * 512:(h + 1) * 512],
                                         start=(j == 0), stop=(j == HT - 1))
                    return r
                return f

            rec.add("scalar",
                    lambda e: e.activation(XBF[:, 0:512], O[:, 0:512],
                                           AF.Copy),
                    waits=[w("tPvA"), w("tO2t")], inc="aO2")
            rec.add("scalar",
                    lambda e: e.activation(XBF[:, 512:1024], O[:, 512:1024],
                                           AF.Copy),
                    waits=[w("tPv")], inc="aO2")
            rec.add("tensor", tr4(0), waits=[w("aO2", lag=1)], inc="tO2t")
            if qt + 1 < nst and n_ch >= 2:
                emit_qk_mm(qt + 1, 1, half=0,
                           extra=[("aExp", exp_done[(qt, n_ch - 1)])])
            rec.add("tensor", tr4(1), waits=[w("aO2")], inc="tO2t")
            rec.add("scalar",
                    lambda e: e.activation(WT2[:, 4, 0:512], P[:, 0:512],
                                           AF.Copy),
                    waits=[w("tO2t", lag=1), ("tMM", nA_mm), w("tAt")],
                    inc="aO2t")
            rec.add("scalar",
                    lambda e: e.activation(WT2[:, 4, 512:1024],
                                           P2[:, 512:1024], AF.Copy),
                    waits=[w("tO2t")], inc="aO2t")
            rec.add("tensor", atmm_half(0),
                    waits=[w("aO2t", lag=1), w("aExp"), w("vX1p"), w("vWe")],
                    inc=None)
            rec.add("tensor", atmm_half(1), waits=[w("aO2t")], inc="tAt")
            if qt + 1 < nst and n_ch >= 2:
                emit_qk_mm(qt + 1, 1, half=1)
                emit_exp(qt + 1, 1)

            rec.add("vector",
                    lambda e: e.scalar_tensor_tensor(TA[:, :], S0[:, :],
                                                     SM[:, 4:5], TA[:, :],
                                                     AL.mult, AL.add),
                    waits=[w("tAt"), w("gXpb")], inc="vX1p")
            rec.add("vector",
                    lambda e: e.tensor_reduce(SM[:, 5:6], TA[:, :], AXX,
                                              AL.add, negate=True),
                    waits=[w("vX1p")], inc="vCen")
            rec.add("vector",
                    lambda e: e.tensor_scalar(SM[:, 5:6], SM[:, 5:6], 1.0 / H,
                                              None, AL.mult),
                    inc="vCen")
            rec.add("scalar",
                    lambda e: e.activation(XTB[:, :], TA[:, :], AF.Square,
                                           accum_out=SM[:, 6:7]),
                    waits=[w("vX1p"), ("tXt", c("tXt"))], inc="aSq1")
            rec.add("vector",
                    lambda e: e.tensor_reduce(SM[:, 9:10], TA[:, :], AXX,
                                              AL.max),
                    inc="vX1")
            rec.add("vector",
                    lambda e: e.tensor_reduce(SM[:, 10:11], TA[:, :], AXX,
                                              AL.min),
                    inc="vX1")
            rec.add("vector",
                    lambda e: e.tensor_scalar(SM[:, 7:8], SM[:, 6:7], 1.0 / H,
                                              1e-5, AL.mult, AL.add),
                    waits=[w("aSq1")], inc="vX1")
            rec.add("vector",
                    lambda e: e.tensor_scalar(SM[:, 6:7], SM[:, 5:6],
                                              SM[:, 5:6], None, AL.mult),
                    inc="vX1")
            rec.add("vector",
                    lambda e: e.tensor_tensor(SM[:, 7:8], SM[:, 7:8],
                                              SM[:, 6:7], AL.subtract),
                    inc="vX1")
            rec.add("scalar",
                    lambda e: e.activation(SM[:, 7:8], SM[:, 7:8], AF.Ln),
                    waits=[w("vX1")], inc="aSq1")
            rec.add("scalar",
                    lambda e: e.activation(SM[:, 8:9], SM[:, 7:8], AF.Exp,
                                           scale=-0.5),
                    inc="aSq1")

            if qt + 1 < nst:
                t = qt + 1
                if n_ch >= 4:
                    emit_qk_mm(t, 0, extra=[w("vX1p")])
                    emit_exp(t, 0)
                    emit_qk_mm(t, 3, extra=[("aExp", exp_done[(t, 1)])])
                    emit_exp(t, 3)
                elif n_ch == 1:
                    emit_qk_mm(t, 0, extra=[w("vX1p")])
                    emit_exp(t, 0)

            rec.add("vector",
                    lambda e: e.tensor_scalar(SM[:, 9:10], SM[:, 9:10],
                                              SM[:, 5:6], None, AL.add),
                    inc="vX1")
            rec.add("vector",
                    lambda e: e.tensor_scalar(SM[:, 10:11], SM[:, 10:11], -1.0,
                                              SM[:, 5:6], AL.mult,
                                              AL.subtract),
                    inc="vX1")
            rec.add("vector",
                    lambda e: e.tensor_tensor(SM[:, 9:10], SM[:, 9:10],
                                              SM[:, 10:11], AL.max),
                    inc="vX1")
            rec.add("vector",
                    lambda e: e.tensor_scalar(SM[:, 9:10], SM[:, 9:10], 1e-12,
                                              None, AL.max),
                    inc="vX1")
            rec.add("vector", lambda e: e.reciprocal(SM[:, 10:11], SM[:, 9:10]),
                    inc="vX1")
            rec.add("vector",
                    lambda e: e.tensor_scalar(SM[:, 10:11], SM[:, 10:11],
                                              127.0, None, AL.mult),
                    inc="vX1")
            rec.add("vector",
                    lambda e: e.tensor_scalar(TB[:, :], TA[:, :], SM[:, 5:6],
                                              SM[:, 10:11], AL.add, AL.mult),
                    waits=[w("aSq2", lag=2), ("dOut", 16 * qt)] + fin,
                    inc="vX1")
            rec.add("vector",
                    lambda e, par=par: e.tensor_scalar(
                        QTB[par][:, :], TB[:, :], MG[:, 0:1], MG[:, 0:1],
                        AL.add, AL.subtract),
                    inc="vQl")

            def ffn_half(lo):
                def f(e, lo=lo):
                    for j in range(4 * lo, 4 * lo + 4):
                        lt = NAT[:, j * 128:(j + 1) * 128]
                        for h in range(2):
                            r = e.matmul(O[:, h * 512:(h + 1) * 512], lt,
                                         WT1[:, j, h * 512:(h + 1) * 512],
                                         start=(j == 0), stop=(j == HT - 1))
                    return r
                return f

            def qlt4(lo):
                pq = (P2, P)[lo]

                def f(e, lo=lo, pq=pq, qb=QTB[par]):
                    for t in range(4 * lo, 4 * lo + 4):
                        r = e.transpose(pq[:, t * 128:(t + 1) * 128],
                                        qb[:, t * 128:(t + 1) * 128],
                                        IDB[:, :])
                    return r
                return f

            if qt + 1 < nst and n_ch >= 2:
                t = qt + 1
                ch = 2 if n_ch >= 4 else 0
                ext = ([("aExp", exp_done[(t, 0)])] if n_ch >= 4
                       else [w("vX1p")])
                emit_qk_mm(t, ch, extra=ext)
                emit_exp(t, ch)
            rec.add("tensor", qlt4(0),
                    waits=[w("vQl"), w("aSq1")], inc="tQlt")
            rec.add("vector",
                    lambda e: e.tensor_copy(NAT[:, 0:512], P2[:, 0:512]),
                    waits=[w("tQlt"), w("aSqK"), ("tKtr", c("tKtr"))],
                    inc="aQlt")
            rec.add("tensor", qlt4(1), waits=[w("vQl")], inc="tQlt")
            rec.add("vector",
                    lambda e: e.tensor_copy(NAT[:, 512:1024], P[:, 512:1024]),
                    waits=[w("tQlt")], inc="aQlt")
            rec.add("vector",
                    lambda e: e.tensor_scalar(SM[:, 11:12], SM[:, 9:10],
                                              1.0 / 127.0, None, AL.mult),
                    inc="vX1")
            rec.add("vector",
                    lambda e: e.tensor_scalar(SM[:, 11:12], SM[:, 11:12],
                                              SM[:, 8:9], None, AL.mult),
                    waits=[w("aSq1")], inc="vX1")
            rec.add("tensor", ffn_half(0),
                    waits=[w("aQlt", lag=1), w("aO2"), w("vMp")],
                    inc=None)
            if qt + 1 < nst:
                emit_pt(qt + 1, 0)
            rec.add("tensor", ffn_half(1), waits=[w("aQlt")], inc="tFf")

            rec.add("vector",
                    lambda e: e.scalar_tensor_tensor(TA[:, :], O[:, :],
                                                     SM[:, 11:12], BC1[:, :],
                                                     AL.mult, AL.mult),
                    waits=[w("tFf"), w("dRs")], inc="vFf")
            rec.add("vector",
                    lambda e: e.scalar_tensor_tensor(TB[:, :], TB[:, :],
                                                     SM[:, 11:12], TA[:, :],
                                                     AL.mult, AL.add),
                    inc="vFf")
            rec.add("vector",
                    lambda e: e.tensor_reduce(SM[:, 12:13], TB[:, :], AXX,
                                              AL.add, negate=True),
                    inc="vFf")
            rec.add("vector",
                    lambda e: e.tensor_scalar(SM[:, 12:13], SM[:, 12:13],
                                              1.0 / H, None, AL.mult),
                    inc="vFf")

        emit_J2(nst - 1)

        rec.add("sync", lambda e: e.nop(), waits=[("dOut", 16 * nst)])

        with nc.Block() as block:
            def emit_stream(eng_name):
                last = {}
                cnt = {}
                self_chain = eng_name in ("vector", "scalar", "gpsimd")
                prev = [None]

                def run(e):
                    for ws, emit, inc, inc_by in rec.ops[eng_name]:
                        if self_chain and prev[0] is not None:
                            s, v = prev[0]
                            if last.get(s, -1) < v:
                                e.wait_ge(sems[s], v)
                                last[s] = v
                        for s, v in ws:
                            if last.get(s, -1) < v:
                                e.wait_ge(sems[s], v)
                                last[s] = v
                        instr = emit(e)
                        if inc is not None:
                            instr.then_inc(sems[inc], inc_by)
                            cnt[inc] = cnt.get(inc, 0) + inc_by
                            prev[0] = (inc, cnt[inc])
                return run

            block.sync(emit_stream("sync"))
            block.tensor(emit_stream("tensor"))
            block.vector(emit_stream("vector"))
            block.scalar(emit_stream("scalar"))
            block.gpsimd(emit_stream("gpsimd"))

    return nc


_CACHE = {}


def kernel(L0_attn_x, L0_attn_Wq, L0_attn_bq, L0_attn_Wk, L0_attn_bk,
           L0_attn_Wv, L0_attn_bv, L0_attn_Wo, L0_attn_bo, L0_ffn_rhs):
    x = np.ascontiguousarray(np.asarray(L0_attn_x, dtype=np.float32))
    B, S, _ = x.shape
    nst = S // 128
    if nst not in _CACHE:
        _CACHE[nst] = build(nst)
    nc = _CACHE[nst]

    com = {
        "wq": np.ascontiguousarray(L0_attn_Wq, dtype=np.float32),
        "wk": np.ascontiguousarray(L0_attn_Wk, dtype=np.float32),
        "wv": np.ascontiguousarray(L0_attn_Wv, dtype=np.float32),
        "wo": np.ascontiguousarray(L0_attn_Wo, dtype=np.float32),
        "bq": np.ascontiguousarray(L0_attn_bq, dtype=np.float32),
        "bk": np.ascontiguousarray(L0_attn_bk, dtype=np.float32),
        "bv": np.ascontiguousarray(L0_attn_bv, dtype=np.float32),
        "bo": np.ascontiguousarray(L0_attn_bo, dtype=np.float32),
        "rhs": np.ascontiguousarray(L0_ffn_rhs, dtype=np.float32),
        "idb": np.eye(128, dtype=ml_dtypes.bfloat16),
        "idf": np.eye(128, dtype=np.float32),
    }
    in_maps = [dict(com, x=np.ascontiguousarray(x[b])) for b in range(B)]
    res = run_bass_kernel_spmd(nc, in_maps, core_ids=list(range(B)))
    return np.stack([res.results[b]["out"] for b in range(B)], axis=0)


# version 32
# speedup vs baseline: 1.0819x; 1.0073x over previous
import math

import numpy as np
import ml_dtypes

import concourse.bass as bass
import concourse.mybir as mybir
from concourse.bass_utils import run_bass_kernel_spmd

F32 = mybir.dt.float32
BF16 = mybir.dt.bfloat16
AL = mybir.AluOpType
AF = mybir.ActivationFunctionType
AXX = mybir.AxisListType.X

H = 1024
HT = 8
NST = 32
SCALE = 1.0 / math.sqrt(512.0)

ENGS = ("sync", "tensor", "vector", "scalar", "gpsimd")


class Rec:

    def __init__(self):
        self.ops = {e: [] for e in ENGS}
        self.cnt = {}

    def c(self, sem):
        return self.cnt.get(sem, 0)

    def w(self, sem, lag=0):
        return (sem, self.cnt.get(sem, 0) - lag)

    def add(self, eng, emit, waits=(), inc=None, inc_by=1):
        ws = [(s, v) for (s, v) in waits if v > 0]
        self.ops[eng].append((ws, emit, inc, inc_by))
        if inc is not None:
            self.cnt[inc] = self.cnt.get(inc, 0) + inc_by


def build(nst=NST):
    nc = bass.Bass("TRN2", target_bir_lowering=False)
    S = nst * 128
    n_ch = max(1, S // 1024)
    blag = 1 if n_ch >= 2 else 0

    x_d = nc.dram_tensor("x", [S, H], F32, kind="ExternalInput")
    wq_d = nc.dram_tensor("wq", [H, H], F32, kind="ExternalInput")
    wk_d = nc.dram_tensor("wk", [H, H], F32, kind="ExternalInput")
    wv_d = nc.dram_tensor("wv", [H, H], F32, kind="ExternalInput")
    wo_d = nc.dram_tensor("wo", [H, H], F32, kind="ExternalInput")
    bq_d = nc.dram_tensor("bq", [H], F32, kind="ExternalInput")
    bk_d = nc.dram_tensor("bk", [H], F32, kind="ExternalInput")
    bv_d = nc.dram_tensor("bv", [H], F32, kind="ExternalInput")
    bo_d = nc.dram_tensor("bo", [H], F32, kind="ExternalInput")
    rhs_d = nc.dram_tensor("rhs", [H, H], F32, kind="ExternalInput")
    idb_d = nc.dram_tensor("idb", [128, 128], BF16, kind="ExternalInput")
    idf_d = nc.dram_tensor("idf", [128, 128], F32, kind="ExternalInput")
    out_d = nc.dram_tensor("out", [S, H], F32, kind="ExternalOutput")
    qsp_d = nc.dram_tensor("qsp", [nst, 128, H], BF16, kind="Internal")
    scr_d = nc.dram_tensor("scr", [1024], F32, kind="Internal")

    xa = x_d.ap()
    outa = out_d.ap()
    qspa = qsp_d.ap()

    rec = Rec()
    c, w = rec.c, rec.w

    from contextlib import ExitStack
    st = ExitStack()
    with st:
        KT = st.enter_context(nc.sbuf_tensor("KT", [128, HT, S], BF16))
        V = st.enter_context(nc.sbuf_tensor("V", [128, nst, H], BF16))
        WT0 = st.enter_context(nc.sbuf_tensor("WT0", [128, HT, H], BF16))
        WT1 = st.enter_context(nc.sbuf_tensor("WT1", [128, HT, H], BF16))
        WT2 = st.enter_context(nc.sbuf_tensor("WT2", [128, HT, H], BF16))
        BC0 = st.enter_context(nc.sbuf_tensor("BC0", [128, H], F32))
        BC1 = st.enter_context(nc.sbuf_tensor("BC1", [128, H], F32))
        TA = st.enter_context(nc.sbuf_tensor("TA", [128, H], F32))
        TB = st.enter_context(nc.sbuf_tensor("TB", [128, H], F32))
        XST = st.enter_context(nc.sbuf_tensor("XST", [128, H], F32))
        XS2 = nc.alloc_sbuf_tensor_at(
            "XS2", [128, H], F32,
            offset=nc.lookup_mloc(V).addr + (nst - 2) * H * 2)
        XBF = st.enter_context(nc.sbuf_tensor("XBF", [128, H], BF16))
        XTB = st.enter_context(nc.sbuf_tensor("XTB", [128, H], BF16))
        NAT = st.enter_context(nc.sbuf_tensor("NAT", [128, H], BF16))
        QTB0 = st.enter_context(nc.sbuf_tensor("QTB0", [128, H], BF16))
        QTB1 = st.enter_context(nc.sbuf_tensor("QTB1", [128, H], BF16))
        IDB = st.enter_context(nc.sbuf_tensor("IDB", [128, 128], BF16))
        IDF = st.enter_context(nc.sbuf_tensor("IDF", [128, 128], F32))
        ONES = st.enter_context(nc.sbuf_tensor("ONES", [1, 128], F32))
        Qss = st.enter_context(nc.sbuf_tensor("Qss", [128, nst], F32))
        Kss = st.enter_context(nc.sbuf_tensor("Kss", [128, nst], F32))
        NEGC = st.enter_context(nc.sbuf_tensor("NEGC", [128, nst], F32))
        SUM4 = st.enter_context(nc.sbuf_tensor("SUM4", [128, n_ch], F32))
        RSA = st.enter_context(nc.sbuf_tensor("RSA", [128, HT], F32))
        KMB = st.enter_context(nc.sbuf_tensor("KMB", [128, 1], F32))
        MG = st.enter_context(nc.sbuf_tensor("MG", [128, 1], F32))
        MGN = st.enter_context(nc.sbuf_tensor("MGN", [128, 1], F32))
        SM = st.enter_context(nc.sbuf_tensor("SM", [128, 16], F32))

        QTB = (QTB0, QTB1)

        S0 = st.enter_context(nc.psum_tensor("S0", [128, H], F32))
        S1 = st.enter_context(nc.psum_tensor("S1", [128, H], F32))
        O = st.enter_context(nc.psum_tensor("O", [128, H], F32))
        P = st.enter_context(nc.psum_tensor("P", [128, H], BF16))
        P2 = st.enter_context(nc.psum_tensor("P2", [128, H], BF16))

        sem_names = [
            "dWst0", "dWst1", "dWst2", "dWst3",
            "dBr", "dRh", "dRs", "dXa", "dSp0", "dSp1", "dQt0",
            "dQt1", "dXc", "dOut",
            "aWc", "tWt", "vWe", "tBc",
            "aCast", "tXt", "vXt", "tMM", "vQ", "tQtr", "vQt", "aSqQ",
            "vK", "tKtr", "vKt", "aSqK", "vV",
            "tRh", "vRq", "tRb", "vMp", "aMp", "dRh1", "aRn",
            "tQk", "aExp", "vRi", "tPtr", "aPt", "vPt", "tPv", "tPvA", "aO2",
            "tO2t", "aO2t", "tAt", "vX1p", "vCen", "aSq1", "vX1",
            "vQl", "tQlt", "aQlt", "tFf", "vFf", "aSq2", "vOut", "gXpb",
        ]
        sems = {n: st.enter_context(nc.semaphore(n)) for n in sem_names}

        def dma(eng, out, in_, waits=(), inc=None):
            rec.add(eng, lambda e, o=out, i=in_: e.dma_start(out=o, in_=i),
                    waits=waits, inc=inc, inc_by=16)

        def tr8(dst_ps, src_sb):
            def f(e, d=dst_ps, s=src_sb):
                for t in range(HT):
                    r = e.transpose(d[:, t * 128:(t + 1) * 128],
                                    s[:, t * 128:(t + 1) * 128], IDB[:, :])
                return r
            return f

        dma("sync", IDB[:, :], idb_d.ap()[:, :], inc="dBr")
        dma("sync", IDF[:, :], idf_d.ap()[:, :], inc="dRh")

        wp_g = [0]

        def weight_prep(w_handle, dst, xst_w=(), dst_w=(), xbf_w=(), p_w=(),
                        xs2_w=(), nbuf=3):
            wap = w_handle.ap()
            for j in range(HT):
                gj = wp_g[0]
                wp_g[0] += 1
                if nbuf == 3:
                    stg = (XST, TB, XS2)[gj % 3]
                    sw = "dWst%d" % (j % 4)
                    lagw = 2
                else:
                    stg = (TB, XST)[j % 2]
                    sw = "dWst%d" % (j % 2)
                    lagw = 1
                cb = (XBF, NAT)[j % 2]
                pp = (P, P2)[j % 2]
                xw = list(xs2_w) if stg is XS2 else []
                if stg is XST:
                    xw += list(xst_w)
                dma(("sync", "gpsimd")[j % 2], stg[:, :],
                    wap[j * 128:(j + 1) * 128, :],
                    waits=[w("aWc", lag=lagw)] + xw, inc=sw)
                rec.add("scalar",
                        lambda e, stg=stg, cb=cb: e.activation(
                            cb[:, :], stg[:, :], AF.Copy),
                        waits=[w(sw), w("tWt", lag=1)] + list(xbf_w),
                        inc="aWc")
                rec.add("tensor", tr8(pp, cb),
                        waits=[w("aWc"), w("vWe", lag=1), ("dBr", 16)]
                        + list(p_w),
                        inc="tWt")
                rec.add("vector",
                        lambda e, j=j, dst=dst, pp=pp: e.tensor_copy(
                            dst[:, :, j * 128:(j + 1) * 128],
                            pp[:, :].rearrange("p (i q) -> p i q", q=128)),
                        waits=[w("tWt")] + list(dst_w), inc="vWe")

        rec.add("vector", lambda e: e.memset(ONES[:, :], 1.0), inc="vWe")
        rec.add("vector", lambda e: e.memset(MG[:, :], 2.0 ** -30), inc="vWe")
        rec.add("vector",
                lambda e: e.tensor_scalar(MG[:, :], MG[:, :], 1.0, 1.0,
                                          AL.add, AL.subtract),
                inc="vWe")
        rec.add("vector",
                lambda e: e.tensor_scalar(MG[:, :], MG[:, :],
                                          float(1.5 * 2.0 ** 82 - 1.5 * 2.0 ** 53),
                                          float(1.5 * 2.0 ** 23),
                                          AL.mult, AL.add),
                inc="vWe")
        rec.add("vector",
                lambda e: e.tensor_scalar(MGN[:, :], MG[:, :], -1.0, None,
                                          AL.mult),
                inc="vWe")

        def bcast_row(dst, stg, src_dma_in=None, mm_waits=(), evac_waits=(),
                      dma_waits=(), n=H):
            if src_dma_in is not None:
                dma("sync", stg[0:1, 0:n], src_dma_in,
                    waits=[w("dBr")] + list(dma_waits), inc="dBr")

            def mm(e, n=n, stg=stg):
                nh = (n + 511) // 512
                for hh in range(nh):
                    q = min(512, n - hh * 512)
                    r = e.matmul(O[:, hh * 512:hh * 512 + q], ONES[0:1, 0:128],
                                 stg[0:1, hh * 512:hh * 512 + q],
                                 start=True, stop=True)
                return r
            rec.add("tensor", mm, waits=[w("dBr"), w("vWe")] + list(mm_waits),
                    inc="tBc")
            rec.add("vector",
                    lambda e, dst=dst, n=n: e.tensor_copy(dst, O[:, 0:n]),
                    waits=[w("tBc")] + list(evac_waits), inc="vWe")

        dma("sync", XS2[:, :], xa[0:128, :], inc="dXa")
        rec.add("scalar",
                lambda e: e.activation(XTB[:, :], XS2[:, :], AF.Copy),
                waits=[("dXa", 16)], inc="aCast")
        rec.add("tensor", tr8(P, XTB),
                waits=[w("aCast"), w("vWe"), ("dBr", 16)], inc="tXt")
        rec.add("vector", lambda e: e.tensor_copy(XTB[:, :], P[:, :]),
                waits=[w("tXt")], inc="vXt")

        weight_prep(wq_d, WT0, xs2_w=[("aCast", 1)], p_w=[("vXt", 1)])
        for bh, dstb in ((bq_d, BC0), (bk_d, BC1), (bv_d, TA)):
            dma("scalar", dstb[:, :],
                bass.AP(bh, 0, [[0, 128], [1, H]]),
                waits=[w("dBr")], inc="dBr")
        weight_prep(wk_d, WT1, xs2_w=[("aCast", 1)])
        weight_prep(wv_d, WT2, xs2_w=[("aCast", 1)])

        dma("sync", XST[:, :], xa[128:256, :],
            waits=[w("aCast"), ("aWc", c("aWc") - 2)], inc="dXa")

        def do_k_evac(i):
            kb = QTB[(i + 1) % 2]
            rec.add("vector",
                    lambda e, kb=kb: e.scalar_tensor_tensor(
                        kb[:, :], S1[:, :], 1.0, BC1[:, :], AL.mult, AL.add),
                    waits=[w("tMM"), w("tKtr"), w("aSqK"), ("dBr", 48),
                           w("dSp%d" % ((i + 1) % 2))],
                    inc="vK")
            rec.add("scalar",
                    lambda e, i=i, kb=kb: e.activation(
                        S1[:, :], kb[:, :], AF.Square,
                        accum_out=Kss[:, i:i + 1]),
                    waits=[w("vK")], inc="aSqK")

        def mm_pass(ps, wt):
            def f(e, ps=ps, wt=wt):
                for j in range(HT):
                    lt = XTB[:, j * 128:(j + 1) * 128]
                    for h in range(2):
                        r = e.matmul(ps[:, h * 512:(h + 1) * 512], lt,
                                     wt[:, j, h * 512:(h + 1) * 512],
                                     start=(j == 0), stop=(j == HT - 1))
                return r
            return f

        for i in range(nst):
            if i + 1 < nst:
                rec.add("scalar",
                        lambda e: e.activation(XBF[:, :], XST[:, :], AF.Copy),
                        waits=[("dXa", 16 * (i + 2)), w("tXt"), w("tWt")],
                        inc="aCast")
            if i + 2 < nst:
                dma("sync", XST[:, :], xa[(i + 2) * 128:(i + 3) * 128, :],
                    waits=[w("aCast")], inc="dXa")
            rec.add("tensor", mm_pass(S0, WT0),
                    waits=[w("vXt"), w("vQ"), w("aSqQ")], inc="tMM")
            rec.add("vector",
                    lambda e: e.scalar_tensor_tensor(NAT[:, :], S0[:, :], 1.0,
                                                     BC0[:, :], AL.mult,
                                                     AL.add),
                    waits=[w("tMM"), w("tKtr"), w("aSqK"), ("dBr", 32)],
                    inc="vQ")
            rec.add("scalar",
                    lambda e, i=i: e.activation(S0[:, :], NAT[:, :], AF.Square,
                                                accum_out=Qss[:, i:i + 1]),
                    waits=[w("vQ")], inc="aSqQ")
            if i + 1 < nst:
                rec.add("tensor", tr8(P, XBF),
                        waits=[w("aCast"), w("vXt"), w("vWe")], inc="tXt")
            rec.add("tensor", mm_pass(S1, WT1),
                    waits=[w("vK"), w("aSqK")], inc="tMM")
            do_k_evac(i)
            rec.add("tensor", tr8(P2, NAT), waits=[w("vQ"), w("vQt")],
                    inc="tQtr")
            rec.add("vector",
                    lambda e, i=i: e.tensor_copy(QTB[i % 2][:, :], P2[:, :]),
                    waits=[w("tQtr"), w("dSp%d" % (i % 2)), w("tKtr"),
                           w("aSqK")],
                    inc="vQt")
            dma("sync", qspa[i], QTB[i % 2][:, :], waits=[w("vQt")],
                inc="dSp%d" % (i % 2))
            rec.add("tensor", mm_pass(O, WT2), waits=[w("vV")], inc="tMM")
            if i + 1 < nst:
                rec.add("vector", lambda e: e.tensor_copy(XTB[:, :], P[:, :]),
                        waits=[w("tXt"), ("tMM", c("tMM"))], inc="vXt")
            rec.add("vector",
                    lambda e, i=i: e.scalar_tensor_tensor(V[:, i, :], O[:, :],
                                                          1.0, TA[:, :],
                                                          AL.mult, AL.add),
                    waits=[w("tMM"), w("aWc"), ("dBr", 64)], inc="vV")
            rec.add("tensor", tr8(P2, QTB[(i + 1) % 2]),
                    waits=[w("vK"), w("vQt")], inc="tKtr")
            rec.add("vector",
                    lambda e, i=i: e.tensor_copy(
                        KT[:, :, i * 128:(i + 1) * 128],
                        P2[:, :].rearrange("p (t q) -> p t q", q=128)),
                    waits=[w("tKtr")], inc="vKt")

        nA_mm = c("tMM")
        weight_prep(wo_d, WT0,
                    xst_w=[w("aCast")], dst_w=[("tMM", nA_mm)],
                    xbf_w=[w("tXt"), w("tQtr"), w("aSqQ"), w("aSqK")],
                    p_w=[w("vKt"), w("vQt")], nbuf=2)
        dma("scalar", BC0[:, :], bass.AP(bo_d, 0, [[0, 128], [1, H]]),
            waits=[w("vQ")], inc="dBr")

        rhs_r = rhs_d.ap().rearrange("(i p) o -> p i o", p=128)
        rq_done = {}
        sread = {}

        def rhs_load_rt(j):
            stg = (XST, TB)[j % 2]
            pp = (S0, S1)[j % 2]
            sw = ("dRh", "dRh1")[j % 2]
            dma(("sync", "gpsimd")[j % 2],
                stg[:, :].rearrange("p (i o) -> p i o", o=128),
                rhs_r[:, :, j * 128:(j + 1) * 128],
                waits=[w(sw), w("aWc"), w("aCast"), w("tWt"),
                       w("tRh", lag=1)],
                inc=sw)

            def rt8(e, stg=stg, pp=pp):
                for t in range(HT):
                    r = e.transpose(pp[:, t * 128:(t + 1) * 128],
                                    stg[:, t * 128:(t + 1) * 128], IDF[:, :])
                return r
            waits = [w(sw), ("dRh", 16), w("aSqQ"), w("aSqK"),
                     ("tMM", nA_mm), w("vWe")]
            if j >= 2:
                waits.append(sread[j - 2])
            rec.add("tensor", rt8, waits=waits, inc="tRh")

        def rhs_quant(j):
            pp = (S0, S1)[j % 2]
            qb = (XBF, NAT, XTB)[j % 3]
            am = SM[:, 0:1] if j % 2 == 0 else SM[:, 2:3]
            sc = SM[:, 1:2] if j % 2 == 0 else SM[:, 3:4]
            rec.add("vector",
                    lambda e, pp=pp, am=am: e.tensor_reduce(
                        am, pp[:, :], AXX, AL.max,
                        apply_absolute_value=True),
                    waits=[("tRh", c("tRh"))], inc="vRq")
            rec.add("vector",
                    lambda e, am=am: e.tensor_scalar(am, am, 1e-12,
                                                     None, AL.max),
                    inc="vRq")
            rec.add("vector",
                    lambda e, j=j, am=am: e.tensor_scalar(
                        RSA[:, j:j + 1], am, 1.0 / 127.0, None, AL.mult),
                    inc="vRq")
            rec.add("vector", lambda e, am=am, sc=sc: e.reciprocal(sc, am),
                    inc="vRq")
            rec.add("vector",
                    lambda e, sc=sc: e.tensor_scalar(sc, sc, 127.0,
                                                     None, AL.mult),
                    inc="vRq")
            rec.add("scalar",
                    lambda e, pp=pp, sc=sc: e.activation(TA[:, :], pp[:, :],
                                                         AF.Identity,
                                                         scale=sc,
                                                         bias=MG[:, 0:1]),
                    waits=[("vRq", c("vRq")), w("vV")], inc="aRn")
            sread[j] = ("aRn", c("aRn"))
            rec.add("scalar",
                    lambda e, qb=qb: e.activation(qb[:, :], TA[:, :],
                                                  AF.Identity,
                                                  bias=MGN[:, 0:1]),
                    waits=[w("aWc"), w("tWt"), w("tRb", lag=1)], inc="aRn")
            rq_done[j] = ("aRn", c("aRn"))

        def rhs_back(j):
            qb = (XBF, NAT, XTB)[j % 3]
            rec.add("tensor", tr8(P, qb),
                    waits=[rq_done[j], w("vMp"), w("vWe"), w("vKt"),
                           w("vQt"), w("tWt")],
                    inc="tRb")
            rec.add("vector",
                    lambda e, j=j: e.tensor_copy(
                        WT1[:, :, j * 128:(j + 1) * 128],
                        P[:, :].rearrange("p (t q) -> p t q", q=128)),
                    waits=[w("tRb"), ("tMM", nA_mm)], inc="vMp")

        for j in range(HT):
            rhs_load_rt(j)
            if j >= 2:
                rhs_back(j - 2)
            rhs_quant(j)
        rhs_back(HT - 2)
        rhs_back(HT - 1)

        rec.add("sync",
                lambda e: e.dma_start(
                    out=bass.AP(scr_d, 0, [[1, 128], [128, HT]]),
                    in_=RSA[:, :],
                    allow_slow_non_contiguous=True),
                waits=[w("vRq")], inc="dRs", inc_by=16)
        dma("scalar", BC1[:, :], bass.AP(scr_d, 0, [[0, 128], [1, H]]),
            waits=[w("dRs"), w("vK")], inc="dRs")

        rec.add("vector",
                lambda e: e.tensor_reduce(SM[:, 2:3], Kss[:, :], AXX, AL.max),
                waits=[w("aSqK"), w("aRn")], inc="vMp")
        rec.add("sync",
                lambda e: e.dma_start(out=bass.AP(scr_d, 0, [[1, 128]]),
                                      in_=SM[:, 2:3]),
                waits=[w("vMp"), w("dRs")], inc="dRs", inc_by=16)
        rec.add("sync",
                lambda e: e.dma_start(out=TB[0:1, 0:128],
                                      in_=bass.AP(scr_d, 0, [[1, 128]])),
                waits=[w("dRs"), w("vWe")], inc="dRs", inc_by=16)
        rec.add("vector",
                lambda e: e.tensor_reduce(TB[0:1, 200:201], TB[0:1, 0:128],
                                          AXX, AL.max),
                waits=[w("dRs")], inc="vMp")
        rec.add("sync",
                lambda e: e.dma_start(out=bass.AP(scr_d, 0, [[1, 1]]),
                                      in_=TB[0:1, 200:201]),
                waits=[w("vMp")], inc="dRs", inc_by=16)
        dma("sync", KMB[:, :], bass.AP(scr_d, 0, [[0, 128], [1, 1]]),
            waits=[w("dRs")], inc="dRs")
        rec.add("vector",
                lambda e: e.tensor_scalar(NEGC[:, :], Qss[:, :], KMB[:, 0:1],
                                          1e-30, AL.mult, AL.max),
                waits=[w("aSqQ"), w("dRs")], inc="vMp")
        rec.add("scalar",
                lambda e: e.activation(NEGC[:, :], NEGC[:, :], AF.Ln,
                                       scale=1.0 / 512.0),
                waits=[w("vMp")], inc="aMp")
        rec.add("scalar",
                lambda e: e.activation(NEGC[:, :], NEGC[:, :], AF.Exp,
                                       scale=0.5),
                inc="aMp")
        rec.add("vector",
                lambda e: e.tensor_scalar(NEGC[:, :], NEGC[:, :], -1.0, -0.01,
                                          AL.mult, AL.add),
                waits=[w("aMp")], inc="vMp")

        fin = [w("dWst0"), w("dWst1"), w("dWst2"), w("dWst3"), w("dBr"),
               w("dRh"), w("dRh1"), w("dRs")]

        qt_ld = {}
        exp_done = {}
        pt_done = {}
        perm = {4: [1, 0, 3, 2], 2: [1, 0], 1: [0]}[n_ch]

        def emit_loads(t):
            dma("sync", XST[:, :], xa[t * 128:(t + 1) * 128, :],
                waits=[w("gXpb"), w("aWc"), w("vRq")], inc="dXc")
            sp = "dQt%d" % (t % 2)
            dma("sync", QTB[t % 2][:, :], qspa[t],
                waits=[w("tQlt"), w("dSp0"), w("dSp1"), w("tKtr")],
                inc=sp)
            qt_ld[t] = (sp, c(sp))

        def emit_xpb(t):
            rec.add("gpsimd",
                    lambda e: e.tensor_tensor(TA[:, :], XST[:, :], BC0[:, :],
                                              AL.add),
                    waits=[("dXc", 16 * (t + 1)), w("vFf"),
                           w("vV"), w("aRn")] + fin,
                    inc="gXpb")

        qk_done = {}

        def emit_qk_mm(t, ch, half=None, extra=()):
            ps = (S0, S1)[ch % 2]
            hs = (0, 1) if half is None else (half,)

            def qk(e, ch=ch, ps=ps, t=t, hs=hs):
                for h in hs:
                    for j in range(HT):
                        lt = QTB[t % 2][:, j * 128:(j + 1) * 128]
                        r = e.matmul(ps[:, h * 512:(h + 1) * 512], lt,
                                     KT[:, j, ch * 1024 + h * 512:
                                        ch * 1024 + (h + 1) * 512],
                                     start=(j == 0), stop=(j == HT - 1))
                return r
            rec.add("tensor", qk, waits=[qt_ld[t], w("aRn")] + list(extra),
                    inc="tQk")
            qk_done[(t, ch)] = c("tQk")

        def emit_exp(t, ch):
            rec.add("scalar",
                    lambda e, ch=ch, t=t: e.activation(
                        WT2[:, ch, :], (S0, S1)[ch % 2][:, :], AF.Exp,
                        bias=NEGC[:, t:t + 1], scale=SCALE,
                        accum_out=SUM4[:, ch:ch + 1]),
                    waits=[("tQk", qk_done[(t, ch)]), ("tMM", nA_mm),
                           w("tPv"), w("aMp"), w("vMp"), w("vRi")],
                    inc="aExp")
            exp_done[(t, ch)] = c("aExp")

        def emit_J2(t):
            rec.add("scalar",
                    lambda e: e.activation(XTB[:, :], TB[:, :], AF.Square,
                                           accum_out=SM[:, 13:14]),
                    waits=[w("vFf"), ("tXt", c("tXt"))], inc="aSq2")
            rec.add("vector",
                    lambda e: e.tensor_scalar(SM[:, 14:15], SM[:, 13:14],
                                              1.0 / H, 1e-5, AL.mult, AL.add),
                    waits=[w("aSq2")], inc="vOut")
            rec.add("vector",
                    lambda e: e.tensor_scalar(SM[:, 13:14], SM[:, 12:13],
                                              SM[:, 12:13], None, AL.mult),
                    inc="vOut")
            rec.add("vector",
                    lambda e: e.tensor_tensor(SM[:, 14:15], SM[:, 14:15],
                                              SM[:, 13:14], AL.subtract),
                    inc="vOut")
            rec.add("scalar",
                    lambda e: e.activation(SM[:, 14:15], SM[:, 14:15], AF.Ln),
                    waits=[w("vOut")], inc="aSq2")
            rec.add("scalar",
                    lambda e: e.activation(SM[:, 15:16], SM[:, 14:15], AF.Exp,
                                           scale=-0.5),
                    inc="aSq2")
            rec.add("vector",
                    lambda e: e.tensor_scalar(TB[:, :], TB[:, :], SM[:, 12:13],
                                              SM[:, 15:16], AL.add, AL.mult),
                    waits=[w("aSq2")], inc="vOut")
            dma("sync", outa[t * 128:(t + 1) * 128, :], TB[:, :],
                waits=[w("vOut")], inc="dOut")

        def mk_pt8(g, pg):
            def pt8(e, g=g, pg=pg):
                for tt in range(HT):
                    r = e.transpose(pg[:, tt * 128:(tt + 1) * 128],
                                    WT2[:, g, tt * 128:(tt + 1) * 128],
                                    IDB[:, :])
                return r
            return pt8

        def mk_pv(g):
            def pvg(e, g=g):
                for kt_ in range(8 * g, 8 * g + 8):
                    lt = WT2[:, kt_ // 8, (kt_ % 8) * 128:
                             (kt_ % 8 + 1) * 128]
                    for h in range(2):
                        r = e.matmul(O[:, h * 512:(h + 1) * 512], lt,
                                     V[:, kt_, h * 512:(h + 1) * 512],
                                     start=(kt_ == 0),
                                     stop=(kt_ == nst - 1))
                return r
            return pvg

        def mk_pv_half(g, h):
            def pvh(e, g=g, h=h):
                for kt_ in range(8 * g, 8 * g + 8):
                    lt = WT2[:, kt_ // 8, (kt_ % 8) * 128:
                             (kt_ % 8 + 1) * 128]
                    r = e.matmul(O[:, h * 512:(h + 1) * 512], lt,
                                 V[:, kt_, h * 512:(h + 1) * 512],
                                 start=(kt_ == 0),
                                 stop=(kt_ == nst - 1))
                return r
            return pvh

        def emit_pt(t, g):
            pg = (P, P2)[g % 2]
            waits = [("aExp", exp_done[(t, g)]), w("aPt", lag=blag)]
            if g == 0:
                waits += [w("aO2t"), w("tRb"), w("aQlt")]
            if g == 1:
                waits += [w("aQlt")]
            rec.add("tensor", mk_pt8(g, pg), waits=waits, inc="tPtr")
            rec.add("scalar",
                    lambda e, g=g, pg=pg: e.activation(
                        WT2[:, g, :], pg[:, :], AF.Copy),
                    waits=[w("tPtr")], inc="aPt")
            pt_done[(t, g)] = ("aPt", c("aPt"))

        def emit_pv(t, g):
            waits = [pt_done[(t, g)]]
            if g == 0:
                waits += [w("aO2"), w("vFf", lag=3)]
            if g == n_ch - 1:
                rec.add("tensor", mk_pv_half(g, 0), waits=waits,
                        inc="tPvA")
                rec.add("tensor", mk_pv_half(g, 1), inc="tPv")
            else:
                rec.add("tensor", mk_pv(g), waits=waits)

        emit_loads(0)
        emit_xpb(0)
        emit_loads(1)
        for ch in perm:
            emit_qk_mm(0, ch)
            emit_exp(0, ch)

        for qt in range(nst):
            par = qt % 2
            rec.add("vector",
                    lambda e: e.tensor_reduce(SM[:, 3:4], SUM4[:, :], AXX,
                                              AL.add),
                    waits=[w("aExp")], inc="vRi")
            rec.add("vector", lambda e: e.reciprocal(SM[:, 4:5], SM[:, 3:4]),
                    inc="vRi")

            if qt >= 1:
                emit_xpb(qt)
            if qt >= 1 and qt + 1 < nst:
                emit_loads(qt + 1)

            if (qt, 0) not in pt_done:
                emit_pt(qt, 0)
            for g in range(1, min(n_ch, 3)):
                emit_pt(qt, g)
            for g in range(n_ch):
                emit_pv(qt, g)
                if g == 0 and n_ch >= 4:
                    emit_pt(qt, 3)

            if qt >= 1:
                emit_J2(qt - 1)

            def tr4(lo):
                pq = (P, P2)[lo]

                def f(e, lo=lo, pq=pq):
                    for t in range(4 * lo, 4 * lo + 4):
                        r = e.transpose(pq[:, t * 128:(t + 1) * 128],
                                        XBF[:, t * 128:(t + 1) * 128],
                                        IDB[:, :])
                    return r
                return f

            def atmm_half(lo):
                def f(e, lo=lo):
                    for j in range(4 * lo, 4 * lo + 4):
                        lt = WT2[:, 4, j * 128:(j + 1) * 128]
                        for h in range(2):
                            r = e.matmul(S0[:, h * 512:(h + 1) * 512], lt,
                                         WT0[:, j, h * 512:(h + 1) * 512],
                                         start=(j == 0), stop=(j == HT - 1))
                    return r
                return f

            rec.add("scalar",
                    lambda e: e.activation(XBF[:, 0:512], O[:, 0:512],
                                           AF.Copy),
                    waits=[w("tPvA"), w("tO2t")], inc="aO2")
            rec.add("scalar",
                    lambda e: e.activation(XBF[:, 512:1024], O[:, 512:1024],
                                           AF.Copy),
                    waits=[w("tPv")], inc="aO2")
            rec.add("tensor", tr4(0), waits=[w("aO2", lag=1)], inc="tO2t")
            if qt + 1 < nst and n_ch >= 2:
                emit_qk_mm(qt + 1, 1, half=0,
                           extra=[("aExp", exp_done[(qt, n_ch - 1)])])
            rec.add("tensor", tr4(1), waits=[w("aO2")], inc="tO2t")
            rec.add("scalar",
                    lambda e: e.activation(WT2[:, 4, 0:512], P[:, 0:512],
                                           AF.Copy),
                    waits=[w("tO2t", lag=1), ("tMM", nA_mm), w("tAt")],
                    inc="aO2t")
            rec.add("scalar",
                    lambda e: e.activation(WT2[:, 4, 512:1024],
                                           P2[:, 512:1024], AF.Copy),
                    waits=[w("tO2t")], inc="aO2t")
            rec.add("tensor", atmm_half(0),
                    waits=[w("aO2t", lag=1), w("aExp"), w("vX1p"), w("vWe")],
                    inc=None)
            rec.add("tensor", atmm_half(1), waits=[w("aO2t")], inc="tAt")
            if qt + 1 < nst and n_ch >= 2:
                emit_qk_mm(qt + 1, 1, half=1)
                emit_exp(qt + 1, 1)

            rec.add("vector",
                    lambda e: e.scalar_tensor_tensor(TA[:, :], S0[:, :],
                                                     SM[:, 4:5], TA[:, :],
                                                     AL.mult, AL.add),
                    waits=[w("tAt"), w("gXpb")], inc="vX1p")
            rec.add("vector",
                    lambda e: e.tensor_reduce(SM[:, 5:6], TA[:, :], AXX,
                                              AL.add, negate=True),
                    waits=[w("vX1p")], inc="vCen")
            rec.add("vector",
                    lambda e: e.tensor_scalar(SM[:, 5:6], SM[:, 5:6], 1.0 / H,
                                              None, AL.mult),
                    inc="vCen")
            rec.add("scalar",
                    lambda e: e.activation(XTB[:, :], TA[:, :], AF.Square,
                                           accum_out=SM[:, 6:7]),
                    waits=[w("vX1p"), ("tXt", c("tXt"))], inc="aSq1")
            rec.add("vector",
                    lambda e: e.tensor_reduce(SM[:, 9:10], TA[:, :], AXX,
                                              AL.max),
                    inc="vX1")
            rec.add("vector",
                    lambda e: e.tensor_reduce(SM[:, 10:11], TA[:, :], AXX,
                                              AL.min),
                    inc="vX1")
            rec.add("vector",
                    lambda e: e.tensor_scalar(SM[:, 7:8], SM[:, 6:7], 1.0 / H,
                                              1e-5, AL.mult, AL.add),
                    waits=[w("aSq1")], inc="vX1")
            rec.add("vector",
                    lambda e: e.tensor_scalar(SM[:, 6:7], SM[:, 5:6],
                                              SM[:, 5:6], None, AL.mult),
                    inc="vX1")
            rec.add("vector",
                    lambda e: e.tensor_tensor(SM[:, 7:8], SM[:, 7:8],
                                              SM[:, 6:7], AL.subtract),
                    inc="vX1")
            rec.add("scalar",
                    lambda e: e.activation(SM[:, 7:8], SM[:, 7:8], AF.Ln),
                    waits=[w("vX1")], inc="aSq1")
            rec.add("scalar",
                    lambda e: e.activation(SM[:, 8:9], SM[:, 7:8], AF.Exp,
                                           scale=-0.5),
                    inc="aSq1")

            if qt + 1 < nst:
                t = qt + 1
                if n_ch >= 4:
                    emit_qk_mm(t, 0, extra=[w("vX1p")])
                    emit_exp(t, 0)
                    emit_qk_mm(t, 3, extra=[("aExp", exp_done[(t, 1)])])
                    emit_exp(t, 3)
                elif n_ch == 1:
                    emit_qk_mm(t, 0, extra=[w("vX1p")])
                    emit_exp(t, 0)

            rec.add("vector",
                    lambda e: e.tensor_scalar(SM[:, 9:10], SM[:, 9:10],
                                              SM[:, 5:6], None, AL.add),
                    inc="vX1")
            rec.add("vector",
                    lambda e: e.tensor_scalar(SM[:, 10:11], SM[:, 10:11], -1.0,
                                              SM[:, 5:6], AL.mult,
                                              AL.subtract),
                    inc="vX1")
            rec.add("vector",
                    lambda e: e.tensor_tensor(SM[:, 9:10], SM[:, 9:10],
                                              SM[:, 10:11], AL.max),
                    inc="vX1")
            rec.add("vector",
                    lambda e: e.tensor_scalar(SM[:, 9:10], SM[:, 9:10], 1e-12,
                                              None, AL.max),
                    inc="vX1")
            rec.add("vector", lambda e: e.reciprocal(SM[:, 10:11], SM[:, 9:10]),
                    inc="vX1")
            rec.add("vector",
                    lambda e: e.tensor_scalar(SM[:, 10:11], SM[:, 10:11],
                                              127.0, None, AL.mult),
                    inc="vX1")
            rec.add("vector",
                    lambda e: e.tensor_scalar(TB[:, :], TA[:, :], SM[:, 5:6],
                                              SM[:, 10:11], AL.add, AL.mult),
                    waits=[w("aSq2", lag=2), ("dOut", 16 * qt)] + fin,
                    inc="vX1")
            rec.add("vector",
                    lambda e, par=par: e.tensor_scalar(
                        QTB[par][:, :], TB[:, :], MG[:, 0:1], MG[:, 0:1],
                        AL.add, AL.subtract),
                    inc="vQl")

            def ffn_half(lo):
                def f(e, lo=lo):
                    for j in range(4 * lo, 4 * lo + 4):
                        lt = NAT[:, j * 128:(j + 1) * 128]
                        for h in range(2):
                            r = e.matmul(O[:, h * 512:(h + 1) * 512], lt,
                                         WT1[:, j, h * 512:(h + 1) * 512],
                                         start=(j == 0), stop=(j == HT - 1))
                    return r
                return f

            def qlt4(lo):
                pq = (P2, P)[lo]

                def f(e, lo=lo, pq=pq, qb=QTB[par]):
                    for t in range(4 * lo, 4 * lo + 4):
                        r = e.transpose(pq[:, t * 128:(t + 1) * 128],
                                        qb[:, t * 128:(t + 1) * 128],
                                        IDB[:, :])
                    return r
                return f

            if qt + 1 < nst and n_ch >= 2:
                t = qt + 1
                ch = 2 if n_ch >= 4 else 0
                ext = ([("aExp", exp_done[(t, 0)])] if n_ch >= 4
                       else [w("vX1p")])
                emit_qk_mm(t, ch, extra=ext)
                emit_exp(t, ch)
            rec.add("tensor", qlt4(0),
                    waits=[w("vQl"), w("aSq1")], inc="tQlt")
            rec.add("vector",
                    lambda e: e.tensor_copy(NAT[:, 0:512], P2[:, 0:512]),
                    waits=[w("tQlt"), w("aSqK"), ("tKtr", c("tKtr"))],
                    inc="aQlt")
            rec.add("tensor", qlt4(1), waits=[w("vQl")], inc="tQlt")
            rec.add("vector",
                    lambda e: e.tensor_copy(NAT[:, 512:1024], P[:, 512:1024]),
                    waits=[w("tQlt")], inc="aQlt")
            rec.add("vector",
                    lambda e: e.tensor_scalar(SM[:, 11:12], SM[:, 9:10],
                                              1.0 / 127.0, None, AL.mult),
                    inc="vX1")
            rec.add("vector",
                    lambda e: e.tensor_scalar(SM[:, 11:12], SM[:, 11:12],
                                              SM[:, 8:9], None, AL.mult),
                    waits=[w("aSq1")], inc="vX1")
            rec.add("tensor", ffn_half(0),
                    waits=[w("aQlt", lag=1), w("aO2"), w("vMp")],
                    inc=None)
            if qt + 1 < nst:
                emit_pt(qt + 1, 0)
            rec.add("tensor", ffn_half(1), waits=[w("aQlt")], inc="tFf")

            rec.add("vector",
                    lambda e: e.scalar_tensor_tensor(TA[:, :], O[:, :],
                                                     SM[:, 11:12], BC1[:, :],
                                                     AL.mult, AL.mult),
                    waits=[w("tFf"), w("dRs")], inc="vFf")
            rec.add("vector",
                    lambda e: e.scalar_tensor_tensor(TB[:, :], TB[:, :],
                                                     SM[:, 11:12], TA[:, :],
                                                     AL.mult, AL.add),
                    inc="vFf")
            rec.add("vector",
                    lambda e: e.tensor_reduce(SM[:, 12:13], TB[:, :], AXX,
                                              AL.add, negate=True),
                    inc="vFf")
            rec.add("vector",
                    lambda e: e.tensor_scalar(SM[:, 12:13], SM[:, 12:13],
                                              1.0 / H, None, AL.mult),
                    inc="vFf")

        emit_J2(nst - 1)

        rec.add("sync", lambda e: e.nop(), waits=[("dOut", 16 * nst)])

        with nc.Block() as block:
            def emit_stream(eng_name):
                last = {}
                cnt = {}
                self_chain = eng_name in ("vector", "scalar", "gpsimd")
                prev = [None]

                def run(e):
                    for ws, emit, inc, inc_by in rec.ops[eng_name]:
                        if self_chain and prev[0] is not None:
                            s, v = prev[0]
                            if last.get(s, -1) < v:
                                e.wait_ge(sems[s], v)
                                last[s] = v
                        for s, v in ws:
                            if last.get(s, -1) < v:
                                e.wait_ge(sems[s], v)
                                last[s] = v
                        instr = emit(e)
                        if inc is not None:
                            instr.then_inc(sems[inc], inc_by)
                            cnt[inc] = cnt.get(inc, 0) + inc_by
                            prev[0] = (inc, cnt[inc])
                return run

            block.sync(emit_stream("sync"))
            block.tensor(emit_stream("tensor"))
            block.vector(emit_stream("vector"))
            block.scalar(emit_stream("scalar"))
            block.gpsimd(emit_stream("gpsimd"))

    return nc


_CACHE = {}


def kernel(L0_attn_x, L0_attn_Wq, L0_attn_bq, L0_attn_Wk, L0_attn_bk,
           L0_attn_Wv, L0_attn_bv, L0_attn_Wo, L0_attn_bo, L0_ffn_rhs):
    x = np.ascontiguousarray(np.asarray(L0_attn_x, dtype=np.float32))
    B, S, _ = x.shape
    nst = S // 128
    if nst not in _CACHE:
        _CACHE[nst] = build(nst)
    nc = _CACHE[nst]

    com = {
        "wq": np.ascontiguousarray(L0_attn_Wq, dtype=np.float32),
        "wk": np.ascontiguousarray(L0_attn_Wk, dtype=np.float32),
        "wv": np.ascontiguousarray(L0_attn_Wv, dtype=np.float32),
        "wo": np.ascontiguousarray(L0_attn_Wo, dtype=np.float32),
        "bq": np.ascontiguousarray(L0_attn_bq, dtype=np.float32),
        "bk": np.ascontiguousarray(L0_attn_bk, dtype=np.float32),
        "bv": np.ascontiguousarray(L0_attn_bv, dtype=np.float32),
        "bo": np.ascontiguousarray(L0_attn_bo, dtype=np.float32),
        "rhs": np.ascontiguousarray(L0_ffn_rhs, dtype=np.float32),
        "idb": np.eye(128, dtype=ml_dtypes.bfloat16),
        "idf": np.eye(128, dtype=np.float32),
    }
    in_maps = [dict(com, x=np.ascontiguousarray(x[b])) for b in range(B)]
    res = run_bass_kernel_spmd(nc, in_maps, core_ids=list(range(B)))
    return np.stack([res.results[b]["out"] for b in range(B)], axis=0)


# version 35
# speedup vs baseline: 1.0940x; 1.0111x over previous
import math

import numpy as np
import ml_dtypes

import concourse.bass as bass
import concourse.mybir as mybir
from concourse.bass_utils import run_bass_kernel_spmd

F32 = mybir.dt.float32
BF16 = mybir.dt.bfloat16
AL = mybir.AluOpType
AF = mybir.ActivationFunctionType
AXX = mybir.AxisListType.X

H = 1024
HT = 8
NST = 32
SCALE = 1.0 / math.sqrt(512.0)

ENGS = ("sync", "tensor", "vector", "scalar", "gpsimd")


class Rec:

    def __init__(self):
        self.ops = {e: [] for e in ENGS}
        self.cnt = {}

    def c(self, sem):
        return self.cnt.get(sem, 0)

    def w(self, sem, lag=0):
        return (sem, self.cnt.get(sem, 0) - lag)

    def add(self, eng, emit, waits=(), inc=None, inc_by=1):
        ws = [(s, v) for (s, v) in waits if v > 0]
        self.ops[eng].append((ws, emit, inc, inc_by))
        if inc is not None:
            self.cnt[inc] = self.cnt.get(inc, 0) + inc_by


def build(nst=NST):
    nc = bass.Bass("TRN2", target_bir_lowering=False)
    S = nst * 128
    n_ch = max(1, S // 1024)
    blag = 1 if n_ch >= 2 else 0

    x_d = nc.dram_tensor("x", [S, H], F32, kind="ExternalInput")
    wq_d = nc.dram_tensor("wq", [H, H], F32, kind="ExternalInput")
    wk_d = nc.dram_tensor("wk", [H, H], F32, kind="ExternalInput")
    wv_d = nc.dram_tensor("wv", [H, H], F32, kind="ExternalInput")
    wo_d = nc.dram_tensor("wo", [H, H], F32, kind="ExternalInput")
    bq_d = nc.dram_tensor("bq", [H], F32, kind="ExternalInput")
    bk_d = nc.dram_tensor("bk", [H], F32, kind="ExternalInput")
    bv_d = nc.dram_tensor("bv", [H], F32, kind="ExternalInput")
    bo_d = nc.dram_tensor("bo", [H], F32, kind="ExternalInput")
    rhs_d = nc.dram_tensor("rhs", [H, H], F32, kind="ExternalInput")
    idb_d = nc.dram_tensor("idb", [128, 128], BF16, kind="ExternalInput")
    idf_d = nc.dram_tensor("idf", [128, 128], F32, kind="ExternalInput")
    out_d = nc.dram_tensor("out", [S, H], F32, kind="ExternalOutput")
    qsp_d = nc.dram_tensor("qsp", [nst, 128, H], BF16, kind="Internal")
    scr_d = nc.dram_tensor("scr", [1024], F32, kind="Internal")

    xa = x_d.ap()
    outa = out_d.ap()
    qspa = qsp_d.ap()

    rec = Rec()
    c, w = rec.c, rec.w

    from contextlib import ExitStack
    st = ExitStack()
    with st:
        KT = st.enter_context(nc.sbuf_tensor("KT", [128, HT, S], BF16))
        V = st.enter_context(nc.sbuf_tensor("V", [128, nst, H], BF16))
        WT0 = st.enter_context(nc.sbuf_tensor("WT0", [128, HT, H], BF16))
        WT1 = st.enter_context(nc.sbuf_tensor("WT1", [128, HT, H], BF16))
        WT2 = st.enter_context(nc.sbuf_tensor("WT2", [128, HT, H], BF16))
        BC0 = st.enter_context(nc.sbuf_tensor("BC0", [128, H], F32))
        BC1 = st.enter_context(nc.sbuf_tensor("BC1", [128, H], F32))
        TA = st.enter_context(nc.sbuf_tensor("TA", [128, H], F32))
        TB = st.enter_context(nc.sbuf_tensor("TB", [128, H], F32))
        XST = st.enter_context(nc.sbuf_tensor("XST", [128, H], F32))
        XS2 = nc.alloc_sbuf_tensor_at(
            "XS2", [128, H], F32,
            offset=nc.lookup_mloc(V).addr + (nst - 2) * H * 2)
        XBF = st.enter_context(nc.sbuf_tensor("XBF", [128, H], BF16))
        XTB = st.enter_context(nc.sbuf_tensor("XTB", [128, H], BF16))
        NAT = st.enter_context(nc.sbuf_tensor("NAT", [128, H], BF16))
        QTB0 = st.enter_context(nc.sbuf_tensor("QTB0", [128, H], BF16))
        QTB1 = st.enter_context(nc.sbuf_tensor("QTB1", [128, H], BF16))
        IDB = st.enter_context(nc.sbuf_tensor("IDB", [128, 128], BF16))
        IDF = st.enter_context(nc.sbuf_tensor("IDF", [128, 128], F32))
        ONES = st.enter_context(nc.sbuf_tensor("ONES", [1, 128], F32))
        Qss = st.enter_context(nc.sbuf_tensor("Qss", [128, nst], F32))
        Kss = st.enter_context(nc.sbuf_tensor("Kss", [128, nst], F32))
        NEGC = st.enter_context(nc.sbuf_tensor("NEGC", [128, nst], F32))
        SUM4 = st.enter_context(nc.sbuf_tensor("SUM4", [128, n_ch], F32))
        RSA = st.enter_context(nc.sbuf_tensor("RSA", [128, HT], F32))
        KMB = st.enter_context(nc.sbuf_tensor("KMB", [128, 1], F32))
        MG = st.enter_context(nc.sbuf_tensor("MG", [128, 1], F32))
        MGN = st.enter_context(nc.sbuf_tensor("MGN", [128, 1], F32))
        SM = st.enter_context(nc.sbuf_tensor("SM", [128, 16], F32))

        QTB = (QTB0, QTB1)

        S0 = st.enter_context(nc.psum_tensor("S0", [128, H], F32))
        S1 = st.enter_context(nc.psum_tensor("S1", [128, H], F32))
        O = st.enter_context(nc.psum_tensor("O", [128, H], F32))
        P = st.enter_context(nc.psum_tensor("P", [128, H], BF16))
        P2 = st.enter_context(nc.psum_tensor("P2", [128, H], BF16))

        sem_names = [
            "dWst0", "dWst1", "dWst2", "dWst3",
            "dBr", "dRh", "dRs", "dXa", "dSp0", "dSp1", "dQt0",
            "dQt1", "dXc", "dOut",
            "aWc", "tWt", "vWe", "tBc",
            "aCast", "tXt", "vXt", "tMM", "vQ", "tQtr", "vQt", "aSqQ",
            "vK", "tKtr", "vKt", "aSqK", "vV",
            "tRh", "vRq", "tRb", "vMp", "aMp", "dRh1", "aRn",
            "tQk", "aExp", "vRi", "tPtr", "aPt", "vPt", "tPv", "tPvA", "aO2",
            "tO2t", "aO2t", "tAt", "vX1p", "vCen", "aSq1", "vX1",
            "vQl", "tQlt", "aQlt", "tFf", "vFf", "aSq2", "vOut", "gXpb",
        ]
        sems = {n: st.enter_context(nc.semaphore(n)) for n in sem_names}

        def dma(eng, out, in_, waits=(), inc=None):
            rec.add(eng, lambda e, o=out, i=in_: e.dma_start(out=o, in_=i),
                    waits=waits, inc=inc, inc_by=16)

        def tr8(dst_ps, src_sb):
            def f(e, d=dst_ps, s=src_sb):
                for t in range(HT):
                    r = e.transpose(d[:, t * 128:(t + 1) * 128],
                                    s[:, t * 128:(t + 1) * 128], IDB[:, :])
                return r
            return f

        dma("sync", IDB[:, :], idb_d.ap()[:, :], inc="dBr")
        dma("sync", IDF[:, :], idf_d.ap()[:, :], inc="dRh")

        wp_g = [0]

        def weight_prep(w_handle, dst, xst_w=(), dst_w=(), xbf_w=(), p_w=(),
                        xs2_w=(), nbuf=3):
            wap = w_handle.ap()
            for j in range(HT):
                gj = wp_g[0]
                wp_g[0] += 1
                if nbuf == 3:
                    stg = (XST, TB, XS2)[gj % 3]
                    sw = "dWst%d" % (j % 4)
                    lagw = 2
                else:
                    stg = (TB, XST)[j % 2]
                    sw = "dWst%d" % (j % 2)
                    lagw = 1
                cb = (XBF, NAT)[j % 2]
                pp = (P, P2)[j % 2]
                xw = list(xs2_w) if stg is XS2 else []
                if stg is XST:
                    xw += list(xst_w)
                dma(("sync", "gpsimd")[j % 2], stg[:, :],
                    wap[j * 128:(j + 1) * 128, :],
                    waits=[w("aWc", lag=lagw)] + xw, inc=sw)
                rec.add("scalar",
                        lambda e, stg=stg, cb=cb: e.activation(
                            cb[:, :], stg[:, :], AF.Copy),
                        waits=[w(sw), w("tWt", lag=1)] + list(xbf_w),
                        inc="aWc")
                rec.add("tensor", tr8(pp, cb),
                        waits=[w("aWc"), w("vWe", lag=1), ("dBr", 16)]
                        + list(p_w),
                        inc="tWt")
                rec.add("vector",
                        lambda e, j=j, dst=dst, pp=pp: e.tensor_copy(
                            dst[:, :, j * 128:(j + 1) * 128],
                            pp[:, :].rearrange("p (i q) -> p i q", q=128)),
                        waits=[w("tWt")] + list(dst_w), inc="vWe")

        rec.add("vector", lambda e: e.memset(ONES[:, :], 1.0), inc="vWe")
        rec.add("vector", lambda e: e.memset(MG[:, :], 2.0 ** -30), inc="vWe")
        rec.add("vector",
                lambda e: e.tensor_scalar(MG[:, :], MG[:, :], 1.0, 1.0,
                                          AL.add, AL.subtract),
                inc="vWe")
        rec.add("vector",
                lambda e: e.tensor_scalar(MG[:, :], MG[:, :],
                                          float(1.5 * 2.0 ** 82 - 1.5 * 2.0 ** 53),
                                          float(1.5 * 2.0 ** 23),
                                          AL.mult, AL.add),
                inc="vWe")
        rec.add("vector",
                lambda e: e.tensor_scalar(MGN[:, :], MG[:, :], -1.0, None,
                                          AL.mult),
                inc="vWe")

        def bcast_row(dst, stg, src_dma_in=None, mm_waits=(), evac_waits=(),
                      dma_waits=(), n=H):
            if src_dma_in is not None:
                dma("sync", stg[0:1, 0:n], src_dma_in,
                    waits=[w("dBr")] + list(dma_waits), inc="dBr")

            def mm(e, n=n, stg=stg):
                nh = (n + 511) // 512
                for hh in range(nh):
                    q = min(512, n - hh * 512)
                    r = e.matmul(O[:, hh * 512:hh * 512 + q], ONES[0:1, 0:128],
                                 stg[0:1, hh * 512:hh * 512 + q],
                                 start=True, stop=True)
                return r
            rec.add("tensor", mm, waits=[w("dBr"), w("vWe")] + list(mm_waits),
                    inc="tBc")
            rec.add("vector",
                    lambda e, dst=dst, n=n: e.tensor_copy(dst, O[:, 0:n]),
                    waits=[w("tBc")] + list(evac_waits), inc="vWe")

        dma("sync", XS2[:, :], xa[0:128, :], inc="dXa")
        rec.add("scalar",
                lambda e: e.activation(XTB[:, :], XS2[:, :], AF.Copy),
                waits=[("dXa", 16)], inc="aCast")
        rec.add("tensor", tr8(P, XTB),
                waits=[w("aCast"), w("vWe"), ("dBr", 16)], inc="tXt")
        rec.add("vector", lambda e: e.tensor_copy(XTB[:, :], P[:, :]),
                waits=[w("tXt")], inc="vXt")

        weight_prep(wq_d, WT0, xs2_w=[("aCast", 1)], p_w=[("vXt", 1)])
        for bh, dstb in ((bq_d, BC0), (bk_d, BC1), (bv_d, TA)):
            dma("scalar", dstb[:, :],
                bass.AP(bh, 0, [[0, 128], [1, H]]),
                waits=[w("dBr")], inc="dBr")
        weight_prep(wk_d, WT1, xs2_w=[("aCast", 1)])
        weight_prep(wv_d, WT2, xs2_w=[("aCast", 1)])

        dma("sync", XST[:, :], xa[128:256, :],
            waits=[w("aCast"), ("aWc", c("aWc") - 2)], inc="dXa")

        def do_k_evac(i):
            kb = QTB[(i + 1) % 2]
            rec.add("vector",
                    lambda e, kb=kb: e.scalar_tensor_tensor(
                        kb[:, :], S1[:, :], 1.0, BC1[:, :], AL.mult, AL.add),
                    waits=[w("tMM"), w("tKtr"), w("aSqK"), ("dBr", 48),
                           w("dSp%d" % ((i + 1) % 2))],
                    inc="vK")
            rec.add("scalar",
                    lambda e, i=i, kb=kb: e.activation(
                        S1[:, :], kb[:, :], AF.Square,
                        accum_out=Kss[:, i:i + 1]),
                    waits=[w("vK")], inc="aSqK")

        def mm_pass(ps, wt):
            def f(e, ps=ps, wt=wt):
                for j in range(HT):
                    lt = XTB[:, j * 128:(j + 1) * 128]
                    for h in range(2):
                        r = e.matmul(ps[:, h * 512:(h + 1) * 512], lt,
                                     wt[:, j, h * 512:(h + 1) * 512],
                                     start=(j == 0), stop=(j == HT - 1))
                return r
            return f

        for i in range(nst):
            if i + 1 < nst:
                rec.add("scalar",
                        lambda e: e.activation(XBF[:, :], XST[:, :], AF.Copy),
                        waits=[("dXa", 16 * (i + 2)), w("tXt"), w("tWt")],
                        inc="aCast")
            if i + 2 < nst:
                dma("sync", XST[:, :], xa[(i + 2) * 128:(i + 3) * 128, :],
                    waits=[w("aCast")], inc="dXa")
            def qmm_half(lo):
                def f(e, lo=lo):
                    for j in range(4 * lo, 4 * lo + 4):
                        lt = XTB[:, j * 128:(j + 1) * 128]
                        for h in range(2):
                            r = e.matmul(S0[:, h * 512:(h + 1) * 512], lt,
                                         WT0[:, j, h * 512:(h + 1) * 512],
                                         start=(j == 0), stop=(j == HT - 1))
                    return r
                return f
            rec.add("tensor", qmm_half(0),
                    waits=[("vXt", max(1, c("vXt") - 1)), w("vQ"),
                           w("aSqQ"), w("vWe")],
                    inc=None)
            rec.add("tensor", qmm_half(1), waits=[w("vXt")], inc="tMM")
            rec.add("vector",
                    lambda e: e.scalar_tensor_tensor(NAT[:, :], S0[:, :], 1.0,
                                                     BC0[:, :], AL.mult,
                                                     AL.add),
                    waits=[w("tMM"), w("tKtr"), w("aSqK"), ("dBr", 32)],
                    inc="vQ")
            rec.add("scalar",
                    lambda e, i=i: e.activation(S0[:, :], NAT[:, :], AF.Square,
                                                accum_out=Qss[:, i:i + 1]),
                    waits=[w("vQ")], inc="aSqQ")
            if i + 1 < nst:
                rec.add("tensor", tr8(P, XBF),
                        waits=[w("aCast"), w("vXt"), w("vWe")], inc="tXt")
            rec.add("tensor", mm_pass(S1, WT1),
                    waits=[w("vK"), w("aSqK")], inc="tMM")
            do_k_evac(i)
            rec.add("tensor", tr8(P2, NAT), waits=[w("vQ"), w("vQt")],
                    inc="tQtr")
            rec.add("vector",
                    lambda e, i=i: e.tensor_copy(QTB[i % 2][:, :], P2[:, :]),
                    waits=[w("tQtr"), w("dSp%d" % (i % 2)), w("tKtr"),
                           w("aSqK")],
                    inc="vQt")
            dma("sync", qspa[i], QTB[i % 2][:, :], waits=[w("vQt")],
                inc="dSp%d" % (i % 2))
            rec.add("tensor", mm_pass(O, WT2), waits=[w("vV")], inc="tMM")
            if i + 1 < nst:
                rec.add("vector",
                        lambda e: e.tensor_copy(XTB[:, 0:512], P[:, 0:512]),
                        waits=[w("tXt"), ("tMM", c("tMM"))], inc="vXt")
                rec.add("vector",
                        lambda e: e.tensor_copy(XTB[:, 512:1024],
                                                P[:, 512:1024]),
                        inc="vXt")
            rec.add("vector",
                    lambda e, i=i: e.scalar_tensor_tensor(V[:, i, :], O[:, :],
                                                          1.0, TA[:, :],
                                                          AL.mult, AL.add),
                    waits=[w("tMM"), w("aWc"), ("dBr", 64)], inc="vV")
            rec.add("tensor", tr8(P2, QTB[(i + 1) % 2]),
                    waits=[w("vK"), w("vQt")], inc="tKtr")
            rec.add("vector",
                    lambda e, i=i: e.tensor_copy(
                        KT[:, :, i * 128:(i + 1) * 128],
                        P2[:, :].rearrange("p (t q) -> p t q", q=128)),
                    waits=[w("tKtr")], inc="vKt")

        nA_mm = c("tMM")
        weight_prep(wo_d, WT0,
                    xst_w=[w("aCast")], dst_w=[("tMM", nA_mm)],
                    xbf_w=[w("tXt"), w("tQtr"), w("aSqQ"), w("aSqK")],
                    p_w=[w("vKt"), w("vQt")], nbuf=2)
        dma("scalar", BC0[:, :], bass.AP(bo_d, 0, [[0, 128], [1, H]]),
            waits=[w("vQ")], inc="dBr")

        rhs_r = rhs_d.ap().rearrange("(i p) o -> p i o", p=128)
        rq_done = {}
        sread = {}

        def rhs_load_rt(j):
            stg = (XST, TB)[j % 2]
            pp = (S0, S1)[j % 2]
            sw = ("dRh", "dRh1")[j % 2]
            dma(("sync", "gpsimd")[j % 2],
                stg[:, :].rearrange("p (i o) -> p i o", o=128),
                rhs_r[:, :, j * 128:(j + 1) * 128],
                waits=[w(sw), w("aWc"), w("aCast"), w("tWt"),
                       w("tRh", lag=1)],
                inc=sw)

            def rt8(e, stg=stg, pp=pp):
                for t in range(HT):
                    r = e.transpose(pp[:, t * 128:(t + 1) * 128],
                                    stg[:, t * 128:(t + 1) * 128], IDF[:, :])
                return r
            waits = [w(sw), ("dRh", 16), w("aSqQ"), w("aSqK"),
                     ("tMM", nA_mm), w("vWe")]
            if j >= 2:
                waits.append(sread[j - 2])
            rec.add("tensor", rt8, waits=waits, inc="tRh")

        def rhs_quant(j):
            pp = (S0, S1)[j % 2]
            qb = (XBF, NAT, XTB)[j % 3]
            am = SM[:, 0:1] if j % 2 == 0 else SM[:, 2:3]
            sc = SM[:, 1:2] if j % 2 == 0 else SM[:, 3:4]
            rec.add("vector",
                    lambda e, pp=pp, am=am: e.tensor_reduce(
                        am, pp[:, :], AXX, AL.max,
                        apply_absolute_value=True),
                    waits=[("tRh", c("tRh"))], inc="vRq")
            rec.add("vector",
                    lambda e, am=am: e.tensor_scalar(am, am, 1e-12,
                                                     None, AL.max),
                    inc="vRq")
            rec.add("vector",
                    lambda e, j=j, am=am: e.tensor_scalar(
                        RSA[:, j:j + 1], am, 1.0 / 127.0, None, AL.mult),
                    inc="vRq")
            rec.add("vector", lambda e, am=am, sc=sc: e.reciprocal(sc, am),
                    inc="vRq")
            rec.add("vector",
                    lambda e, sc=sc: e.tensor_scalar(sc, sc, 127.0,
                                                     None, AL.mult),
                    inc="vRq")
            rec.add("scalar",
                    lambda e, pp=pp, sc=sc: e.activation(TA[:, :], pp[:, :],
                                                         AF.Identity,
                                                         scale=sc,
                                                         bias=MG[:, 0:1]),
                    waits=[("vRq", c("vRq")), w("vV")], inc="aRn")
            sread[j] = ("aRn", c("aRn"))
            rec.add("scalar",
                    lambda e, qb=qb: e.activation(qb[:, :], TA[:, :],
                                                  AF.Identity,
                                                  bias=MGN[:, 0:1]),
                    waits=[w("aWc"), w("tWt"), w("tRb", lag=1)], inc="aRn")
            rq_done[j] = ("aRn", c("aRn"))

        def rhs_back(j):
            qb = (XBF, NAT, XTB)[j % 3]
            rec.add("tensor", tr8(P, qb),
                    waits=[rq_done[j], w("vMp"), w("vWe"), w("vKt"),
                           w("vQt"), w("tWt")],
                    inc="tRb")
            rec.add("vector",
                    lambda e, j=j: e.tensor_copy(
                        WT1[:, :, j * 128:(j + 1) * 128],
                        P[:, :].rearrange("p (t q) -> p t q", q=128)),
                    waits=[w("tRb"), ("tMM", nA_mm)], inc="vMp")

        for j in range(HT):
            rhs_load_rt(j)
            if j >= 2:
                rhs_back(j - 2)
            rhs_quant(j)
        rhs_back(HT - 2)
        rhs_back(HT - 1)

        rec.add("sync",
                lambda e: e.dma_start(
                    out=bass.AP(scr_d, 0, [[1, 128], [128, HT]]),
                    in_=RSA[:, :],
                    allow_slow_non_contiguous=True),
                waits=[w("vRq")], inc="dRs", inc_by=16)
        dma("scalar", BC1[:, :], bass.AP(scr_d, 0, [[0, 128], [1, H]]),
            waits=[w("dRs"), w("vK")], inc="dRs")

        rec.add("vector",
                lambda e: e.tensor_reduce(SM[:, 2:3], Kss[:, :], AXX, AL.max),
                waits=[w("aSqK"), w("aRn")], inc="vMp")
        rec.add("sync",
                lambda e: e.dma_start(out=bass.AP(scr_d, 0, [[1, 128]]),
                                      in_=SM[:, 2:3]),
                waits=[w("vMp"), w("dRs")], inc="dRs", inc_by=16)
        rec.add("sync",
                lambda e: e.dma_start(out=TB[0:1, 0:128],
                                      in_=bass.AP(scr_d, 0, [[1, 128]])),
                waits=[w("dRs"), w("vWe")], inc="dRs", inc_by=16)
        rec.add("vector",
                lambda e: e.tensor_reduce(TB[0:1, 200:201], TB[0:1, 0:128],
                                          AXX, AL.max),
                waits=[w("dRs")], inc="vMp")
        rec.add("sync",
                lambda e: e.dma_start(out=bass.AP(scr_d, 0, [[1, 1]]),
                                      in_=TB[0:1, 200:201]),
                waits=[w("vMp")], inc="dRs", inc_by=16)
        dma("sync", KMB[:, :], bass.AP(scr_d, 0, [[0, 128], [1, 1]]),
            waits=[w("dRs")], inc="dRs")
        rec.add("vector",
                lambda e: e.tensor_scalar(NEGC[:, :], Qss[:, :], KMB[:, 0:1],
                                          1e-30, AL.mult, AL.max),
                waits=[w("aSqQ"), w("dRs")], inc="vMp")
        rec.add("scalar",
                lambda e: e.activation(NEGC[:, :], NEGC[:, :], AF.Ln,
                                       scale=1.0 / 512.0),
                waits=[w("vMp")], inc="aMp")
        rec.add("scalar",
                lambda e: e.activation(NEGC[:, :], NEGC[:, :], AF.Exp,
                                       scale=0.5),
                inc="aMp")
        rec.add("vector",
                lambda e: e.tensor_scalar(NEGC[:, :], NEGC[:, :], -1.0, -0.01,
                                          AL.mult, AL.add),
                waits=[w("aMp")], inc="vMp")

        fin = [w("dWst0"), w("dWst1"), w("dWst2"), w("dWst3"), w("dBr"),
               w("dRh"), w("dRh1"), w("dRs")]

        qt_ld = {}
        exp_done = {}
        pt_done = {}
        perm = {4: [1, 0, 3, 2], 2: [1, 0], 1: [0]}[n_ch]

        def emit_loads(t):
            dma("sync", XST[:, :], xa[t * 128:(t + 1) * 128, :],
                waits=[w("gXpb"), w("aWc"), w("vRq")], inc="dXc")
            sp = "dQt%d" % (t % 2)
            dma("sync", QTB[t % 2][:, :], qspa[t],
                waits=[w("tQlt"), w("dSp0"), w("dSp1"), w("tKtr")],
                inc=sp)
            qt_ld[t] = (sp, c(sp))

        def emit_xpb(t):
            rec.add("gpsimd",
                    lambda e: e.tensor_tensor(TA[:, :], XST[:, :], BC0[:, :],
                                              AL.add),
                    waits=[("dXc", 16 * (t + 1)), w("vFf"),
                           w("vV"), w("aRn")] + fin,
                    inc="gXpb")

        qk_done = {}

        def emit_qk_mm(t, ch, half=None, extra=()):
            ps = (S0, S1)[ch % 2]
            hs = (0, 1) if half is None else (half,)

            def qk(e, ch=ch, ps=ps, t=t, hs=hs):
                for h in hs:
                    for j in range(HT):
                        lt = QTB[t % 2][:, j * 128:(j + 1) * 128]
                        r = e.matmul(ps[:, h * 512:(h + 1) * 512], lt,
                                     KT[:, j, ch * 1024 + h * 512:
                                        ch * 1024 + (h + 1) * 512],
                                     start=(j == 0), stop=(j == HT - 1))
                return r
            rec.add("tensor", qk, waits=[qt_ld[t], w("aRn")] + list(extra),
                    inc="tQk")
            qk_done[(t, ch)] = c("tQk")

        def emit_exp(t, ch):
            rec.add("scalar",
                    lambda e, ch=ch, t=t: e.activation(
                        WT2[:, ch, :], (S0, S1)[ch % 2][:, :], AF.Exp,
                        bias=NEGC[:, t:t + 1], scale=SCALE,
                        accum_out=SUM4[:, ch:ch + 1]),
                    waits=[("tQk", qk_done[(t, ch)]), ("tMM", nA_mm),
                           w("tPv"), w("aMp"), w("vMp"), w("vRi")],
                    inc="aExp")
            exp_done[(t, ch)] = c("aExp")

        def emit_J2(t):
            rec.add("scalar",
                    lambda e: e.activation(XTB[:, :], TB[:, :], AF.Square,
                                           accum_out=SM[:, 13:14]),
                    waits=[w("vFf"), ("tXt", c("tXt"))], inc="aSq2")
            rec.add("vector",
                    lambda e: e.tensor_scalar(SM[:, 14:15], SM[:, 13:14],
                                              1.0 / H, 1e-5, AL.mult, AL.add),
                    waits=[w("aSq2")], inc="vOut")
            rec.add("vector",
                    lambda e: e.tensor_scalar(SM[:, 13:14], SM[:, 12:13],
                                              SM[:, 12:13], None, AL.mult),
                    inc="vOut")
            rec.add("vector",
                    lambda e: e.tensor_tensor(SM[:, 14:15], SM[:, 14:15],
                                              SM[:, 13:14], AL.subtract),
                    inc="vOut")
            rec.add("scalar",
                    lambda e: e.activation(SM[:, 14:15], SM[:, 14:15], AF.Ln),
                    waits=[w("vOut")], inc="aSq2")
            rec.add("scalar",
                    lambda e: e.activation(SM[:, 15:16], SM[:, 14:15], AF.Exp,
                                           scale=-0.5),
                    inc="aSq2")
            rec.add("vector",
                    lambda e: e.tensor_scalar(TB[:, :], TB[:, :], SM[:, 12:13],
                                              SM[:, 15:16], AL.add, AL.mult),
                    waits=[w("aSq2")], inc="vOut")
            dma("sync", outa[t * 128:(t + 1) * 128, :], TB[:, :],
                waits=[w("vOut")], inc="dOut")

        def mk_pt8(g, pg):
            def pt8(e, g=g, pg=pg):
                for tt in range(HT):
                    r = e.transpose(pg[:, tt * 128:(tt + 1) * 128],
                                    WT2[:, g, tt * 128:(tt + 1) * 128],
                                    IDB[:, :])
                return r
            return pt8

        def mk_pv(g):
            def pvg(e, g=g):
                for kt_ in range(8 * g, 8 * g + 8):
                    lt = WT2[:, kt_ // 8, (kt_ % 8) * 128:
                             (kt_ % 8 + 1) * 128]
                    for h in range(2):
                        r = e.matmul(O[:, h * 512:(h + 1) * 512], lt,
                                     V[:, kt_, h * 512:(h + 1) * 512],
                                     start=(kt_ == 0),
                                     stop=(kt_ == nst - 1))
                return r
            return pvg

        def mk_pv_half(g, h):
            def pvh(e, g=g, h=h):
                for kt_ in range(8 * g, 8 * g + 8):
                    lt = WT2[:, kt_ // 8, (kt_ % 8) * 128:
                             (kt_ % 8 + 1) * 128]
                    r = e.matmul(O[:, h * 512:(h + 1) * 512], lt,
                                 V[:, kt_, h * 512:(h + 1) * 512],
                                 start=(kt_ == 0),
                                 stop=(kt_ == nst - 1))
                return r
            return pvh

        def emit_pt(t, g):
            pg = (P, P2)[g % 2]
            waits = [("aExp", exp_done[(t, g)]), w("aPt", lag=blag)]
            if g == 0:
                waits += [w("aO2t"), w("tRb"), w("aQlt")]
            if g == 1:
                waits += [w("aQlt")]
            rec.add("tensor", mk_pt8(g, pg), waits=waits, inc="tPtr")
            rec.add("scalar",
                    lambda e, g=g, pg=pg: e.activation(
                        WT2[:, g, :], pg[:, :], AF.Copy),
                    waits=[w("tPtr")], inc="aPt")
            pt_done[(t, g)] = ("aPt", c("aPt"))

        def emit_pv(t, g):
            waits = [pt_done[(t, g)]]
            if g == 0:
                waits += [w("aO2"), w("vFf", lag=3)]
            if g == n_ch - 1:
                rec.add("tensor", mk_pv_half(g, 0), waits=waits,
                        inc="tPvA")
                rec.add("tensor", mk_pv_half(g, 1), inc="tPv")
            else:
                rec.add("tensor", mk_pv(g), waits=waits)

        emit_loads(0)
        emit_xpb(0)
        emit_loads(1)
        for ch in perm:
            emit_qk_mm(0, ch)
            emit_exp(0, ch)

        for qt in range(nst):
            par = qt % 2
            rec.add("vector",
                    lambda e: e.tensor_reduce(SM[:, 3:4], SUM4[:, :], AXX,
                                              AL.add),
                    waits=[w("aExp")], inc="vRi")
            rec.add("vector", lambda e: e.reciprocal(SM[:, 4:5], SM[:, 3:4]),
                    inc="vRi")

            if qt >= 1:
                emit_xpb(qt)
            if qt >= 1 and qt + 1 < nst:
                emit_loads(qt + 1)

            if (qt, 0) not in pt_done:
                emit_pt(qt, 0)
            for g in range(1, min(n_ch, 3)):
                emit_pt(qt, g)
            for g in range(n_ch):
                emit_pv(qt, g)
                if g == 0 and n_ch >= 4:
                    emit_pt(qt, 3)

            if qt >= 1:
                emit_J2(qt - 1)

            def tr4(lo):
                pq = (P, P2)[lo]

                def f(e, lo=lo, pq=pq):
                    for t in range(4 * lo, 4 * lo + 4):
                        r = e.transpose(pq[:, t * 128:(t + 1) * 128],
                                        XBF[:, t * 128:(t + 1) * 128],
                                        IDB[:, :])
                    return r
                return f

            def atmm_half(lo):
                def f(e, lo=lo):
                    for j in range(4 * lo, 4 * lo + 4):
                        lt = WT2[:, 4, j * 128:(j + 1) * 128]
                        for h in range(2):
                            r = e.matmul(S0[:, h * 512:(h + 1) * 512], lt,
                                         WT0[:, j, h * 512:(h + 1) * 512],
                                         start=(j == 0), stop=(j == HT - 1))
                    return r
                return f

            rec.add("scalar",
                    lambda e: e.activation(XBF[:, 0:512], O[:, 0:512],
                                           AF.Copy),
                    waits=[w("tPvA"), w("tO2t")], inc="aO2")
            rec.add("scalar",
                    lambda e: e.activation(XBF[:, 512:1024], O[:, 512:1024],
                                           AF.Copy),
                    waits=[w("tPv")], inc="aO2")
            rec.add("tensor", tr4(0), waits=[w("aO2", lag=1)], inc="tO2t")
            if qt + 1 < nst and n_ch >= 2:
                emit_qk_mm(qt + 1, 1, half=0,
                           extra=[("aExp", exp_done[(qt, n_ch - 1)])])
            rec.add("tensor", tr4(1), waits=[w("aO2")], inc="tO2t")
            rec.add("scalar",
                    lambda e: e.activation(WT2[:, 4, 0:512], P[:, 0:512],
                                           AF.Copy),
                    waits=[w("tO2t", lag=1), ("tMM", nA_mm), w("tAt")],
                    inc="aO2t")
            rec.add("scalar",
                    lambda e: e.activation(WT2[:, 4, 512:1024],
                                           P2[:, 512:1024], AF.Copy),
                    waits=[w("tO2t")], inc="aO2t")
            rec.add("tensor", atmm_half(0),
                    waits=[w("aO2t", lag=1), w("aExp"), w("vX1p"), w("vWe")],
                    inc=None)
            rec.add("tensor", atmm_half(1), waits=[w("aO2t")], inc="tAt")
            if qt + 1 < nst and n_ch >= 2:
                emit_qk_mm(qt + 1, 1, half=1)
                emit_exp(qt + 1, 1)

            rec.add("vector",
                    lambda e: e.scalar_tensor_tensor(TA[:, :], S0[:, :],
                                                     SM[:, 4:5], TA[:, :],
                                                     AL.mult, AL.add),
                    waits=[w("tAt"), w("gXpb")], inc="vX1p")
            rec.add("vector",
                    lambda e: e.tensor_reduce(SM[:, 5:6], TA[:, :], AXX,
                                              AL.add, negate=True),
                    waits=[w("vX1p")], inc="vCen")
            rec.add("vector",
                    lambda e: e.tensor_scalar(SM[:, 5:6], SM[:, 5:6], 1.0 / H,
                                              None, AL.mult),
                    inc="vCen")
            rec.add("scalar",
                    lambda e: e.activation(XTB[:, :], TA[:, :], AF.Square,
                                           accum_out=SM[:, 6:7]),
                    waits=[w("vX1p"), ("tXt", c("tXt"))], inc="aSq1")
            rec.add("vector",
                    lambda e: e.tensor_reduce(SM[:, 9:10], TA[:, :], AXX,
                                              AL.max),
                    inc="vX1")
            rec.add("vector",
                    lambda e: e.tensor_reduce(SM[:, 10:11], TA[:, :], AXX,
                                              AL.min),
                    inc="vX1")
            rec.add("vector",
                    lambda e: e.tensor_scalar(SM[:, 7:8], SM[:, 6:7], 1.0 / H,
                                              1e-5, AL.mult, AL.add),
                    waits=[w("aSq1")], inc="vX1")
            rec.add("vector",
                    lambda e: e.tensor_scalar(SM[:, 6:7], SM[:, 5:6],
                                              SM[:, 5:6], None, AL.mult),
                    inc="vX1")
            rec.add("vector",
                    lambda e: e.tensor_tensor(SM[:, 7:8], SM[:, 7:8],
                                              SM[:, 6:7], AL.subtract),
                    inc="vX1")
            rec.add("scalar",
                    lambda e: e.activation(SM[:, 7:8], SM[:, 7:8], AF.Ln),
                    waits=[w("vX1")], inc="aSq1")
            rec.add("scalar",
                    lambda e: e.activation(SM[:, 8:9], SM[:, 7:8], AF.Exp,
                                           scale=-0.5),
                    inc="aSq1")

            if qt + 1 < nst:
                t = qt + 1
                if n_ch >= 4:
                    emit_qk_mm(t, 0, extra=[w("vX1p")])
                    emit_exp(t, 0)
                    emit_qk_mm(t, 3, extra=[("aExp", exp_done[(t, 1)])])
                    emit_exp(t, 3)
                elif n_ch == 1:
                    emit_qk_mm(t, 0, extra=[w("vX1p")])
                    emit_exp(t, 0)

            rec.add("vector",
                    lambda e: e.tensor_scalar(SM[:, 9:10], SM[:, 9:10],
                                              SM[:, 5:6], None, AL.add),
                    inc="vX1")
            rec.add("vector",
                    lambda e: e.tensor_scalar(SM[:, 10:11], SM[:, 10:11], -1.0,
                                              SM[:, 5:6], AL.mult,
                                              AL.subtract),
                    inc="vX1")
            rec.add("vector",
                    lambda e: e.tensor_tensor(SM[:, 9:10], SM[:, 9:10],
                                              SM[:, 10:11], AL.max),
                    inc="vX1")
            rec.add("vector",
                    lambda e: e.tensor_scalar(SM[:, 9:10], SM[:, 9:10], 1e-12,
                                              None, AL.max),
                    inc="vX1")
            rec.add("vector", lambda e: e.reciprocal(SM[:, 10:11], SM[:, 9:10]),
                    inc="vX1")
            rec.add("vector",
                    lambda e: e.tensor_scalar(SM[:, 10:11], SM[:, 10:11],
                                              127.0, None, AL.mult),
                    inc="vX1")
            rec.add("vector",
                    lambda e: e.tensor_scalar(TB[:, :], TA[:, :], SM[:, 5:6],
                                              SM[:, 10:11], AL.add, AL.mult),
                    waits=[w("aSq2", lag=2), ("dOut", 16 * qt)] + fin,
                    inc="vX1")
            rec.add("vector",
                    lambda e, par=par: e.tensor_scalar(
                        QTB[par][:, :], TB[:, :], MG[:, 0:1], MG[:, 0:1],
                        AL.add, AL.subtract),
                    inc="vQl")

            def ffn_half(lo):
                def f(e, lo=lo):
                    for j in range(4 * lo, 4 * lo + 4):
                        lt = NAT[:, j * 128:(j + 1) * 128]
                        for h in range(2):
                            r = e.matmul(O[:, h * 512:(h + 1) * 512], lt,
                                         WT1[:, j, h * 512:(h + 1) * 512],
                                         start=(j == 0), stop=(j == HT - 1))
                    return r
                return f

            def qlt4(lo):
                pq = (P2, P)[lo]

                def f(e, lo=lo, pq=pq, qb=QTB[par]):
                    for t in range(4 * lo, 4 * lo + 4):
                        r = e.transpose(pq[:, t * 128:(t + 1) * 128],
                                        qb[:, t * 128:(t + 1) * 128],
                                        IDB[:, :])
                    return r
                return f

            if qt + 1 < nst and n_ch >= 2:
                t = qt + 1
                ch = 2 if n_ch >= 4 else 0
                ext = ([("aExp", exp_done[(t, 0)])] if n_ch >= 4
                       else [w("vX1p")])
                emit_qk_mm(t, ch, extra=ext)
                emit_exp(t, ch)
            rec.add("tensor", qlt4(0),
                    waits=[w("vQl"), w("aSq1")], inc="tQlt")
            rec.add("vector",
                    lambda e: e.tensor_copy(NAT[:, 0:512], P2[:, 0:512]),
                    waits=[w("tQlt"), w("aSqK"), ("tKtr", c("tKtr"))],
                    inc="aQlt")
            rec.add("tensor", qlt4(1), waits=[w("vQl")], inc="tQlt")
            rec.add("vector",
                    lambda e: e.tensor_copy(NAT[:, 512:1024], P[:, 512:1024]),
                    waits=[w("tQlt")], inc="aQlt")
            rec.add("vector",
                    lambda e: e.tensor_scalar(SM[:, 11:12], SM[:, 9:10],
                                              1.0 / 127.0, None, AL.mult),
                    inc="vX1")
            rec.add("vector",
                    lambda e: e.tensor_scalar(SM[:, 11:12], SM[:, 11:12],
                                              SM[:, 8:9], None, AL.mult),
                    waits=[w("aSq1")], inc="vX1")
            rec.add("tensor", ffn_half(0),
                    waits=[w("aQlt", lag=1), w("aO2"), w("vMp")],
                    inc=None)
            if qt + 1 < nst:
                emit_pt(qt + 1, 0)
            rec.add("tensor", ffn_half(1), waits=[w("aQlt")], inc="tFf")

            rec.add("vector",
                    lambda e: e.scalar_tensor_tensor(TA[:, :], O[:, :],
                                                     SM[:, 11:12], BC1[:, :],
                                                     AL.mult, AL.mult),
                    waits=[w("tFf"), w("dRs")], inc="vFf")
            rec.add("vector",
                    lambda e: e.scalar_tensor_tensor(TB[:, :], TB[:, :],
                                                     SM[:, 11:12], TA[:, :],
                                                     AL.mult, AL.add),
                    inc="vFf")
            rec.add("vector",
                    lambda e: e.tensor_reduce(SM[:, 12:13], TB[:, :], AXX,
                                              AL.add, negate=True),
                    inc="vFf")
            rec.add("vector",
                    lambda e: e.tensor_scalar(SM[:, 12:13], SM[:, 12:13],
                                              1.0 / H, None, AL.mult),
                    inc="vFf")

        emit_J2(nst - 1)

        rec.add("sync", lambda e: e.nop(), waits=[("dOut", 16 * nst)])

        with nc.Block() as block:
            def emit_stream(eng_name):
                last = {}
                cnt = {}
                self_chain = eng_name in ("vector", "scalar", "gpsimd")
                prev = [None]

                def run(e):
                    for ws, emit, inc, inc_by in rec.ops[eng_name]:
                        if self_chain and prev[0] is not None:
                            s, v = prev[0]
                            if last.get(s, -1) < v:
                                e.wait_ge(sems[s], v)
                                last[s] = v
                        for s, v in ws:
                            if last.get(s, -1) < v:
                                e.wait_ge(sems[s], v)
                                last[s] = v
                        instr = emit(e)
                        if inc is not None:
                            instr.then_inc(sems[inc], inc_by)
                            cnt[inc] = cnt.get(inc, 0) + inc_by
                            prev[0] = (inc, cnt[inc])
                return run

            block.sync(emit_stream("sync"))
            block.tensor(emit_stream("tensor"))
            block.vector(emit_stream("vector"))
            block.scalar(emit_stream("scalar"))
            block.gpsimd(emit_stream("gpsimd"))

    return nc


_CACHE = {}


def kernel(L0_attn_x, L0_attn_Wq, L0_attn_bq, L0_attn_Wk, L0_attn_bk,
           L0_attn_Wv, L0_attn_bv, L0_attn_Wo, L0_attn_bo, L0_ffn_rhs):
    x = np.ascontiguousarray(np.asarray(L0_attn_x, dtype=np.float32))
    B, S, _ = x.shape
    nst = S // 128
    if nst not in _CACHE:
        _CACHE[nst] = build(nst)
    nc = _CACHE[nst]

    com = {
        "wq": np.ascontiguousarray(L0_attn_Wq, dtype=np.float32),
        "wk": np.ascontiguousarray(L0_attn_Wk, dtype=np.float32),
        "wv": np.ascontiguousarray(L0_attn_Wv, dtype=np.float32),
        "wo": np.ascontiguousarray(L0_attn_Wo, dtype=np.float32),
        "bq": np.ascontiguousarray(L0_attn_bq, dtype=np.float32),
        "bk": np.ascontiguousarray(L0_attn_bk, dtype=np.float32),
        "bv": np.ascontiguousarray(L0_attn_bv, dtype=np.float32),
        "bo": np.ascontiguousarray(L0_attn_bo, dtype=np.float32),
        "rhs": np.ascontiguousarray(L0_ffn_rhs, dtype=np.float32),
        "idb": np.eye(128, dtype=ml_dtypes.bfloat16),
        "idf": np.eye(128, dtype=np.float32),
    }
    in_maps = [dict(com, x=np.ascontiguousarray(x[b])) for b in range(B)]
    res = run_bass_kernel_spmd(nc, in_maps, core_ids=list(range(B)))
    return np.stack([res.results[b]["out"] for b in range(B)], axis=0)
